# version 07
# baseline (speedup 1.0000x reference)
import numpy as np
import concourse.bass as bass
import concourse.mybir as mybir
from concourse.bass_utils import run_bass_kernel_spmd

F32 = mybir.dt.float32
BF16 = mybir.dt.bfloat16
AF = mybir.ActivationFunctionType
ALU = mybir.AluOpType
AX = mybir.AxisListType

NCORE = 8
B, S, D = 4, 4096, 1024
SL = 2048
NH, HD = 16, 64
E, H = 8, 4096
NDC = D // 128
T = 512
NTT = SL // T
HC = 512
NHC = H // HC
SF = 4096
NKC = SF // 128
NQC = SL // 128
SCALE = 0.125
EPS = 1e-5


class Sem:
    def __init__(self, nc, ctx, name):
        self.h = ctx.enter_context(nc.semaphore(name))
        self.n = 0

    def inc(self, inst, k=1):
        inst.then_inc(self.h, k)
        self.n += k
        return self.n


def build_program(stop_after="all"):
    nc = bass.Bass("TRN2", target_bir_lowering=False, detect_race_conditions=False)

    xT = nc.dram_tensor("xT", [D, SL], F32, kind="ExternalInput").ap()
    wqT = nc.dram_tensor("wqT", [D, D], F32, kind="ExternalInput").ap()
    wkT = nc.dram_tensor("wkT", [D, D], F32, kind="ExternalInput").ap()
    wvT = nc.dram_tensor("wvT", [D, D], F32, kind="ExternalInput").ap()
    woT = nc.dram_tensor("woT", [D, D], F32, kind="ExternalInput").ap()
    bqd = nc.dram_tensor("bq", [D], F32, kind="ExternalInput").ap()
    bkd = nc.dram_tensor("bk", [D], F32, kind="ExternalInput").ap()
    bvd = nc.dram_tensor("bv", [D], F32, kind="ExternalInput").ap()
    bod = nc.dram_tensor("bo", [D], F32, kind="ExternalInput").ap()
    wgd = nc.dram_tensor("wg", [D, E], F32, kind="ExternalInput").ap()
    bgd = nc.dram_tensor("bg", [E], F32, kind="ExternalInput").ap()
    w1d = nc.dram_tensor("w1", [E, D, H], F32, kind="ExternalInput").ap()
    w2d = nc.dram_tensor("w2", [E, H, D], F32, kind="ExternalInput").ap()
    w3d = nc.dram_tensor("w3", [E, D, H], F32, kind="ExternalInput").ap()
    outT = nc.dram_tensor("outT", [D, SL], F32, kind="ExternalOutput").ap()

    kexp = nc.dram_tensor("kexp", [D, SL], BF16).ap()
    vexp = nc.dram_tensor("vexp", [SL, D], BF16).ap()
    kg = nc.dram_tensor("kg", [2 * D, SL], BF16).ap()
    vg = nc.dram_tensor("vg", [SF, D], BF16).ap()
    cdram = nc.dram_tensor("cdram", [E, SL], BF16).ap()

    from contextlib import ExitStack
    with ExitStack() as ctx:
        BASE = 16512
        off = [BASE]

        def At(name, shape, dt_, offset=None):
            nbytes = int(np.prod(shape[1:])) * mybir.dt.size(dt_)
            if offset is None:
                offset = off[0]
                off[0] += (nbytes + 31) // 32 * 32
            return nc.alloc_sbuf_tensor_at(name, list(shape), dt_, offset=offset).ap()

        xfm = At("xfm", [128, NDC * SL], F32)
        ub = At("ub", [128, NDC * SL], BF16)
        qoz = At("qoz", [128, NDC * SL], BF16)
        wreg = off[0]
        w1t = [At(f"w1t{p}", [128, NDC * HC], BF16, wreg + p * 24576) for p in range(2)]
        w3t = [At(f"w3t{p}", [128, NDC * HC], BF16, wreg + p * 24576 + 8192) for p in range(2)]
        w2t = [At(f"w2t{p}", [128, 4 * D], BF16, wreg + p * 24576 + 16384) for p in range(2)]
        wta = At("wta", [128, NDC * D], BF16, wreg)
        qtile = [At(f"qtile{p}", [128, SL], BF16, wreg + p * 4096) for p in range(2)]
        khb = [At(f"khb{p}", [128, SF], BF16, wreg + 8192 + p * 8192) for p in range(2)]
        vh = [At(f"vh{p}", [128, NKC * 65], BF16, wreg + 24576 + p * 4160) for p in range(2)]
        kvx = At("kvx", [128, SL], BF16, wreg + 16384)
        bvrowb = At("bvrowb", [1, D], BF16, wreg + 20480)
        off[0] = wreg + 49152
        screg = off[0]
        sq = [At(f"sq{p}", [128, T], F32, screg + p * 2048) for p in range(2)]
        uf32 = [At(f"uf32{p}", [128, T], F32, screg + 4096 + p * 2048) for p in range(2)]
        hbuf = At("hbuf", [128, 4 * T], BF16, screg + 8192)
        s1 = [At(f"s1{p}", [128, T], F32, screg + 12288 + p * 2048) for p in range(2)]
        mrow = At("mrow", [1, SL], BF16, screg)
        rsb = At("rsb", [128, 512], F32, screg + 5120)
        srow = At("srow", [128, 512], F32, screg + 7168)
        obuf2 = [At(f"obuf{p}", [64, 512], BF16, screg + 9216 + p * 1024) for p in range(2)]
        att = [At("att0", [128, 4 * 512], BF16, wreg + 32896),
               At("att1", [128, 4 * 512], BF16, wreg + 36992)]
        off[0] = screg + 16384
        rowbuf = At("rowbuf", [1, 4 * T], F32)
        ctoff = off[0]
        cT = At("cT", [8, SL], BF16)
        crow = At("crow", [1, SL], BF16, offset=ctoff)
        battn = At("battn", [128, 4 * NDC], F32)
        onesf = At("onesf", [128, 128], F32)
        onesb = At("onesb", [1, 128], BF16)
        idf = At("idf", [128, 128], F32)
        idb = At("idb", [128, 128], BF16)
        wg_sb = At("wg_sb", [128, NDC * E], F32)
        g1b = At("g1b", [128, E], F32)
        bgb = At("bgb", [128, E], F32)
        grow = At("grow", [1, 2 * E], F32)
        gsc = At("gsc", [128, 4 * E], F32)
        gcol = At("gcol", [128, 8], F32)
        assert off[0] <= 229344, off[0]

        ROW_TMP, ROW_R, ROW_MR, ROW_NM = (rowbuf[0:1, k * T:(k + 1) * T] for k in range(4))

        pA = ctx.enter_context(nc.psum_tensor("pA", [128, 2048], F32))
        pB = ctx.enter_context(nc.psum_tensor("pB", [128, 2048], F32))
        pAa, pBa = pA[:, :], pB[:, :]
        pAb = pA[:, :].bitcast(BF16)

        def bank(ap_, i):
            return ap_[:, i * 512:(i + 1) * 512]

        names = ["qsem", "dsem", "tsem", "vsem", "asem", "ck", "cv", "esem",
                 "ws0", "ws1", "ks0", "ks1", "kb0", "kb1"]
        SM = {nm: Sem(nc, ctx, nm) for nm in names}
        qsem, dsem, tsem, vsem, asem, ck, cv, esem, ws0, ws1, ks0, ks1, kb0, kb1 = \
            (SM[n] for n in names)
        wsp = [ws0, ws1]
        ksp = [ks0, ks1]
        kbp = [kb0, kb1]
        block = ctx.enter_context(nc.Block())

        @block.gpsimd
        def _(g):
            g.memset(onesf[:, :], 1.0)
            g.memset(onesb[:, :], 1.0)
            g.memset(idf[:, :], 1.0)
            esem.inc(g.memset(idb[:, :], 1.0))
            g.wait_ge(esem.h, 1)
            g.affine_select(idf[:, :], idf[:, :], [[1, 128]], ALU.is_equal, 0.0,
                            base=0, channel_multiplier=-1)
            esem.inc(g.affine_select(idb[:, :], idb[:, :], [[1, 128]], ALU.is_equal, 0.0,
                                     base=0, channel_multiplier=-1))
            with nc.allow_non_contiguous_dma("tiny one-time bias loads"):
                for j, bb in enumerate([bqd, bkd, bvd, bod]):
                    g.dma_start(out=battn[:, j * NDC:(j + 1) * NDC],
                                in_=bb.rearrange("(dc p) -> p dc", p=128),
                                allow_slow_non_contiguous=True).then_inc(dsem.h, 16)
            g.dma_start(out=wg_sb[:, :].rearrange("p (dc e) -> p dc e", e=E),
                        in_=wgd.rearrange("(dc p) e -> p dc e", p=128)).then_inc(dsem.h, 16)
            g.dma_start(out=grow[0:1, E:2 * E], in_=bgd[None, :]).then_inc(dsem.h, 16)
            dsem.n += 6 * 16

        @block.sync
        def _(sy):
            for dc in range(NDC):
                sy.dma_start(out=xfm[:, dc * SL:(dc + 1) * SL],
                             in_=xT[dc * 128:(dc + 1) * 128, :]).then_inc(qsem.h, 16)
            qsem.n += NDC * 16
        q_x = qsem.n

        @block.tensor
        def _(t):
            t.wait_ge(dsem.h, dsem.n)
            t.wait_ge(esem.h, 2)
            t.wait_ge(qsem.h, q_x)
            for dc in range(NDC):
                ins = t.matmul(pBa[0:1, 0:E], onesf[:, 0:1], wg_sb[:, dc * E:(dc + 1) * E],
                               start=(dc == 0), stop=(dc == NDC - 1))
            tsem.inc(ins)
        ev0 = tsem.n

        @block.vector
        def _(v):
            v.wait_ge(tsem.h, ev0)
            vsem.inc(v.tensor_copy(grow[0:1, 0:E], pBa[0:1, 0:E]))
        ev1 = vsem.n

        @block.tensor
        def _(t):
            t.wait_ge(vsem.h, ev1)
            tsem.inc(t.matmul(pAa[:, 0:2 * E], onesf[0:1, 0:128], grow[0:1, :],
                              start=True, stop=True))
        ev2 = tsem.n

        @block.vector
        def _(v):
            v.wait_ge(tsem.h, ev2)
            v.tensor_copy(g1b[:, :], pAa[:, 0:E])
            v.tensor_scalar_mul(battn[:, 0:NDC], battn[:, 0:NDC], SCALE)
            vsem.inc(v.tensor_copy(bgb[:, :], pAa[:, E:2 * E]))

        def gate(tt, v_rows):
            for tc in range(4):
                vg0 = vsem.n
                tok0 = tt * T + tc * 128

                @block.tensor
                def _(t, tt=tt, tc=tc, tok0=tok0, vg0=vg0, v_rows=v_rows):
                    if tc == 0:
                        t.wait_ge(vsem.h, v_rows)
                    else:
                        t.wait_ge(vsem.h, vg0)
                    for dc in range(NDC):
                        ins = t.matmul(pBa[0:128, 0:E], xfm[:, dc * SL + tok0:][:, 0:128],
                                       wg_sb[:, dc * E:(dc + 1) * E],
                                       start=(dc == 0), stop=(dc == NDC - 1))
                    tsem.inc(ins)
                    tsem.inc(t.transpose(pAa[0:128, 1536:1537],
                                         ROW_NM[0:1, tc * 128:(tc + 1) * 128], onesf[0:1, 0:1]))
                    tsem.inc(t.transpose(pAa[0:128, 1537:1538],
                                         ROW_R[0:1, tc * 128:(tc + 1) * 128], onesf[0:1, 0:1]))
                t_g = tsem.n

                @block.vector
                def _(v, t_g=t_g):
                    def vf(ins):
                        vsem.inc(ins)
                        v.wait_ge(vsem.h, vsem.n)
                    v.wait_ge(tsem.h, t_g)
                    vf(v.tensor_copy(gcol[:, 6:8], pAa[:, 1536:1538]))
                    vf(v.scalar_tensor_tensor(gsc[:, 0:E], g1b[:, :], gcol[:, 6:7],
                                              pBa[0:128, 0:E], ALU.mult, ALU.add))
                    vf(v.tensor_scalar(gsc[:, 0:E], gsc[:, 0:E], gcol[:, 7:8], None, ALU.mult))
                    vf(v.tensor_add(gsc[:, 0:E], gsc[:, 0:E], bgb[:, :]))
                    vf(v.reduce_max(gcol[:, 0:1], gsc[:, 0:E], axis=AX.X))
                    vf(v.tensor_scalar_mul(gcol[:, 1:2], gcol[:, 0:1], -1.0))
                    vf(v.tensor_scalar(gsc[:, E:2 * E], gsc[:, 0:E], gcol[:, 0:1], None, ALU.is_ge))
                    vf(v.scalar_tensor_tensor(gsc[:, 2 * E:3 * E], gsc[:, E:2 * E], -1e30,
                                              gsc[:, 0:E], ALU.mult, ALU.add))
                    vsem.inc(v.reduce_max(gcol[:, 2:3], gsc[:, 2 * E:3 * E], axis=AX.X))
                v_g1 = vsem.n

                @block.scalar
                def _(s, v_g1=v_g1):
                    s.wait_ge(vsem.h, v_g1)
                    s.activation(gsc[:, 2 * E:3 * E], gsc[:, 0:E], AF.Exp, bias=gcol[:, 1:2])
                    asem.inc(s.activation(gcol[:, 3:4], gcol[:, 2:3], AF.Exp, bias=gcol[:, 1:2]))
                a_g = asem.n

                @block.vector
                def _(v, a_g=a_g):
                    def vf(ins):
                        vsem.inc(ins)
                        v.wait_ge(vsem.h, vsem.n)
                    v.wait_ge(asem.h, a_g)
                    vf(v.tensor_scalar_add(gcol[:, 4:5], gcol[:, 3:4], 1.0))
                    vf(v.reciprocal(gcol[:, 5:6], gcol[:, 4:5]))
                    vf(v.tensor_scalar(gsc[:, E:2 * E], gsc[:, 0:E], gcol[:, 2:3], None, ALU.is_ge))
                    vf(v.tensor_mul(gsc[:, 2 * E:3 * E], gsc[:, 2 * E:3 * E],
                                    gsc[:, E:2 * E]))
                    vsem.inc(v.tensor_scalar(gsc[:, 3 * E:4 * E].bitcast(BF16)[:, 0:E],
                                             gsc[:, 2 * E:3 * E], gcol[:, 5:6], None, ALU.mult))
                v_g2 = vsem.n

                @block.tensor
                def _(t, v_g2=v_g2):
                    t.wait_ge(vsem.h, v_g2)
                    tsem.inc(t.transpose(pAb[0:8, 3072:3200],
                                         gsc[:, 3 * E:4 * E].bitcast(BF16)[:, 0:E], idb[:, :]))
                t_c = tsem.n

                @block.vector
                def _(v, t_c=t_c, tok0=tok0):
                    v.wait_ge(tsem.h, t_c)
                    vsem.inc(v.tensor_copy(cT[0:8, tok0:tok0 + 128], pAb[0:8, 3072:3200]))

        def layernorm(fold_x, with_gate):
            v_enter = vsem.n
            prev_vu = [v_enter]
            for tt in range(NTT):
                av0 = asem.n
                tv0 = tsem.n

                @block.scalar
                def _(s, tt=tt, tv0=tv0, v_enter=v_enter):
                    if tt == 0:
                        s.wait_ge(vsem.h, v_enter)
                        s.wait_ge(qsem.h, q_x)
                    for dc in range(NDC):
                        if dc >= 2:
                            s.wait_ge(tsem.h, tv0 + dc - 1)
                        asem.inc(s.activation(sq[dc % 2][:, :],
                                              xfm[:, dc * SL + tt * T:][:, 0:T], AF.Square))

                @block.tensor
                def _(t, tt=tt, av0=av0, wait_v=prev_vu[0]):
                    t.wait_ge(vsem.h, wait_v)
                    for dc in range(NDC):
                        x_sl = xfm[:, dc * SL + tt * T:][:, 0:T]
                        t.matmul(pBa[0:1, 1024:1536], onesf[:, 0:1], x_sl,
                                 start=(dc == 0), stop=(dc == NDC - 1))
                        t.wait_ge(asem.h, av0 + dc + 1)
                        tsem.inc(t.matmul(pBa[0:1, 1536:2048], onesf[:, 0:1], sq[dc % 2][:, :],
                                          start=(dc == 0), stop=(dc == NDC - 1)))
                t_sums = tsem.n

                @block.vector
                def _(v, t_sums=t_sums):
                    v.wait_ge(tsem.h, t_sums)
                    v.tensor_scalar_mul(pBa[0:1, 1024:1536], pBa[0:1, 1024:1536], 1.0 / D)
                    vsem.inc(v.tensor_scalar_mul(pBa[0:1, 1536:2048], pBa[0:1, 1536:2048], 1.0 / D))
                v_mean = vsem.n

                @block.scalar
                def _(s, v_mean=v_mean):
                    s.wait_ge(vsem.h, v_mean)
                    asem.inc(s.activation(ROW_TMP, pBa[0:1, 1024:1536], AF.Square))
                a_msq = asem.n

                @block.vector
                def _(v, a_msq=a_msq):
                    v.wait_ge(asem.h, a_msq)
                    v.tensor_sub(pBa[0:1, 1536:2048], pBa[0:1, 1536:2048], ROW_TMP)
                    vsem.inc(v.tensor_scalar_add(pBa[0:1, 1536:2048], pBa[0:1, 1536:2048], EPS))
                v_var = vsem.n

                @block.scalar
                def _(s, v_var=v_var):
                    s.wait_ge(vsem.h, v_var)
                    asem.inc(s.activation(ROW_TMP, pBa[0:1, 1536:2048], AF.Ln))
                a_ln = asem.n

                @block.vector
                def _(v, a_ln=a_ln):
                    v.wait_ge(asem.h, a_ln)
                    vsem.inc(v.tensor_scalar_mul(ROW_TMP, ROW_TMP, -0.5))
                v_ln = vsem.n

                @block.scalar
                def _(s, v_ln=v_ln):
                    s.wait_ge(vsem.h, v_ln)
                    asem.inc(s.activation(ROW_R, ROW_TMP, AF.Exp))
                a_r = asem.n

                @block.vector
                def _(v, a_r=a_r, with_gate=with_gate):
                    v.wait_ge(asem.h, a_r)
                    v.tensor_mul(ROW_MR, pBa[0:1, 1024:1536], ROW_R)
                    if with_gate:
                        vsem.inc(v.tensor_scalar_mul(ROW_NM, pBa[0:1, 1024:1536], -1.0))
                    else:
                        vsem.inc(v.engine_nop())
                v_rows = vsem.n

                if with_gate:
                    gate(tt, v_rows)

                @block.tensor
                def _(t, v_rows=v_rows):
                    t.wait_ge(vsem.h, v_rows)
                    tsem.inc(t.matmul(pBa[:, 1024:1536], onesf[0:1, 0:128], ROW_R,
                                      start=True, stop=True))
                    tsem.inc(t.matmul(pBa[:, 1536:2048], onesf[0:1, 0:128], ROW_MR,
                                      start=True, stop=True))
                t_bc = tsem.n
                vu0 = vsem.n
                au0 = asem.n

                @block.vector
                def _(v, tt=tt, t_bc=t_bc, fold_x=fold_x, au0=au0):
                    v.wait_ge(tsem.h, t_bc)
                    for dc in range(NDC):
                        x_sl = xfm[:, dc * SL + tt * T:][:, 0:T]
                        p = dc % 2
                        if dc >= 2:
                            v.wait_ge(asem.h, au0 + dc - 1)
                        v.tensor_mul(uf32[p][:, :], x_sl, pBa[:, 1024:1536])
                        vsem.inc(v.tensor_sub(uf32[p][:, :], uf32[p][:, :], pBa[:, 1536:2048]))
                        if fold_x:
                            vsem.inc(v.scalar_tensor_tensor(x_sl, uf32[p][:, :], 0.5, x_sl,
                                                            ALU.mult, ALU.add))
                        else:
                            vsem.inc(v.engine_nop())

                @block.scalar
                def _(s, tt=tt, vu0=vu0):
                    for dc in range(NDC):
                        u_sl = ub[:, dc * SL + tt * T:][:, 0:T]
                        s.wait_ge(vsem.h, vu0 + 2 * dc + 1)
                        asem.inc(s.copy(u_sl, uf32[dc % 2][:, :]))
                prev_vu[0] = vsem.n

        def ffn_experts(call):
            t0 = tsem.n
            v0 = vsem.n
            a0 = asem.n
            a_enter = asem.n
            v_pre = vsem.n
            d0w = [wsp[0].n, wsp[1].n]

            @block.sync
            def _(sy, v_pre=v_pre):
                sy.wait_ge(vsem.h, v_pre)
                sy.dma_start(out=cdram[:, :], in_=cT[0:8, :]).then_inc(qsem.h, 16)
                qsem.n += 16
            q_cflat = qsem.n

            def t_base(e):
                return t0 + e * (4 + 32 * 16)

            def t_cb(e, tt):
                return t_base(e) + tt + 1

            def t_blk(e, hc, tt):
                return t_base(e) + 4 + (hc * NTT + tt) * 16

            def t_h1(e, hc, tt, i):
                return t_blk(e, hc, tt) + 2 * i + 1

            def t_h3(e, hc, tt, i):
                return t_blk(e, hc, tt) + 2 * i + 2

            def t_y(e, hc, tt, dc):
                return t_blk(e, hc, tt) + 8 + dc + 1

            def v_base(e):
                return v0 + e * (4 + 32 * 12)

            def v_z(e, tt):
                return v_base(e) + tt + 1

            def v_blkbase(e, hc, tt):
                return v_base(e) + 4 + (hc * NTT + tt) * 12

            def v_hmul(e, hc, tt, i):
                return v_blkbase(e, hc, tt) + i + 1

            def v_yadd(e, hc, tt, dc):
                return v_blkbase(e, hc, tt) + 4 + dc + 1

            def lin_h(e, hc, tt, i):
                return ((e * NHC + hc) * NTT + tt) * 4 + i

            def lin_y(e, hc, tt, dc):
                return ((e * NHC + hc) * NTT + tt) * 8 + dc

            def unlin_h(k):
                return (k // (4 * NTT * NHC), (k // (4 * NTT)) % NHC,
                        (k // 4) % NTT, k % 4)

            def unlin_y(k):
                return (k // (8 * NTT * NHC), (k // (8 * NTT)) % NHC,
                        (k // 8) % NTT, k % 8)

            def a_silu(e, hc, tt, i):
                return a0 + ((e * NHC + hc) * NTT + tt) * 4 + i + 1

            @block.gpsimd
            def _(g, call=call, t0=t0):
                for c in range(E * NHC):
                    e, hc = divmod(c, NHC)
                    p = c % 2
                    if c < 2:
                        g.wait_ge(tsem.h, t0)
                    else:
                        ee, hh = divmod(c - 2, NHC)
                        g.wait_ge(tsem.h, t_y(ee, hh, NTT - 1, 7))
                    g.dma_start(out=w1t[p].rearrange("p (dc h) -> p dc h", h=HC),
                                in_=w1d[e][:, hc * HC:(hc + 1) * HC]
                                .rearrange("(dc p) h -> p dc h", p=128)).then_inc(wsp[p].h, 16)
                    g.dma_start(out=w3t[p].rearrange("p (dc h) -> p dc h", h=HC),
                                in_=w3d[e][:, hc * HC:(hc + 1) * HC]
                                .rearrange("(dc p) h -> p dc h", p=128)).then_inc(wsp[p].h, 16)
                    g.dma_start(out=w2t[p].rearrange("p (hh d) -> p hh d", d=D),
                                in_=w2d[e][hc * HC:(hc + 1) * HC, :]
                                .rearrange("(hh p) d -> p hh d", p=128)).then_inc(wsp[p].h, 16)
                    wsp[p].n += 48

            @block.sync
            def _(sy, q_cflat=q_cflat):
                for e in range(E):
                    if e == 0:
                        sy.wait_ge(qsem.h, q_cflat)
                    else:
                        sy.wait_ge(tsem.h, t_cb(e - 1, NTT - 1))
                    sy.dma_start(out=crow[0:1, :], in_=cdram[e:e + 1, :]).then_inc(qsem.h, 16)
                    qsem.n += 16
            q_crow0 = q_cflat

            @block.tensor
            def _(t, call=call, q_crow0=q_crow0, d0w=d0w):
                for e in range(E):
                    for tt in range(NTT):
                        t.wait_ge(qsem.h, q_crow0 + 16 * (e + 1))
                        if tt > 0:
                            t.wait_ge(vsem.h, v_z(e, tt - 1))
                        if e > 0 and tt == 0:
                            t.wait_ge(vsem.h, v_z(e - 1, NTT - 1))
                        tsem.inc(t.matmul(pBa[:, 1024:1536], onesb[:, 0:128],
                                          crow[0:1, tt * T:(tt + 1) * T], start=True, stop=True))
                        assert tsem.n == t_cb(e, tt)
                    for hc in range(NHC):
                        c = e * NHC + hc
                        p = c % 2
                        t.wait_ge(wsp[p].h, d0w[p] + 48 * (c // 2 + 1))
                        for tt in range(NTT):
                            t.wait_ge(vsem.h, v_z(e, tt))
                            for i in range(4):
                                kk = lin_h(e, hc, tt, i)
                                if kk >= 2:
                                    t.wait_ge(vsem.h, v_hmul(*unlin_h(kk - 2)))
                                pp = kk % 2
                                for dc in range(NDC):
                                    ins = t.matmul(bank(pAa, pp),
                                                   w1t[p][:, dc * HC + i * 128:][:, 0:128],
                                                   qoz[:, dc * SL + tt * T:][:, 0:T],
                                                   start=(dc == 0), stop=(dc == NDC - 1))
                                tsem.inc(ins)
                                for dc in range(NDC):
                                    ins = t.matmul(bank(pAa, 2 + pp),
                                                   w3t[p][:, dc * HC + i * 128:][:, 0:128],
                                                   qoz[:, dc * SL + tt * T:][:, 0:T],
                                                   start=(dc == 0), stop=(dc == NDC - 1))
                                tsem.inc(ins)
                                assert tsem.n == t_h3(e, hc, tt, i)
                            t.wait_ge(vsem.h, v_hmul(e, hc, tt, 3))
                            for dc in range(NDC):
                                kk = lin_y(e, hc, tt, dc)
                                if kk >= 2:
                                    t.wait_ge(vsem.h, v_yadd(*unlin_y(kk - 2)))
                                pp = kk % 2
                                for i in range(4):
                                    ins = t.matmul(bank(pBa, pp),
                                                   w2t[p][:, i * D + dc * 128:][:, 0:128],
                                                   hbuf[:, i * T:(i + 1) * T],
                                                   start=(i == 0), stop=(i == 3))
                                tsem.inc(ins)
                                assert tsem.n == t_y(e, hc, tt, dc)

            @block.vector
            def _(v, call=call, a_enter=a_enter):
                for e in range(E):
                    for tt in range(NTT):
                        if e == 0 and tt == 0:
                            v.wait_ge(asem.h, a_enter)
                        v.wait_ge(tsem.h, t_cb(e, tt))
                        if e > 0:
                            v.wait_ge(tsem.h, t_h3(e - 1, NHC - 1, tt, 3))
                        ins = None
                        for dc in range(NDC):
                            ins = v.tensor_mul(qoz[:, dc * SL + tt * T:][:, 0:T],
                                               ub[:, dc * SL + tt * T:][:, 0:T],
                                               pBa[:, 1024:1536])
                        vsem.inc(ins)
                        assert vsem.n == v_z(e, tt)
                    for hc in range(NHC):
                        for tt in range(NTT):
                            for i in range(4):
                                kk = lin_h(e, hc, tt, i)
                                pp = kk % 2
                                v.wait_ge(asem.h, a_silu(e, hc, tt, i))
                                v.wait_ge(tsem.h, t_h3(e, hc, tt, i))
                                vsem.inc(v.tensor_mul(hbuf[:, i * T:(i + 1) * T],
                                                      s1[pp][:, :], bank(pAa, 2 + pp)))
                                assert vsem.n == v_hmul(e, hc, tt, i)
                            for dc in range(NDC):
                                kk = lin_y(e, hc, tt, dc)
                                pp = kk % 2
                                v.wait_ge(tsem.h, t_y(e, hc, tt, dc))
                                x_sl = xfm[:, dc * SL + tt * T:][:, 0:T]
                                vsem.inc(v.scalar_tensor_tensor(x_sl, bank(pBa, pp), 0.5,
                                                                x_sl, ALU.mult, ALU.add))
                                assert vsem.n == v_yadd(e, hc, tt, dc)

            @block.scalar
            def _(s, call=call):
                for e in range(E):
                    for hc in range(NHC):
                        for tt in range(NTT):
                            for i in range(4):
                                kk = lin_h(e, hc, tt, i)
                                pp = kk % 2
                                s.wait_ge(tsem.h, t_h1(e, hc, tt, i))
                                if kk >= 2:
                                    s.wait_ge(vsem.h, v_hmul(*unlin_h(kk - 2)))
                                asem.inc(s.activation(s1[pp][:, :], bank(pAa, pp), AF.Silu))
                                assert asem.n == a_silu(e, hc, tt, i)

        def attention(stop_after="all"):
            a_enter = asem.n
            for wi, wsrc_ in enumerate([wqT, wkT, wvT]):
                t_pre = tsem.n

                @block.gpsimd
                def _(g, wsrc_=wsrc_, t_pre=t_pre, wi=wi):
                    g.wait_ge(tsem.h, t_pre)
                    if wi == 2:
                        g.dma_start(out=bvrowb[:, :], in_=bvd[None, :]).then_inc(dsem.h, 16)
                        dsem.n += 16
                    g.dma_start(out=wta[:, :].rearrange("p (dc f) -> p dc f", f=D),
                                in_=wsrc_.rearrange("(dc p) f -> p dc f", p=128)
                                ).then_inc(dsem.h, 16)
                    dsem.n += 16
                d_w = dsem.n

                if wi == 0:
                    a_pre = asem.n

                    @block.tensor
                    def _(t, d_w=d_w, a_pre=a_pre, a_enter=a_enter):
                        t.wait_ge(dsem.h, d_w)
                        t.wait_ge(asem.h, a_enter)
                        for fc in range(NDC):
                            for tt in range(NTT):
                                k = fc * NTT + tt
                                if k >= 2:
                                    t.wait_ge(asem.h, a_pre + k - 1)
                                for dc in range(NDC):
                                    ins = t.matmul(bank(pAa, k % 2),
                                                   wta[:, dc * D + fc * 128:][:, 0:128],
                                                   ub[:, dc * SL + tt * T:][:, 0:T],
                                                   start=(dc == 0), stop=(dc == NDC - 1))
                                tsem.inc(ins)

                    @block.scalar
                    def _(s, t_pre=t_pre):
                        for fc in range(NDC):
                            for tt in range(NTT):
                                k = fc * NTT + tt
                                s.wait_ge(tsem.h, t_pre + k + 1)
                                asem.inc(s.activation(qoz[:, fc * SL + tt * T:][:, 0:T],
                                                      bank(pAa, k % 2), AF.Identity,
                                                      bias=battn[:, 0 * NDC + fc:][:, 0:1],
                                                      scale=SCALE))
                elif wi == 1:
                    a_pre = asem.n
                    q_pre = qsem.n

                    @block.tensor
                    def _(t, d_w=d_w, a_pre=a_pre):
                        t.wait_ge(dsem.h, d_w)
                        t.wait_ge(asem.h, a_pre)
                        for fc in range(NDC):
                            for tt in range(NTT):
                                k = fc * NTT + tt
                                if k >= 2:
                                    t.wait_ge(asem.h, a_pre + k - 1)
                                for dc in range(NDC):
                                    ins = t.matmul(bank(pAa, k % 2),
                                                   wta[:, dc * D + fc * 128:][:, 0:128],
                                                   ub[:, dc * SL + tt * T:][:, 0:T],
                                                   start=(dc == 0), stop=(dc == NDC - 1))
                                tsem.inc(ins)

                    @block.scalar
                    def _(s, t_pre=t_pre, q_pre=q_pre, a_pre=a_pre):
                        for fc in range(NDC):
                            for tt in range(NTT):
                                k = fc * NTT + tt
                                s.wait_ge(tsem.h, t_pre + k + 1)
                                if fc >= 1 and tt == 0:
                                    s.wait_ge(qsem.h, q_pre + 16 * fc)
                                asem.inc(s.activation(kvx[:, tt * T:(tt + 1) * T],
                                                      bank(pAa, k % 2), AF.Identity,
                                                      bias=battn[:, 1 * NDC + fc:][:, 0:1]))

                    @block.sync
                    def _(sy, a_pre=a_pre):
                        for fc in range(NDC):
                            sy.wait_ge(asem.h, a_pre + NTT * (fc + 1))
                            sy.dma_start(out=kexp[fc * 128:(fc + 1) * 128, :],
                                         in_=kvx[:, :]).then_inc(qsem.h, 16)
                            qsem.n += 16
                    q_kexp = qsem.n

                    @block.gpsimd
                    def _(g, q_kexp=q_kexp):
                        g.wait_ge(qsem.h, q_kexp)
                        g.collective_compute(
                            "AllGather", ALU.bypass,
                            replica_groups=[[0, 1], [2, 3], [4, 5], [6, 7]],
                            ins=[kexp.opt()], outs=[kg.opt()]).then_inc(ck.h, 1)
                        ck.n = 1
                elif wi == 2:
                    v_pre = vsem.n
                    q_pre = qsem.n

                    @block.tensor
                    def _(t, d_w=d_w, v_pre=v_pre):
                        t.wait_ge(dsem.h, d_w)
                        t.wait_ge(vsem.h, v_pre)
                        for tc in range(NQC):
                            for fb in range(2):
                                k = tc * 2 + fb
                                if k >= 4:
                                    t.wait_ge(vsem.h, v_pre + k - 3)
                                for dc in range(NDC):
                                    t.matmul(bank(pBa, k % 4),
                                             ub[:, dc * SL + tc * 128:][:, 0:128],
                                             wta[:, dc * D + fb * T:][:, 0:T],
                                             start=(dc == 0), stop=False)
                                ins = t.matmul(bank(pBa, k % 4), onesb[:, 0:128],
                                               bvrowb[0:1, fb * T:(fb + 1) * T],
                                               start=False, stop=True)
                                tsem.inc(ins)

                    @block.vector
                    def _(v, t_pre=t_pre, q_pre=q_pre):
                        for tc in range(NQC):
                            for fb in range(2):
                                k = tc * 2 + fb
                                v.wait_ge(tsem.h, t_pre + k + 1)
                                if tc >= 1 and fb == 0:
                                    v.wait_ge(qsem.h, q_pre + 16 * tc)
                                vsem.inc(v.tensor_copy(kvx[:, fb * T:(fb + 1) * T],
                                                       bank(pBa, k % 4)))

                    @block.sync
                    def _(sy, v_pre=v_pre):
                        for tc in range(NQC):
                            sy.wait_ge(vsem.h, v_pre + 2 * (tc + 1))
                            sy.dma_start(out=vexp[tc * 128:(tc + 1) * 128, :],
                                         in_=kvx[:, 0:1024]).then_inc(qsem.h, 16)
                            qsem.n += 16
                    q_vexp = qsem.n

                    @block.gpsimd
                    def _(g, q_vexp=q_vexp):
                        g.wait_ge(qsem.h, q_vexp)
                        g.collective_compute(
                            "AllGather", ALU.bypass,
                            replica_groups=[[0, 1], [2, 3], [4, 5], [6, 7]],
                            ins=[vexp.opt()], outs=[vg.opt()]).then_inc(cv.h, 1)
                        cv.n = 1

            t_postv = tsem.n

            @block.gpsimd
            def _(g, t_postv=t_postv):
                g.wait_ge(tsem.h, t_postv)
                for p in range(2):
                    vv = vh[p].rearrange("p (k c) -> p k c", c=65)
                    g.memset(vv[:, :, 64:65], 1.0)
                    g.memset(khb[p][64:128, :], 0.0)
                    g.memset(khb[p][64:65, :], 1.0)
                    g.memset(qtile[p][64:128, :], 0.0)
                g.memset(srow[:, :], 1.0).then_inc(esem.h, 1)
                esem.n += 1

            if stop_after == "qkv":
                return
            NF = 8 * NQC
            tfill = {}
            ttp = {}
            tBsT = {}
            tBAV = {}
            tBrs = {}
            tBend = {}
            tAend = {}
            vbaseA = {}
            sbaseB = {}
            q_mst = {}
            q_kb = {}
            prev_osc = [0]
            prev_vred_tail = [None, None]

            for s in range(NH + 1):
                hA = s if s < NH else None
                hB = s - 1 if s >= 1 else None
                pA_ = None if hA is None else hA % 2
                pB_ = None if hB is None else hB % 2

                if hA is not None:
                    kbe = kbp[pA_]
                    loA = 64 * (hA % 2)
                    fcqA = hA // 2

                    @block.sync
                    def _(sy, hA=hA, loA=loA, fcqA=fcqA, kbe=kbe, pA_=pA_, s=s):
                        if s == 0:
                            sy.wait_ge(ck.h, 1)
                            sy.wait_ge(cv.h, 1)
                            sy.wait_ge(esem.h, esem.n)
                        if hA >= 2:
                            sy.wait_ge(tsem.h, tBend[hA - 2])
                        sy.dma_start(out=khb[pA_][0:64, 0:SL],
                                     in_=kg[64 * hA:64 * hA + 64, :]).then_inc(kbe.h, 16)
                        sy.dma_start(out=khb[pA_][0:64, SL:2 * SL],
                                     in_=kg[D + 64 * hA:D + 64 * hA + 64, :]).then_inc(kbe.h, 16)
                        vv = vh[pA_].rearrange("p (k c) -> p k c", c=65)
                        sy.dma_start(out=vv[:, :, 0:64],
                                     in_=vg.rearrange("(k p) f -> p k f", p=128)
                                     [:, :, 64 * hA:64 * hA + 64]).then_inc(kbe.h, 16)
                        sy.dma_start(out=qtile[pA_][0:64, :],
                                     in_=qoz[loA:loA + 64, fcqA * SL:(fcqA + 1) * SL]
                                     ).then_inc(kbe.h, 16)
                        kbe.n += 64
                    q_kb[hA] = kbp[pA_].n
                if hB is not None:
                    q_kB = q_kb[hB]

                vbase = vsem.n
                if hA is not None:
                    vbaseA[hA] = vbase

                def v_red(qc, f, vbase=vbase, hB=hB):
                    ex = 2 * (qc // 4) if hB is not None else 0
                    return vbase + 10 * qc + ex + f + 1

                def v_rfin(qc, vbase=vbase, hB=hB):
                    ex = 2 * (qc // 4) if hB is not None else 0
                    return vbase + 10 * qc + ex + 9

                def v_copy(qc, vbase=vbase, hB=hB):
                    ex = 2 * (qc // 4) if hB is not None else 0
                    return vbase + 10 * qc + ex + 10

                def v_recip(qb, vbase=vbase, hA=hA):
                    return vbase + (42 * qb + 41 if hA is not None else 2 * qb + 1)

                def v_osc(qb, vbase=vbase, hA=hA):
                    return vbase + (42 * qb + 42 if hA is not None else 2 * qb + 2)

                sbase = asem.n
                if hB is not None:
                    sbaseB[hB] = sbase

                def s_exp(qb, g, sbase=sbase):
                    return sbase + 33 * qb + g + 1

                def s_rsb(qb, sbase=sbase):
                    return sbase + 33 * qb + 33

                qpred = [None]

                @block.tensor
                def _(t, s=s, hA=hA, hB=hB, pA_=pA_, pB_=pB_,
                      v_red=v_red, v_rfin=v_rfin, v_copy=v_copy,
                      v_recip=v_recip, v_osc=v_osc, s_exp=s_exp, s_rsb=s_rsb,
                      vbase=vbase, sbase=sbase):
                    items = []
                    if hA is not None:
                        loA = 64 * (hA % 2)
                        fcqA = hA // 2

                        def mk_fill(qc, f, hA=hA, pA_=pA_):
                            def emit():
                                FF = 8 * qc + f
                                if FF == 0:
                                    t.wait_ge(kbp[pA_].h, q_kb[hA])
                                if FF < 2:
                                    if prev_vred_tail[f] is not None:
                                        t.wait_ge(vsem.h, prev_vred_tail[f])
                                elif f == 0 and FF % 8 == 0:
                                    t.wait_ge(vsem.h, v_copy(qc - 1))
                                else:
                                    t.wait_ge(vsem.h, v_red((FF - 2) // 8, (FF - 2) % 8))
                                q_sl = qtile[pA_][0:64, qc * 128:(qc + 1) * 128]
                                ins = t.matmul(pBa[:, (2 + f % 2) * 512:(3 + f % 2) * 512],
                                               q_sl, khb[pA_][0:64, f * 512:(f + 1) * 512],
                                               start=True, stop=True)
                                tsem.inc(ins)
                                tfill[(hA, FF)] = tsem.n
                            return emit

                        def mk_tp(qc, hA=hA):
                            def emit():
                                t.wait_ge(vsem.h, v_rfin(qc))
                                tsem.inc(t.transpose(pBa[0:1, 1024:1152],
                                                     gsc[:, 0:1], idf[:, :]))
                                ttp[(hA, qc)] = tsem.n
                            return emit

                        for qc in range(NQC):
                            for f in range(8):
                                items.append(mk_fill(qc, f))
                            items.append(mk_tp(qc))
                    it = iter(items)

                    def inject(n):
                        for _i in range(n):
                            em = next(it, None)
                            if em is None:
                                return
                            em()

                    if hB is not None:
                        loB = 64 * (hB % 2)
                        kbe = kbp[pB_]
                        t.wait_ge(kbe.h, q_kB)
                        t.wait_ge(qsem.h, q_mst[hB])

                        def attsl(g):
                            sl = g % 8
                            return att[sl // 4][:, (sl % 4) * 512:(sl % 4 + 1) * 512]

                        def sTq(qb, g, hB=hB, pB_=pB_):
                            ins = t.matmul(pAa[:, (g % 4) * 512:(g % 4 + 1) * 512],
                                           khb[pB_][0:128, g * 128:(g + 1) * 128],
                                           qtile[pB_][0:128, qb * T:(qb + 1) * T],
                                           start=True, stop=True)
                            tsem.inc(ins)
                            tBsT[(hB, qb, g)] = tsem.n
                        for qb in range(NTT):
                            for g in range(4):
                                if qb == 0:
                                    t.wait_ge(asem.h, sbase)
                                else:
                                    t.wait_ge(asem.h, s_exp(qb - 1, 28 + g))
                                sTq(qb, g)
                            for g in range(NKC):
                                inject((2 if g < 4 else 1) if hA is not None else 0)
                                t.wait_ge(asem.h, s_exp(qb, g))
                                if g == 0:
                                    t.wait_ge(vsem.h,
                                              v_osc(qb - 1) if qb > 0 else prev_osc[0])
                                ins = t.matmul(pBa[0:65, 0:512],
                                               vh[pB_][:, g * 65:(g + 1) * 65],
                                               attsl(g),
                                               start=(g == 0), stop=(g == NKC - 1))
                                tsem.inc(ins)
                                tBAV[(hB, qb, g)] = tsem.n
                                if g + 4 < NKC:
                                    sTq(qb, g + 4)
                            t.wait_ge(vsem.h, v_recip(qb))
                            tsem.inc(t.matmul(pBa[:, 512:1024], onesf[64:65, 0:128],
                                              srow[64:65, :], start=True, stop=True))
                            tBrs[(hB, qb)] = tsem.n
                    inject(10 ** 9)
                    if hA is not None:
                        tAend[hA] = tsem.n
                    if hB is not None:
                        tBend[hB] = tsem.n

                if hB is not None:
                    @block.scalar
                    def _(s_, hB=hB):
                        def attsl(g):
                            sl = g % 8
                            return att[sl // 4][:, (sl % 4) * 512:(sl % 4 + 1) * 512]
                        for qb in range(NTT):
                            for g in range(NKC):
                                s_.wait_ge(tsem.h, tBsT[(hB, qb, g)])
                                if g >= 8:
                                    s_.wait_ge(tsem.h, tBAV[(hB, qb, g - 8)])
                                elif qb > 0:
                                    s_.wait_ge(tsem.h, tBAV[(hB, qb - 1, 24 + g)])
                                asem.inc(s_.activation(attsl(g),
                                                       pAa[:, (g % 4) * 512:(g % 4 + 1) * 512],
                                                       AF.Exp))
                            s_.wait_ge(tsem.h, tBrs[(hB, qb)])
                            asem.inc(s_.copy(rsb[0:64, :], pBa[0:64, 512:1024]))

                qpred[0] = qsem.n

                @block.vector
                def _(v, s=s, hA=hA, hB=hB, pB_=pB_, vbase=vbase, sbase=sbase,
                      v_red=v_red, v_rfin=v_rfin, v_copy=v_copy,
                      v_recip=v_recip, v_osc=v_osc, s_rsb=s_rsb, qpred=qpred):
                    def a_block(qc, hA=hA):
                        if qc == 0 and hA is not None and hA >= 1:
                            v.wait_ge(qsem.h, q_mst[hA - 1])
                        for f in range(8):
                            v.wait_ge(tsem.h, tfill[(hA, 8 * qc + f)])
                            vsem.inc(v.reduce_max(gcol[:, f:f + 1],
                                                  pBa[:, (2 + f % 2) * 512:(3 + f % 2) * 512],
                                                  axis=AX.X))
                        v.wait_ge(vsem.h, vsem.n)
                        vsem.inc(v.reduce_max(gsc[:, 0:1], gcol[:, 0:8], axis=AX.X,
                                              negate=True))
                        v.wait_ge(tsem.h, ttp[(hA, qc)])
                        vsem.inc(v.tensor_copy(mrow[0:1, qc * 128:(qc + 1) * 128],
                                               pBa[0:1, 1024:1152]))

                    def b_block(qb, hB=hB, pB_=pB_):
                        loB = 64 * (hB % 2)
                        fcqB = hB // 2
                        v.wait_ge(tsem.h, tBAV[(hB, qb, 31)])
                        vsem.inc(v.reciprocal(srow[64:65, :], pBa[64:65, 0:512]))
                        v.wait_ge(asem.h, s_rsb(qb))
                        if loB == 0:
                            vsem.inc(v.tensor_mul(
                                qoz[0:64, fcqB * SL + qb * T:][:, 0:T],
                                pBa[0:64, 0:512], rsb[0:64, :]))
                        else:
                            if qb >= 2:
                                v.wait_ge(qsem.h, qpred[0] + 16 * (qb - 1))
                            vsem.inc(v.tensor_mul(obuf2[qb % 2][:, :],
                                                  pBa[0:64, 0:512], rsb[0:64, :]))

                    if hA is not None and hB is not None:
                        for qb in range(NTT):
                            for qc in range(4 * qb, 4 * qb + 4):
                                a_block(qc)
                            b_block(qb)
                    elif hA is not None:
                        for qc in range(NQC):
                            a_block(qc)
                    else:
                        for qb in range(NTT):
                            b_block(qb)
                    if hB is not None:
                        assert vsem.n == v_osc(NTT - 1), (vsem.n, v_osc(NTT - 1))
                    else:
                        assert vsem.n == v_copy(NQC - 1), (vsem.n, v_copy(NQC - 1))

                if hA is not None:
                    prev_vred_tail[0] = v_copy(NQC - 1)
                    prev_vred_tail[1] = v_red(NQC - 1, 7)
                if hB is not None:
                    prev_osc[0] = v_osc(NTT - 1)

                @block.sync
                def _(sy, hA=hA, hB=hB, v_copy=v_copy, v_osc=v_osc):
                    if hB is not None and hB % 2 == 1:
                        fcqB = hB // 2
                        for qb in range(NTT):
                            sy.wait_ge(vsem.h, v_osc(qb))
                            sy.dma_start(out=qoz[64:128, fcqB * SL + qb * T:][:, 0:T],
                                         in_=obuf2[qb % 2][:, :]).then_inc(qsem.h, 16)
                            qsem.n += 16
                    if hA is not None:
                        sy.wait_ge(vsem.h, v_copy(NQC - 1))
                        sy.dma_start(out=qtile[pA_][64:65, :],
                                     in_=mrow[0:1, :]).then_inc(qsem.h, 16)
                        qsem.n += 16
                if hA is not None:
                    q_mst[hA] = qsem.n

            t_heads = tsem.n

            @block.gpsimd
            def _(g, t_heads=t_heads):
                g.wait_ge(tsem.h, t_heads)
                g.dma_start(out=wta[:, :].rearrange("p (dc f) -> p dc f", f=D),
                            in_=woT.rearrange("(dc p) f -> p dc f", p=128)
                            ).then_inc(dsem.h, 16)
                dsem.n += 16
            d_w_wo = dsem.n
            v_wo0, q_wo0 = vsem.n, qsem.n

            @block.tensor
            def _(t, v_wo0=v_wo0, q_wo0=q_wo0, d_w_wo=d_w_wo):
                t.wait_ge(vsem.h, v_wo0)
                t.wait_ge(qsem.h, q_wo0)
                t.wait_ge(dsem.h, d_w_wo)
                for dc in range(NDC):
                    for tt in range(NTT):
                        k = dc * NTT + tt
                        if k >= 2:
                            t.wait_ge(vsem.h, v_wo0 + k - 1)
                        for fc in range(NDC):
                            ins = t.matmul(bank(pAa, k % 2),
                                           wta[:, fc * D + dc * 128:][:, 0:128],
                                           qoz[:, fc * SL + tt * T:][:, 0:T],
                                           start=(fc == 0), stop=(fc == NDC - 1))
                        tsem.inc(ins)
            t_wo0 = tsem.n - NDC * NTT

            @block.vector
            def _(v, t_wo0=t_wo0):
                for dc in range(NDC):
                    for tt in range(NTT):
                        k = dc * NTT + tt
                        v.wait_ge(tsem.h, t_wo0 + k + 1)
                        x_sl = xfm[:, dc * SL + tt * T:][:, 0:T]
                        vsem.inc(v.scalar_tensor_tensor(
                            x_sl, bank(pAa, k % 2), battn[:, 3 * NDC + dc:][:, 0:1],
                            x_sl, ALU.add, ALU.add))

        if stop_after != "p0":
            layernorm(fold_x=True, with_gate=True)
        if stop_after in ("u1", "c1"):
            t_d, v_d, a_d = tsem.n, vsem.n, asem.n

            @block.gpsimd
            def _(g):
                g.wait_ge(tsem.h, t_d)
                g.wait_ge(vsem.h, v_d)
                g.wait_ge(asem.h, a_d)
                if stop_after == "u1":
                    for dc in range(NDC):
                        g.dma_start(out=outT[dc * 128:(dc + 1) * 128, :],
                                    in_=ub[:, dc * SL:(dc + 1) * SL]).then_inc(dsem.h, 16)
                        dsem.n += 16
                else:
                    g.dma_start(out=outT[0:8, :], in_=cT[0:8, :]).then_inc(dsem.h, 16)
                    dsem.n += 16
                g.wait_ge(dsem.h, dsem.n)
        if stop_after not in ("p0", "ln1", "u1", "c1"):
            ffn_experts(0)
        if stop_after not in ("p0", "ln1", "u1", "c1", "ffn1"):
            layernorm(fold_x=False, with_gate=False)
            attention("qkv" if stop_after == "qkv" else "all")
        if stop_after not in ("p0", "ln1", "u1", "c1", "ffn1", "qkv", "attn"):
            layernorm(fold_x=True, with_gate=True)
            ffn_experts(1)

        t_fin, v_fin, a_fin = tsem.n, vsem.n, asem.n
        skip_out = stop_after in ("u1", "c1")

        @block.sync
        def _(sy):
            if skip_out:
                sy.wait_ge(dsem.h, dsem.n)
                return
            sy.wait_ge(vsem.h, v_fin)
            sy.wait_ge(tsem.h, t_fin)
            sy.wait_ge(asem.h, a_fin)
            for dc in range(NDC):
                sy.dma_start(out=outT[dc * 128:(dc + 1) * 128, :],
                             in_=xfm[:, dc * SL:(dc + 1) * SL]).then_inc(qsem.h, 16)
            qsem.n += NDC * 16
            sy.wait_ge(qsem.h, qsem.n)

    return nc


def kernel(L0_attn_x, L0_attn_Wq, L0_attn_bq, L0_attn_Wk, L0_attn_bk,
           L0_attn_Wv, L0_attn_bv, L0_attn_Wo, L0_attn_bo,
           L0_ffn_Wg, L0_ffn_bg, L0_ffn_W1, L0_ffn_W2, L0_ffn_W3):
    x = np.ascontiguousarray(np.asarray(L0_attn_x, dtype=np.float32)).reshape(B * S, D)
    com = dict(
        wqT=np.ascontiguousarray(np.asarray(L0_attn_Wq, np.float32).T),
        wkT=np.ascontiguousarray(np.asarray(L0_attn_Wk, np.float32).T),
        wvT=np.ascontiguousarray(np.asarray(L0_attn_Wv, np.float32).T),
        woT=np.ascontiguousarray(np.asarray(L0_attn_Wo, np.float32).T),
        bq=np.asarray(L0_attn_bq, np.float32), bk=np.asarray(L0_attn_bk, np.float32),
        bv=np.asarray(L0_attn_bv, np.float32), bo=np.asarray(L0_attn_bo, np.float32),
        wg=np.asarray(L0_ffn_Wg, np.float32), bg=np.asarray(L0_ffn_bg, np.float32),
        w1=np.ascontiguousarray(np.asarray(L0_ffn_W1, np.float32)),
        w2=np.ascontiguousarray(np.asarray(L0_ffn_W2, np.float32)),
        w3=np.ascontiguousarray(np.asarray(L0_ffn_W3, np.float32)),
    )
    in_maps = []
    for c in range(NCORE):
        xs = np.ascontiguousarray(x[c * SL:(c + 1) * SL, :].T)
        in_maps.append(dict(xT=xs, **com))
    nc = build_program()
    res = run_bass_kernel_spmd(nc, in_maps, core_ids=list(range(NCORE)))
    outs = [np.asarray(r["outT"]).T for r in res.results]
    return np.concatenate(outs, axis=0).reshape(B, S, D)


# version 08
# speedup vs baseline: 1.0562x; 1.0562x over previous
import numpy as np
import concourse.bass as bass
import concourse.mybir as mybir
from concourse.bass_utils import run_bass_kernel_spmd

F32 = mybir.dt.float32
BF16 = mybir.dt.bfloat16
AF = mybir.ActivationFunctionType
ALU = mybir.AluOpType
AX = mybir.AxisListType

NCORE = 8
B, S, D = 4, 4096, 1024
SL = 2048
NH, HD = 16, 64
E, H = 8, 4096
NDC = D // 128
T = 512
NTT = SL // T
HC = 512
NHC = H // HC
SF = 4096
NKC = SF // 128
NQC = SL // 128
SCALE = 0.125
EPS = 1e-5


class Sem:
    def __init__(self, nc, ctx, name):
        self.h = ctx.enter_context(nc.semaphore(name))
        self.n = 0

    def inc(self, inst, k=1):
        inst.then_inc(self.h, k)
        self.n += k
        return self.n


def build_program(stop_after="all"):
    nc = bass.Bass("TRN2", target_bir_lowering=False, detect_race_conditions=False)

    xT = nc.dram_tensor("xT", [D, SL], F32, kind="ExternalInput").ap()
    wqT = nc.dram_tensor("wqT", [D, D], F32, kind="ExternalInput").ap()
    wkT = nc.dram_tensor("wkT", [D, D], F32, kind="ExternalInput").ap()
    wvT = nc.dram_tensor("wvT", [D, D], F32, kind="ExternalInput").ap()
    woT = nc.dram_tensor("woT", [D, D], F32, kind="ExternalInput").ap()
    bqd = nc.dram_tensor("bq", [D], F32, kind="ExternalInput").ap()
    bkd = nc.dram_tensor("bk", [D], F32, kind="ExternalInput").ap()
    bvd = nc.dram_tensor("bv", [D], F32, kind="ExternalInput").ap()
    bod = nc.dram_tensor("bo", [D], F32, kind="ExternalInput").ap()
    wgd = nc.dram_tensor("wg", [D, E], F32, kind="ExternalInput").ap()
    bgd = nc.dram_tensor("bg", [E], F32, kind="ExternalInput").ap()
    w1d = nc.dram_tensor("w1", [E, D, H], F32, kind="ExternalInput").ap()
    w2d = nc.dram_tensor("w2", [E, H, D], F32, kind="ExternalInput").ap()
    w3d = nc.dram_tensor("w3", [E, D, H], F32, kind="ExternalInput").ap()
    outT = nc.dram_tensor("outT", [D, SL], F32, kind="ExternalOutput").ap()

    kexp = nc.dram_tensor("kexp", [D, SL], BF16).ap()
    vexp = nc.dram_tensor("vexp", [SL, D], BF16).ap()
    kg = nc.dram_tensor("kg", [2 * D, SL], BF16).ap()
    vg = nc.dram_tensor("vg", [SF, D], BF16).ap()
    cdram = nc.dram_tensor("cdram", [E, SL], BF16).ap()

    from contextlib import ExitStack
    with ExitStack() as ctx:
        BASE = 16512
        off = [BASE]

        def At(name, shape, dt_, offset=None):
            nbytes = int(np.prod(shape[1:])) * mybir.dt.size(dt_)
            if offset is None:
                offset = off[0]
                off[0] += (nbytes + 31) // 32 * 32
            return nc.alloc_sbuf_tensor_at(name, list(shape), dt_, offset=offset).ap()

        xfm = At("xfm", [128, NDC * SL], F32)
        ub = At("ub", [128, NDC * SL], BF16)
        qoz = At("qoz", [128, NDC * SL], BF16)
        wreg = off[0]
        w1t = [At(f"w1t{p}", [128, NDC * HC], BF16, wreg + p * 24576) for p in range(2)]
        w3t = [At(f"w3t{p}", [128, NDC * HC], BF16, wreg + p * 24576 + 8192) for p in range(2)]
        w2t = [At(f"w2t{p}", [128, 4 * D], BF16, wreg + p * 24576 + 16384) for p in range(2)]
        wta = At("wta", [128, NDC * D], BF16, wreg)
        qtile = [At(f"qtile{p}", [128, SL], BF16, wreg + p * 4096) for p in range(2)]
        khb = [At(f"khb{p}", [128, SF], BF16, wreg + 8192 + p * 8192) for p in range(2)]
        vh = [At(f"vh{p}", [128, NKC * 65], BF16, wreg + 24576 + p * 4160) for p in range(2)]
        kvx = At("kvx", [128, SL], BF16, wreg + 16384)
        bvrowb = At("bvrowb", [1, D], BF16, wreg + 20480)
        off[0] = wreg + 49152
        screg = off[0]
        sq = [At(f"sq{p}", [128, T], F32, screg + p * 2048) for p in range(2)]
        uf32 = [At(f"uf32{p}", [128, T], F32, screg + 4096 + p * 2048) for p in range(2)]
        hbuf = At("hbuf", [128, 4 * T], BF16, screg + 8192)
        s1 = [At(f"s1{p}", [128, T], F32, screg + 12288 + p * 2048) for p in range(2)]
        mrow = At("mrow", [1, SL], BF16, screg)
        rsb = At("rsb", [128, 512], F32, screg + 5120)
        srow = At("srow", [128, 512], F32, screg + 7168)
        obuf2 = [At(f"obuf{p}", [64, 512], BF16, screg + 9216 + p * 1024) for p in range(2)]
        att = [At("att0", [128, 4 * 512], BF16, wreg + 32896),
               At("att1", [128, 4 * 512], BF16, wreg + 36992)]
        off[0] = screg + 16384
        rowbuf = At("rowbuf", [1, 4 * T], F32)
        ctoff = off[0]
        cT = At("cT", [8, SL], BF16)
        crow = At("crow", [1, SL], BF16, offset=ctoff)
        battn = At("battn", [128, 4 * NDC], F32)
        onesf = At("onesf", [128, 128], F32)
        onesb = At("onesb", [1, 128], BF16)
        idf = At("idf", [128, 128], F32)
        idb = At("idb", [128, 128], BF16)
        wg_sb = At("wg_sb", [128, NDC * E], F32)
        g1b = At("g1b", [128, E], F32)
        bgb = At("bgb", [128, E], F32)
        grow = At("grow", [1, 2 * E], F32)
        gsc = At("gsc", [128, 4 * E], F32)
        gcol = At("gcol", [128, 8], F32)
        assert off[0] <= 229344, off[0]

        ROW_TMP, ROW_R, ROW_MR, ROW_NM = (rowbuf[0:1, k * T:(k + 1) * T] for k in range(4))

        pA = ctx.enter_context(nc.psum_tensor("pA", [128, 2048], F32))
        pB = ctx.enter_context(nc.psum_tensor("pB", [128, 2048], F32))
        pAa, pBa = pA[:, :], pB[:, :]
        pAb = pA[:, :].bitcast(BF16)

        def bank(ap_, i):
            return ap_[:, i * 512:(i + 1) * 512]

        names = ["qsem", "dsem", "tsem", "vsem", "asem", "ck", "cv", "esem",
                 "ws0", "ws1", "ks0", "ks1", "kb0", "kb1"]
        SM = {nm: Sem(nc, ctx, nm) for nm in names}
        qsem, dsem, tsem, vsem, asem, ck, cv, esem, ws0, ws1, ks0, ks1, kb0, kb1 = \
            (SM[n] for n in names)
        wsp = [ws0, ws1]
        ksp = [ks0, ks1]
        kbp = [kb0, kb1]
        block = ctx.enter_context(nc.Block())

        @block.gpsimd
        def _(g):
            g.memset(onesf[:, :], 1.0)
            g.memset(onesb[:, :], 1.0)
            g.memset(idf[:, :], 1.0)
            esem.inc(g.memset(idb[:, :], 1.0))
            g.wait_ge(esem.h, 1)
            g.affine_select(idf[:, :], idf[:, :], [[1, 128]], ALU.is_equal, 0.0,
                            base=0, channel_multiplier=-1)
            esem.inc(g.affine_select(idb[:, :], idb[:, :], [[1, 128]], ALU.is_equal, 0.0,
                                     base=0, channel_multiplier=-1))
            with nc.allow_non_contiguous_dma("tiny one-time bias loads"):
                for j, bb in enumerate([bqd, bkd, bvd, bod]):
                    g.dma_start(out=battn[:, j * NDC:(j + 1) * NDC],
                                in_=bb.rearrange("(dc p) -> p dc", p=128),
                                allow_slow_non_contiguous=True).then_inc(dsem.h, 16)
            g.dma_start(out=wg_sb[:, :].rearrange("p (dc e) -> p dc e", e=E),
                        in_=wgd.rearrange("(dc p) e -> p dc e", p=128)).then_inc(dsem.h, 16)
            g.dma_start(out=grow[0:1, E:2 * E], in_=bgd[None, :]).then_inc(dsem.h, 16)
            dsem.n += 6 * 16

        @block.sync
        def _(sy):
            for dc in range(NDC):
                sy.dma_start(out=xfm[:, dc * SL:(dc + 1) * SL],
                             in_=xT[dc * 128:(dc + 1) * 128, :]).then_inc(qsem.h, 16)
            qsem.n += NDC * 16
        q_x = qsem.n

        @block.tensor
        def _(t):
            t.wait_ge(dsem.h, dsem.n)
            t.wait_ge(esem.h, 2)
            t.wait_ge(qsem.h, q_x)
            for dc in range(NDC):
                ins = t.matmul(pBa[0:1, 0:E], onesf[:, 0:1], wg_sb[:, dc * E:(dc + 1) * E],
                               start=(dc == 0), stop=(dc == NDC - 1))
            tsem.inc(ins)
        ev0 = tsem.n

        @block.vector
        def _(v):
            v.wait_ge(tsem.h, ev0)
            vsem.inc(v.tensor_copy(grow[0:1, 0:E], pBa[0:1, 0:E]))
        ev1 = vsem.n

        @block.tensor
        def _(t):
            t.wait_ge(vsem.h, ev1)
            tsem.inc(t.matmul(pAa[:, 0:2 * E], onesf[0:1, 0:128], grow[0:1, :],
                              start=True, stop=True))
        ev2 = tsem.n

        @block.vector
        def _(v):
            v.wait_ge(tsem.h, ev2)
            v.tensor_copy(g1b[:, :], pAa[:, 0:E])
            v.tensor_scalar_mul(battn[:, 0:NDC], battn[:, 0:NDC], SCALE)
            vsem.inc(v.tensor_copy(bgb[:, :], pAa[:, E:2 * E]))

        def gate(tt, v_rows):
            for tc in range(4):
                vg0 = vsem.n
                tok0 = tt * T + tc * 128

                @block.tensor
                def _(t, tt=tt, tc=tc, tok0=tok0, vg0=vg0, v_rows=v_rows):
                    if tc == 0:
                        t.wait_ge(vsem.h, v_rows)
                    else:
                        t.wait_ge(vsem.h, vg0)
                    for dc in range(NDC):
                        ins = t.matmul(pBa[0:128, 0:E], xfm[:, dc * SL + tok0:][:, 0:128],
                                       wg_sb[:, dc * E:(dc + 1) * E],
                                       start=(dc == 0), stop=(dc == NDC - 1))
                    tsem.inc(ins)
                    tsem.inc(t.transpose(pAa[0:128, 1536:1537],
                                         ROW_NM[0:1, tc * 128:(tc + 1) * 128], onesf[0:1, 0:1]))
                    tsem.inc(t.transpose(pAa[0:128, 1537:1538],
                                         ROW_R[0:1, tc * 128:(tc + 1) * 128], onesf[0:1, 0:1]))
                t_g = tsem.n

                @block.vector
                def _(v, t_g=t_g):
                    def vf(ins):
                        vsem.inc(ins)
                        v.wait_ge(vsem.h, vsem.n)
                    v.wait_ge(tsem.h, t_g)
                    vf(v.tensor_copy(gcol[:, 6:8], pAa[:, 1536:1538]))
                    vf(v.scalar_tensor_tensor(gsc[:, 0:E], g1b[:, :], gcol[:, 6:7],
                                              pBa[0:128, 0:E], ALU.mult, ALU.add))
                    vf(v.tensor_scalar(gsc[:, 0:E], gsc[:, 0:E], gcol[:, 7:8], None, ALU.mult))
                    vf(v.tensor_add(gsc[:, 0:E], gsc[:, 0:E], bgb[:, :]))
                    vf(v.reduce_max(gcol[:, 0:1], gsc[:, 0:E], axis=AX.X))
                    vf(v.tensor_scalar_mul(gcol[:, 1:2], gcol[:, 0:1], -1.0))
                    vf(v.tensor_scalar(gsc[:, E:2 * E], gsc[:, 0:E], gcol[:, 0:1], None, ALU.is_ge))
                    vf(v.scalar_tensor_tensor(gsc[:, 2 * E:3 * E], gsc[:, E:2 * E], -1e30,
                                              gsc[:, 0:E], ALU.mult, ALU.add))
                    vsem.inc(v.reduce_max(gcol[:, 2:3], gsc[:, 2 * E:3 * E], axis=AX.X))
                v_g1 = vsem.n

                @block.scalar
                def _(s, v_g1=v_g1):
                    s.wait_ge(vsem.h, v_g1)
                    s.activation(gsc[:, 2 * E:3 * E], gsc[:, 0:E], AF.Exp, bias=gcol[:, 1:2])
                    asem.inc(s.activation(gcol[:, 3:4], gcol[:, 2:3], AF.Exp, bias=gcol[:, 1:2]))
                a_g = asem.n

                @block.vector
                def _(v, a_g=a_g):
                    def vf(ins):
                        vsem.inc(ins)
                        v.wait_ge(vsem.h, vsem.n)
                    v.wait_ge(asem.h, a_g)
                    vf(v.tensor_scalar_add(gcol[:, 4:5], gcol[:, 3:4], 1.0))
                    vf(v.reciprocal(gcol[:, 5:6], gcol[:, 4:5]))
                    vf(v.tensor_scalar(gsc[:, E:2 * E], gsc[:, 0:E], gcol[:, 2:3], None, ALU.is_ge))
                    vf(v.tensor_mul(gsc[:, 2 * E:3 * E], gsc[:, 2 * E:3 * E],
                                    gsc[:, E:2 * E]))
                    vsem.inc(v.tensor_scalar(gsc[:, 3 * E:4 * E].bitcast(BF16)[:, 0:E],
                                             gsc[:, 2 * E:3 * E], gcol[:, 5:6], None, ALU.mult))
                v_g2 = vsem.n

                @block.tensor
                def _(t, v_g2=v_g2):
                    t.wait_ge(vsem.h, v_g2)
                    tsem.inc(t.transpose(pAb[0:8, 3072:3200],
                                         gsc[:, 3 * E:4 * E].bitcast(BF16)[:, 0:E], idb[:, :]))
                t_c = tsem.n

                @block.vector
                def _(v, t_c=t_c, tok0=tok0):
                    v.wait_ge(tsem.h, t_c)
                    vsem.inc(v.tensor_copy(cT[0:8, tok0:tok0 + 128], pAb[0:8, 3072:3200]))

        def layernorm(fold_x, with_gate):
            v_enter = vsem.n
            prev_vu = [v_enter]
            for tt in range(NTT):
                av0 = asem.n
                tv0 = tsem.n

                @block.scalar
                def _(s, tt=tt, tv0=tv0, v_enter=v_enter):
                    if tt == 0:
                        s.wait_ge(vsem.h, v_enter)
                        s.wait_ge(qsem.h, q_x)
                    for dc in range(NDC):
                        if dc >= 2:
                            s.wait_ge(tsem.h, tv0 + dc - 1)
                        asem.inc(s.activation(sq[dc % 2][:, :],
                                              xfm[:, dc * SL + tt * T:][:, 0:T], AF.Square))

                @block.tensor
                def _(t, tt=tt, av0=av0, wait_v=prev_vu[0]):
                    t.wait_ge(vsem.h, wait_v)
                    for dc in range(NDC):
                        x_sl = xfm[:, dc * SL + tt * T:][:, 0:T]
                        t.matmul(pBa[0:1, 1024:1536], onesf[:, 0:1], x_sl,
                                 start=(dc == 0), stop=(dc == NDC - 1))
                        t.wait_ge(asem.h, av0 + dc + 1)
                        tsem.inc(t.matmul(pBa[0:1, 1536:2048], onesf[:, 0:1], sq[dc % 2][:, :],
                                          start=(dc == 0), stop=(dc == NDC - 1)))
                t_sums = tsem.n

                @block.vector
                def _(v, t_sums=t_sums):
                    v.wait_ge(tsem.h, t_sums)
                    v.tensor_scalar_mul(pBa[0:1, 1024:1536], pBa[0:1, 1024:1536], 1.0 / D)
                    vsem.inc(v.tensor_scalar_mul(pBa[0:1, 1536:2048], pBa[0:1, 1536:2048], 1.0 / D))
                v_mean = vsem.n

                @block.scalar
                def _(s, v_mean=v_mean):
                    s.wait_ge(vsem.h, v_mean)
                    asem.inc(s.activation(ROW_TMP, pBa[0:1, 1024:1536], AF.Square))
                a_msq = asem.n

                @block.vector
                def _(v, a_msq=a_msq):
                    v.wait_ge(asem.h, a_msq)
                    v.tensor_sub(pBa[0:1, 1536:2048], pBa[0:1, 1536:2048], ROW_TMP)
                    vsem.inc(v.tensor_scalar_add(pBa[0:1, 1536:2048], pBa[0:1, 1536:2048], EPS))
                v_var = vsem.n

                @block.scalar
                def _(s, v_var=v_var):
                    s.wait_ge(vsem.h, v_var)
                    asem.inc(s.activation(ROW_TMP, pBa[0:1, 1536:2048], AF.Ln))
                a_ln = asem.n

                @block.vector
                def _(v, a_ln=a_ln):
                    v.wait_ge(asem.h, a_ln)
                    vsem.inc(v.tensor_scalar_mul(ROW_TMP, ROW_TMP, -0.5))
                v_ln = vsem.n

                @block.scalar
                def _(s, v_ln=v_ln):
                    s.wait_ge(vsem.h, v_ln)
                    asem.inc(s.activation(ROW_R, ROW_TMP, AF.Exp))
                a_r = asem.n

                @block.vector
                def _(v, a_r=a_r, with_gate=with_gate):
                    v.wait_ge(asem.h, a_r)
                    v.tensor_mul(ROW_MR, pBa[0:1, 1024:1536], ROW_R)
                    if with_gate:
                        vsem.inc(v.tensor_scalar_mul(ROW_NM, pBa[0:1, 1024:1536], -1.0))
                    else:
                        vsem.inc(v.engine_nop())
                v_rows = vsem.n

                if with_gate:
                    gate(tt, v_rows)

                @block.tensor
                def _(t, v_rows=v_rows):
                    t.wait_ge(vsem.h, v_rows)
                    tsem.inc(t.matmul(pBa[:, 1024:1536], onesf[0:1, 0:128], ROW_R,
                                      start=True, stop=True))
                    tsem.inc(t.matmul(pBa[:, 1536:2048], onesf[0:1, 0:128], ROW_MR,
                                      start=True, stop=True))
                t_bc = tsem.n
                vu0 = vsem.n
                au0 = asem.n

                @block.vector
                def _(v, tt=tt, t_bc=t_bc, fold_x=fold_x, au0=au0):
                    v.wait_ge(tsem.h, t_bc)
                    for dc in range(NDC):
                        x_sl = xfm[:, dc * SL + tt * T:][:, 0:T]
                        p = dc % 2
                        if dc >= 2:
                            v.wait_ge(asem.h, au0 + dc - 1)
                        v.tensor_mul(uf32[p][:, :], x_sl, pBa[:, 1024:1536])
                        vsem.inc(v.tensor_sub(uf32[p][:, :], uf32[p][:, :], pBa[:, 1536:2048]))
                        if fold_x:
                            vsem.inc(v.scalar_tensor_tensor(x_sl, uf32[p][:, :], 0.5, x_sl,
                                                            ALU.mult, ALU.add))
                        else:
                            vsem.inc(v.engine_nop())

                @block.scalar
                def _(s, tt=tt, vu0=vu0):
                    for dc in range(NDC):
                        u_sl = ub[:, dc * SL + tt * T:][:, 0:T]
                        s.wait_ge(vsem.h, vu0 + 2 * dc + 1)
                        asem.inc(s.copy(u_sl, uf32[dc % 2][:, :]))
                prev_vu[0] = vsem.n

        def ffn_experts(call):
            t0 = tsem.n
            v0 = vsem.n
            a0 = asem.n
            a_enter = asem.n
            v_pre = vsem.n
            d0w = [wsp[0].n, wsp[1].n]

            @block.sync
            def _(sy, v_pre=v_pre):
                sy.wait_ge(vsem.h, v_pre)
                sy.dma_start(out=cdram[:, :], in_=cT[0:8, :]).then_inc(qsem.h, 16)
                qsem.n += 16
            q_cflat = qsem.n

            def t_base(e):
                return t0 + e * (4 + 32 * 16)

            def t_cb(e, tt):
                return t_base(e) + tt + 1

            def t_blk(e, hc, tt):
                return t_base(e) + 4 + (hc * NTT + tt) * 16

            def t_h1(e, hc, tt, i):
                return t_blk(e, hc, tt) + 2 * i + 1

            def t_h3(e, hc, tt, i):
                return t_blk(e, hc, tt) + 2 * i + 2

            def t_y(e, hc, tt, dc):
                return t_blk(e, hc, tt) + 8 + dc + 1

            def v_base(e):
                return v0 + e * (4 + 32 * 12)

            def v_z(e, tt):
                return v_base(e) + tt + 1

            def v_blkbase(e, hc, tt):
                return v_base(e) + 4 + (hc * NTT + tt) * 12

            def v_hmul(e, hc, tt, i):
                return v_blkbase(e, hc, tt) + i + 1

            def v_yadd(e, hc, tt, dc):
                return v_blkbase(e, hc, tt) + 4 + dc + 1

            def lin_h(e, hc, tt, i):
                return ((e * NHC + hc) * NTT + tt) * 4 + i

            def lin_y(e, hc, tt, dc):
                return ((e * NHC + hc) * NTT + tt) * 8 + dc

            def unlin_h(k):
                return (k // (4 * NTT * NHC), (k // (4 * NTT)) % NHC,
                        (k // 4) % NTT, k % 4)

            def unlin_y(k):
                return (k // (8 * NTT * NHC), (k // (8 * NTT)) % NHC,
                        (k // 8) % NTT, k % 8)

            def a_silu(e, hc, tt, i):
                return a0 + ((e * NHC + hc) * NTT + tt) * 4 + i + 1

            @block.gpsimd
            def _(g, call=call, t0=t0):
                for c in range(E * NHC):
                    e, hc = divmod(c, NHC)
                    p = c % 2
                    if c < 2:
                        g.wait_ge(tsem.h, t0)
                    else:
                        ee, hh = divmod(c - 2, NHC)
                        g.wait_ge(tsem.h, t_y(ee, hh, NTT - 1, 7))
                    g.dma_start(out=w1t[p].rearrange("p (dc h) -> p dc h", h=HC),
                                in_=w1d[e][:, hc * HC:(hc + 1) * HC]
                                .rearrange("(dc p) h -> p dc h", p=128)).then_inc(wsp[p].h, 16)
                    g.dma_start(out=w3t[p].rearrange("p (dc h) -> p dc h", h=HC),
                                in_=w3d[e][:, hc * HC:(hc + 1) * HC]
                                .rearrange("(dc p) h -> p dc h", p=128)).then_inc(wsp[p].h, 16)
                    g.dma_start(out=w2t[p].rearrange("p (hh d) -> p hh d", d=D),
                                in_=w2d[e][hc * HC:(hc + 1) * HC, :]
                                .rearrange("(hh p) d -> p hh d", p=128)).then_inc(wsp[p].h, 16)
                    wsp[p].n += 48

            @block.sync
            def _(sy, q_cflat=q_cflat):
                for e in range(E):
                    if e == 0:
                        sy.wait_ge(qsem.h, q_cflat)
                    else:
                        sy.wait_ge(tsem.h, t_cb(e - 1, NTT - 1))
                    sy.dma_start(out=crow[0:1, :], in_=cdram[e:e + 1, :]).then_inc(qsem.h, 16)
                    qsem.n += 16
            q_crow0 = q_cflat

            @block.tensor
            def _(t, call=call, q_crow0=q_crow0, d0w=d0w):
                for e in range(E):
                    for tt in range(NTT):
                        t.wait_ge(qsem.h, q_crow0 + 16 * (e + 1))
                        if tt > 0:
                            t.wait_ge(vsem.h, v_z(e, tt - 1))
                        if e > 0 and tt == 0:
                            t.wait_ge(vsem.h, v_z(e - 1, NTT - 1))
                        tsem.inc(t.matmul(pBa[:, 1024:1536], onesb[:, 0:128],
                                          crow[0:1, tt * T:(tt + 1) * T], start=True, stop=True))
                        assert tsem.n == t_cb(e, tt)
                    for hc in range(NHC):
                        c = e * NHC + hc
                        p = c % 2
                        t.wait_ge(wsp[p].h, d0w[p] + 48 * (c // 2 + 1))
                        for tt in range(NTT):
                            t.wait_ge(vsem.h, v_z(e, tt))
                            for i in range(4):
                                kk = lin_h(e, hc, tt, i)
                                if kk >= 2:
                                    t.wait_ge(vsem.h, v_hmul(*unlin_h(kk - 2)))
                                pp = kk % 2
                                for dc in range(NDC):
                                    ins = t.matmul(bank(pAa, pp),
                                                   w1t[p][:, dc * HC + i * 128:][:, 0:128],
                                                   qoz[:, dc * SL + tt * T:][:, 0:T],
                                                   start=(dc == 0), stop=(dc == NDC - 1))
                                tsem.inc(ins)
                                for dc in range(NDC):
                                    ins = t.matmul(bank(pAa, 2 + pp),
                                                   w3t[p][:, dc * HC + i * 128:][:, 0:128],
                                                   qoz[:, dc * SL + tt * T:][:, 0:T],
                                                   start=(dc == 0), stop=(dc == NDC - 1))
                                tsem.inc(ins)
                                assert tsem.n == t_h3(e, hc, tt, i)
                            t.wait_ge(vsem.h, v_hmul(e, hc, tt, 3))
                            for dc in range(NDC):
                                kk = lin_y(e, hc, tt, dc)
                                if kk >= 2:
                                    t.wait_ge(vsem.h, v_yadd(*unlin_y(kk - 2)))
                                pp = kk % 2
                                for i in range(4):
                                    ins = t.matmul(bank(pBa, pp),
                                                   w2t[p][:, i * D + dc * 128:][:, 0:128],
                                                   hbuf[:, i * T:(i + 1) * T],
                                                   start=(i == 0), stop=(i == 3))
                                tsem.inc(ins)
                                assert tsem.n == t_y(e, hc, tt, dc)

            @block.vector
            def _(v, call=call, a_enter=a_enter):
                for e in range(E):
                    for tt in range(NTT):
                        if e == 0 and tt == 0:
                            v.wait_ge(asem.h, a_enter)
                        v.wait_ge(tsem.h, t_cb(e, tt))
                        if e > 0:
                            v.wait_ge(tsem.h, t_h3(e - 1, NHC - 1, tt, 3))
                        ins = None
                        for dc in range(NDC):
                            ins = v.tensor_mul(qoz[:, dc * SL + tt * T:][:, 0:T],
                                               ub[:, dc * SL + tt * T:][:, 0:T],
                                               pBa[:, 1024:1536])
                        vsem.inc(ins)
                        assert vsem.n == v_z(e, tt)
                    for hc in range(NHC):
                        for tt in range(NTT):
                            for i in range(4):
                                kk = lin_h(e, hc, tt, i)
                                pp = kk % 2
                                v.wait_ge(asem.h, a_silu(e, hc, tt, i))
                                v.wait_ge(tsem.h, t_h3(e, hc, tt, i))
                                vsem.inc(v.tensor_mul(hbuf[:, i * T:(i + 1) * T],
                                                      s1[pp][:, :], bank(pAa, 2 + pp)))
                                assert vsem.n == v_hmul(e, hc, tt, i)
                            for dc in range(NDC):
                                kk = lin_y(e, hc, tt, dc)
                                pp = kk % 2
                                v.wait_ge(tsem.h, t_y(e, hc, tt, dc))
                                x_sl = xfm[:, dc * SL + tt * T:][:, 0:T]
                                vsem.inc(v.scalar_tensor_tensor(x_sl, bank(pBa, pp), 0.5,
                                                                x_sl, ALU.mult, ALU.add))
                                assert vsem.n == v_yadd(e, hc, tt, dc)

            @block.scalar
            def _(s, call=call):
                for e in range(E):
                    for hc in range(NHC):
                        for tt in range(NTT):
                            for i in range(4):
                                kk = lin_h(e, hc, tt, i)
                                pp = kk % 2
                                s.wait_ge(tsem.h, t_h1(e, hc, tt, i))
                                if kk >= 2:
                                    s.wait_ge(vsem.h, v_hmul(*unlin_h(kk - 2)))
                                asem.inc(s.activation(s1[pp][:, :], bank(pAa, pp), AF.Silu))
                                assert asem.n == a_silu(e, hc, tt, i)

        def attention(stop_after="all"):
            a_enter = asem.n
            for wi, wsrc_ in enumerate([wqT, wkT, wvT]):
                t_pre = tsem.n

                @block.gpsimd
                def _(g, wsrc_=wsrc_, t_pre=t_pre, wi=wi):
                    g.wait_ge(tsem.h, t_pre)
                    if wi == 2:
                        g.dma_start(out=bvrowb[:, :], in_=bvd[None, :]).then_inc(dsem.h, 16)
                        dsem.n += 16
                    g.dma_start(out=wta[:, :].rearrange("p (dc f) -> p dc f", f=D),
                                in_=wsrc_.rearrange("(dc p) f -> p dc f", p=128)
                                ).then_inc(dsem.h, 16)
                    dsem.n += 16
                d_w = dsem.n

                if wi == 0:
                    a_pre = asem.n

                    @block.tensor
                    def _(t, d_w=d_w, a_pre=a_pre, a_enter=a_enter):
                        t.wait_ge(dsem.h, d_w)
                        t.wait_ge(asem.h, a_enter)
                        for fc in range(NDC):
                            for tt in range(NTT):
                                k = fc * NTT + tt
                                if k >= 2:
                                    t.wait_ge(asem.h, a_pre + k - 1)
                                for dc in range(NDC):
                                    ins = t.matmul(bank(pAa, k % 2),
                                                   wta[:, dc * D + fc * 128:][:, 0:128],
                                                   ub[:, dc * SL + tt * T:][:, 0:T],
                                                   start=(dc == 0), stop=(dc == NDC - 1))
                                tsem.inc(ins)

                    @block.scalar
                    def _(s, t_pre=t_pre):
                        for fc in range(NDC):
                            for tt in range(NTT):
                                k = fc * NTT + tt
                                s.wait_ge(tsem.h, t_pre + k + 1)
                                asem.inc(s.activation(qoz[:, fc * SL + tt * T:][:, 0:T],
                                                      bank(pAa, k % 2), AF.Identity,
                                                      bias=battn[:, 0 * NDC + fc:][:, 0:1],
                                                      scale=SCALE))
                elif wi == 1:
                    a_pre = asem.n
                    q_pre = qsem.n

                    @block.tensor
                    def _(t, d_w=d_w, a_pre=a_pre):
                        t.wait_ge(dsem.h, d_w)
                        t.wait_ge(asem.h, a_pre)
                        for fc in range(NDC):
                            for tt in range(NTT):
                                k = fc * NTT + tt
                                if k >= 2:
                                    t.wait_ge(asem.h, a_pre + k - 1)
                                for dc in range(NDC):
                                    ins = t.matmul(bank(pAa, k % 2),
                                                   wta[:, dc * D + fc * 128:][:, 0:128],
                                                   ub[:, dc * SL + tt * T:][:, 0:T],
                                                   start=(dc == 0), stop=(dc == NDC - 1))
                                tsem.inc(ins)

                    @block.scalar
                    def _(s, t_pre=t_pre, q_pre=q_pre, a_pre=a_pre):
                        for fc in range(NDC):
                            for tt in range(NTT):
                                k = fc * NTT + tt
                                s.wait_ge(tsem.h, t_pre + k + 1)
                                if fc >= 1 and tt == 0:
                                    s.wait_ge(qsem.h, q_pre + 16 * fc)
                                asem.inc(s.activation(kvx[:, tt * T:(tt + 1) * T],
                                                      bank(pAa, k % 2), AF.Identity,
                                                      bias=battn[:, 1 * NDC + fc:][:, 0:1]))

                    @block.sync
                    def _(sy, a_pre=a_pre):
                        for fc in range(NDC):
                            sy.wait_ge(asem.h, a_pre + NTT * (fc + 1))
                            sy.dma_start(out=kexp[fc * 128:(fc + 1) * 128, :],
                                         in_=kvx[:, :]).then_inc(qsem.h, 16)
                            qsem.n += 16
                    q_kexp = qsem.n

                    @block.gpsimd
                    def _(g, q_kexp=q_kexp):
                        g.wait_ge(qsem.h, q_kexp)
                        g.collective_compute(
                            "AllGather", ALU.bypass,
                            replica_groups=[[0, 1], [2, 3], [4, 5], [6, 7]],
                            ins=[kexp.opt()], outs=[kg.opt()]).then_inc(ck.h, 1)
                        ck.n = 1
                elif wi == 2:
                    v_pre = vsem.n
                    q_pre = qsem.n

                    @block.tensor
                    def _(t, d_w=d_w, v_pre=v_pre):
                        t.wait_ge(dsem.h, d_w)
                        t.wait_ge(vsem.h, v_pre)
                        for tc in range(NQC):
                            for fb in range(2):
                                k = tc * 2 + fb
                                if k >= 4:
                                    t.wait_ge(vsem.h, v_pre + k - 3)
                                for dc in range(NDC):
                                    t.matmul(bank(pBa, k % 4),
                                             ub[:, dc * SL + tc * 128:][:, 0:128],
                                             wta[:, dc * D + fb * T:][:, 0:T],
                                             start=(dc == 0), stop=False)
                                ins = t.matmul(bank(pBa, k % 4), onesb[:, 0:128],
                                               bvrowb[0:1, fb * T:(fb + 1) * T],
                                               start=False, stop=True)
                                tsem.inc(ins)

                    @block.vector
                    def _(v, t_pre=t_pre, q_pre=q_pre):
                        for tc in range(NQC):
                            for fb in range(2):
                                k = tc * 2 + fb
                                v.wait_ge(tsem.h, t_pre + k + 1)
                                if tc >= 1 and fb == 0:
                                    v.wait_ge(qsem.h, q_pre + 16 * tc)
                                vsem.inc(v.tensor_copy(kvx[:, fb * T:(fb + 1) * T],
                                                       bank(pBa, k % 4)))

                    @block.sync
                    def _(sy, v_pre=v_pre):
                        for tc in range(NQC):
                            sy.wait_ge(vsem.h, v_pre + 2 * (tc + 1))
                            sy.dma_start(out=vexp[tc * 128:(tc + 1) * 128, :],
                                         in_=kvx[:, 0:1024]).then_inc(qsem.h, 16)
                            qsem.n += 16
                    q_vexp = qsem.n

                    @block.gpsimd
                    def _(g, q_vexp=q_vexp):
                        g.wait_ge(qsem.h, q_vexp)
                        g.collective_compute(
                            "AllGather", ALU.bypass,
                            replica_groups=[[0, 1], [2, 3], [4, 5], [6, 7]],
                            ins=[vexp.opt()], outs=[vg.opt()]).then_inc(cv.h, 1)
                        cv.n = 1

            t_postv = tsem.n

            @block.gpsimd
            def _(g, t_postv=t_postv):
                g.wait_ge(tsem.h, t_postv)
                for p in range(2):
                    vv = vh[p].rearrange("p (k c) -> p k c", c=65)
                    g.memset(vv[:, :, 64:65], 1.0)
                    g.memset(khb[p][64:128, :], 0.0)
                    g.memset(khb[p][64:65, :], 1.0)
                    g.memset(qtile[p][64:128, :], 0.0)
                g.memset(srow[:, :], 1.0).then_inc(esem.h, 1)
                esem.n += 1

            if stop_after == "qkv":
                return
            NF = 8 * NQC
            tfill = {}
            ttp = {}
            tBsT = {}
            tBAV = {}
            tBrs = {}
            tBend = {}
            tAend = {}
            vbaseA = {}
            sbaseB = {}
            q_mst = {}
            q_kb = {}
            prev_osc = [0]
            prev_vred_tail = [None, None]

            for s in range(NH + 1):
                hA = s if s < NH else None
                hB = s - 1 if s >= 1 else None
                pA_ = None if hA is None else hA % 2
                pB_ = None if hB is None else hB % 2

                def emit_loads(hh):
                    pp = hh % 2
                    kbe = kbp[pp]
                    lo = 64 * (hh % 2)
                    fcq = hh // 2

                    @block.sync
                    def _(sy, hh=hh, lo=lo, fcq=fcq, kbe=kbe, pp=pp):
                        if hh == 0:
                            sy.wait_ge(ck.h, 1)
                            sy.wait_ge(cv.h, 1)
                            sy.wait_ge(esem.h, esem.n)
                        if hh >= 2:
                            sy.wait_ge(tsem.h, tBend[hh - 2])
                        sy.dma_start(out=khb[pp][0:64, 0:SL],
                                     in_=kg[64 * hh:64 * hh + 64, :]).then_inc(kbe.h, 16)
                        sy.dma_start(out=khb[pp][0:64, SL:2 * SL],
                                     in_=kg[D + 64 * hh:D + 64 * hh + 64, :]).then_inc(kbe.h, 16)
                        vv = vh[pp].rearrange("p (k c) -> p k c", c=65)
                        sy.dma_start(out=vv[:, :, 0:64],
                                     in_=vg.rearrange("(k p) f -> p k f", p=128)
                                     [:, :, 64 * hh:64 * hh + 64]).then_inc(kbe.h, 16)
                        sy.dma_start(out=qtile[pp][0:64, :],
                                     in_=qoz[lo:lo + 64, fcq * SL:(fcq + 1) * SL]
                                     ).then_inc(kbe.h, 16)
                        kbe.n += 64
                    q_kb[hh] = kbp[pp].n

                if s == 0:
                    emit_loads(0)
                if hB is not None:
                    q_kB = q_kb[hB]

                vbase = vsem.n
                if hA is not None:
                    vbaseA[hA] = vbase

                def v_red(qc, f, vbase=vbase, hB=hB):
                    ex = 2 * (qc // 4) if hB is not None else 0
                    return vbase + 10 * qc + ex + f + 1

                def v_rfin(qc, vbase=vbase, hB=hB):
                    ex = 2 * (qc // 4) if hB is not None else 0
                    return vbase + 10 * qc + ex + 9

                def v_copy(qc, vbase=vbase, hB=hB):
                    ex = 2 * (qc // 4) if hB is not None else 0
                    return vbase + 10 * qc + ex + 10

                def v_recip(qb, vbase=vbase, hA=hA):
                    return vbase + (42 * qb + 41 if hA is not None else 2 * qb + 1)

                def v_osc(qb, vbase=vbase, hA=hA):
                    return vbase + (42 * qb + 42 if hA is not None else 2 * qb + 2)

                sbase = asem.n
                if hB is not None:
                    sbaseB[hB] = sbase

                def s_exp(qb, g, sbase=sbase):
                    return sbase + 33 * qb + g + 1

                def s_rsb(qb, sbase=sbase):
                    return sbase + 33 * qb + 33

                qpred = [None]

                @block.tensor
                def _(t, s=s, hA=hA, hB=hB, pA_=pA_, pB_=pB_,
                      v_red=v_red, v_rfin=v_rfin, v_copy=v_copy,
                      v_recip=v_recip, v_osc=v_osc, s_exp=s_exp, s_rsb=s_rsb,
                      vbase=vbase, sbase=sbase):
                    items = []
                    if hA is not None:
                        loA = 64 * (hA % 2)
                        fcqA = hA // 2

                        def mk_fill(qc, f, hA=hA, pA_=pA_):
                            def emit():
                                FF = 8 * qc + f
                                if FF == 0:
                                    t.wait_ge(kbp[pA_].h, q_kb[hA])
                                if FF < 2:
                                    if prev_vred_tail[f] is not None:
                                        t.wait_ge(vsem.h, prev_vred_tail[f])
                                elif f == 0 and FF % 8 == 0:
                                    t.wait_ge(vsem.h, v_copy(qc - 1))
                                else:
                                    t.wait_ge(vsem.h, v_red((FF - 2) // 8, (FF - 2) % 8))
                                q_sl = qtile[pA_][0:64, qc * 128:(qc + 1) * 128]
                                ins = t.matmul(pBa[:, (2 + f % 2) * 512:(3 + f % 2) * 512],
                                               q_sl, khb[pA_][0:64, f * 512:(f + 1) * 512],
                                               start=True, stop=True)
                                tsem.inc(ins)
                                tfill[(hA, FF)] = tsem.n
                            return emit

                        def mk_tp(qc, hA=hA):
                            def emit():
                                t.wait_ge(vsem.h, v_rfin(qc))
                                tsem.inc(t.transpose(pBa[0:1, 1024:1152],
                                                     gsc[:, 0:1], idf[:, :]))
                                ttp[(hA, qc)] = tsem.n
                            return emit

                        for qc in range(NQC):
                            for f in range(8):
                                items.append(mk_fill(qc, f))
                            items.append(mk_tp(qc))
                    it = iter(items)

                    def inject(n):
                        for _i in range(n):
                            em = next(it, None)
                            if em is None:
                                return
                            em()

                    if hB is not None:
                        loB = 64 * (hB % 2)
                        kbe = kbp[pB_]
                        t.wait_ge(kbe.h, q_kB)
                        t.wait_ge(qsem.h, q_mst[hB])

                        def attsl(g):
                            sl = g % 8
                            return att[sl // 4][:, (sl % 4) * 512:(sl % 4 + 1) * 512]

                        def sTq(qb, g, hB=hB, pB_=pB_):
                            ins = t.matmul(pAa[:, (g % 4) * 512:(g % 4 + 1) * 512],
                                           khb[pB_][0:128, g * 128:(g + 1) * 128],
                                           qtile[pB_][0:128, qb * T:(qb + 1) * T],
                                           start=True, stop=True)
                            tsem.inc(ins)
                            tBsT[(hB, qb, g)] = tsem.n
                        for qb in range(NTT):
                            for g in range(4):
                                if qb == 0:
                                    t.wait_ge(asem.h, sbase)
                                else:
                                    t.wait_ge(asem.h, s_exp(qb - 1, 28 + g))
                                sTq(qb, g)
                            for g in range(NKC):
                                inject((2 if g < 4 else 1) if hA is not None else 0)
                                t.wait_ge(asem.h, s_exp(qb, g))
                                if g == 0:
                                    t.wait_ge(vsem.h,
                                              v_osc(qb - 1) if qb > 0 else prev_osc[0])
                                ins = t.matmul(pBa[0:65, 0:512],
                                               vh[pB_][:, g * 65:(g + 1) * 65],
                                               attsl(g),
                                               start=(g == 0), stop=(g == NKC - 1))
                                tsem.inc(ins)
                                tBAV[(hB, qb, g)] = tsem.n
                                if g + 4 < NKC:
                                    sTq(qb, g + 4)
                            t.wait_ge(vsem.h, v_recip(qb))
                            tsem.inc(t.matmul(pBa[:, 512:1024], onesf[64:65, 0:128],
                                              srow[64:65, :], start=True, stop=True))
                            tBrs[(hB, qb)] = tsem.n
                    inject(10 ** 9)
                    if hA is not None:
                        tAend[hA] = tsem.n
                    if hB is not None:
                        tBend[hB] = tsem.n

                if hB is not None:
                    @block.scalar
                    def _(s_, hB=hB):
                        def attsl(g):
                            sl = g % 8
                            return att[sl // 4][:, (sl % 4) * 512:(sl % 4 + 1) * 512]
                        for qb in range(NTT):
                            for g in range(NKC):
                                s_.wait_ge(tsem.h, tBsT[(hB, qb, g)])
                                if g >= 8:
                                    s_.wait_ge(tsem.h, tBAV[(hB, qb, g - 8)])
                                elif qb > 0:
                                    s_.wait_ge(tsem.h, tBAV[(hB, qb - 1, 24 + g)])
                                asem.inc(s_.activation(attsl(g),
                                                       pAa[:, (g % 4) * 512:(g % 4 + 1) * 512],
                                                       AF.Exp))
                            s_.wait_ge(tsem.h, tBrs[(hB, qb)])
                            asem.inc(s_.copy(rsb[0:64, :], pBa[0:64, 512:1024]))

                qpred[0] = qsem.n

                @block.vector
                def _(v, s=s, hA=hA, hB=hB, pB_=pB_, vbase=vbase, sbase=sbase,
                      v_red=v_red, v_rfin=v_rfin, v_copy=v_copy,
                      v_recip=v_recip, v_osc=v_osc, s_rsb=s_rsb, qpred=qpred):
                    def a_block(qc, hA=hA):
                        if qc == 0 and hA is not None and hA >= 1:
                            v.wait_ge(qsem.h, q_mst[hA - 1])
                        for f in range(8):
                            v.wait_ge(tsem.h, tfill[(hA, 8 * qc + f)])
                            vsem.inc(v.reduce_max(gcol[:, f:f + 1],
                                                  pBa[:, (2 + f % 2) * 512:(3 + f % 2) * 512],
                                                  axis=AX.X))
                        v.wait_ge(vsem.h, vsem.n)
                        vsem.inc(v.reduce_max(gsc[:, 0:1], gcol[:, 0:8], axis=AX.X,
                                              negate=True))
                        v.wait_ge(tsem.h, ttp[(hA, qc)])
                        vsem.inc(v.tensor_copy(mrow[0:1, qc * 128:(qc + 1) * 128],
                                               pBa[0:1, 1024:1152]))

                    def b_block(qb, hB=hB, pB_=pB_):
                        loB = 64 * (hB % 2)
                        fcqB = hB // 2
                        v.wait_ge(tsem.h, tBAV[(hB, qb, 31)])
                        vsem.inc(v.reciprocal(srow[64:65, :], pBa[64:65, 0:512]))
                        v.wait_ge(asem.h, s_rsb(qb))
                        if loB == 0:
                            vsem.inc(v.tensor_mul(
                                qoz[0:64, fcqB * SL + qb * T:][:, 0:T],
                                pBa[0:64, 0:512], rsb[0:64, :]))
                        else:
                            if qb >= 2:
                                v.wait_ge(qsem.h, qpred[0] + 16 * (qb - 1))
                            vsem.inc(v.tensor_mul(obuf2[qb % 2][:, :],
                                                  pBa[0:64, 0:512], rsb[0:64, :]))

                    if hA is not None and hB is not None:
                        for qb in range(NTT):
                            for qc in range(4 * qb, 4 * qb + 4):
                                a_block(qc)
                            b_block(qb)
                    elif hA is not None:
                        for qc in range(NQC):
                            a_block(qc)
                    else:
                        for qb in range(NTT):
                            b_block(qb)
                    if hB is not None:
                        assert vsem.n == v_osc(NTT - 1), (vsem.n, v_osc(NTT - 1))
                    else:
                        assert vsem.n == v_copy(NQC - 1), (vsem.n, v_copy(NQC - 1))

                if hA is not None:
                    prev_vred_tail[0] = v_copy(NQC - 1)
                    prev_vred_tail[1] = v_red(NQC - 1, 7)
                if hB is not None:
                    prev_osc[0] = v_osc(NTT - 1)

                @block.sync
                def _(sy, hA=hA, hB=hB, v_copy=v_copy, v_osc=v_osc):
                    if hB is not None and hB % 2 == 1:
                        fcqB = hB // 2
                        for qb in range(NTT):
                            sy.wait_ge(vsem.h, v_osc(qb))
                            sy.dma_start(out=qoz[64:128, fcqB * SL + qb * T:][:, 0:T],
                                         in_=obuf2[qb % 2][:, :]).then_inc(qsem.h, 16)
                            qsem.n += 16
                    if hA is not None:
                        sy.wait_ge(vsem.h, v_copy(NQC - 1))
                        sy.dma_start(out=qtile[pA_][64:65, :],
                                     in_=mrow[0:1, :]).then_inc(qsem.h, 16)
                        qsem.n += 16
                if hA is not None:
                    q_mst[hA] = qsem.n
                if s + 1 < NH:
                    emit_loads(s + 1)

            t_heads = tsem.n

            @block.gpsimd
            def _(g, t_heads=t_heads):
                g.wait_ge(tsem.h, t_heads)
                g.dma_start(out=wta[:, :].rearrange("p (dc f) -> p dc f", f=D),
                            in_=woT.rearrange("(dc p) f -> p dc f", p=128)
                            ).then_inc(dsem.h, 16)
                dsem.n += 16
            d_w_wo = dsem.n
            v_wo0, q_wo0 = vsem.n, qsem.n

            @block.tensor
            def _(t, v_wo0=v_wo0, q_wo0=q_wo0, d_w_wo=d_w_wo):
                t.wait_ge(vsem.h, v_wo0)
                t.wait_ge(qsem.h, q_wo0)
                t.wait_ge(dsem.h, d_w_wo)
                for dc in range(NDC):
                    for tt in range(NTT):
                        k = dc * NTT + tt
                        if k >= 2:
                            t.wait_ge(vsem.h, v_wo0 + k - 1)
                        for fc in range(NDC):
                            ins = t.matmul(bank(pAa, k % 2),
                                           wta[:, fc * D + dc * 128:][:, 0:128],
                                           qoz[:, fc * SL + tt * T:][:, 0:T],
                                           start=(fc == 0), stop=(fc == NDC - 1))
                        tsem.inc(ins)
            t_wo0 = tsem.n - NDC * NTT

            @block.vector
            def _(v, t_wo0=t_wo0):
                for dc in range(NDC):
                    for tt in range(NTT):
                        k = dc * NTT + tt
                        v.wait_ge(tsem.h, t_wo0 + k + 1)
                        x_sl = xfm[:, dc * SL + tt * T:][:, 0:T]
                        vsem.inc(v.scalar_tensor_tensor(
                            x_sl, bank(pAa, k % 2), battn[:, 3 * NDC + dc:][:, 0:1],
                            x_sl, ALU.add, ALU.add))

        if stop_after != "p0":
            layernorm(fold_x=True, with_gate=True)
        if stop_after in ("u1", "c1"):
            t_d, v_d, a_d = tsem.n, vsem.n, asem.n

            @block.gpsimd
            def _(g):
                g.wait_ge(tsem.h, t_d)
                g.wait_ge(vsem.h, v_d)
                g.wait_ge(asem.h, a_d)
                if stop_after == "u1":
                    for dc in range(NDC):
                        g.dma_start(out=outT[dc * 128:(dc + 1) * 128, :],
                                    in_=ub[:, dc * SL:(dc + 1) * SL]).then_inc(dsem.h, 16)
                        dsem.n += 16
                else:
                    g.dma_start(out=outT[0:8, :], in_=cT[0:8, :]).then_inc(dsem.h, 16)
                    dsem.n += 16
                g.wait_ge(dsem.h, dsem.n)
        if stop_after not in ("p0", "ln1", "u1", "c1"):
            ffn_experts(0)
        if stop_after not in ("p0", "ln1", "u1", "c1", "ffn1"):
            layernorm(fold_x=False, with_gate=False)
            attention("qkv" if stop_after == "qkv" else "all")
        if stop_after not in ("p0", "ln1", "u1", "c1", "ffn1", "qkv", "attn"):
            layernorm(fold_x=True, with_gate=True)
            ffn_experts(1)

        t_fin, v_fin, a_fin = tsem.n, vsem.n, asem.n
        skip_out = stop_after in ("u1", "c1")

        @block.sync
        def _(sy):
            if skip_out:
                sy.wait_ge(dsem.h, dsem.n)
                return
            sy.wait_ge(vsem.h, v_fin)
            sy.wait_ge(tsem.h, t_fin)
            sy.wait_ge(asem.h, a_fin)
            for dc in range(NDC):
                sy.dma_start(out=outT[dc * 128:(dc + 1) * 128, :],
                             in_=xfm[:, dc * SL:(dc + 1) * SL]).then_inc(qsem.h, 16)
            qsem.n += NDC * 16
            sy.wait_ge(qsem.h, qsem.n)

    return nc


def kernel(L0_attn_x, L0_attn_Wq, L0_attn_bq, L0_attn_Wk, L0_attn_bk,
           L0_attn_Wv, L0_attn_bv, L0_attn_Wo, L0_attn_bo,
           L0_ffn_Wg, L0_ffn_bg, L0_ffn_W1, L0_ffn_W2, L0_ffn_W3):
    x = np.ascontiguousarray(np.asarray(L0_attn_x, dtype=np.float32)).reshape(B * S, D)
    com = dict(
        wqT=np.ascontiguousarray(np.asarray(L0_attn_Wq, np.float32).T),
        wkT=np.ascontiguousarray(np.asarray(L0_attn_Wk, np.float32).T),
        wvT=np.ascontiguousarray(np.asarray(L0_attn_Wv, np.float32).T),
        woT=np.ascontiguousarray(np.asarray(L0_attn_Wo, np.float32).T),
        bq=np.asarray(L0_attn_bq, np.float32), bk=np.asarray(L0_attn_bk, np.float32),
        bv=np.asarray(L0_attn_bv, np.float32), bo=np.asarray(L0_attn_bo, np.float32),
        wg=np.asarray(L0_ffn_Wg, np.float32), bg=np.asarray(L0_ffn_bg, np.float32),
        w1=np.ascontiguousarray(np.asarray(L0_ffn_W1, np.float32)),
        w2=np.ascontiguousarray(np.asarray(L0_ffn_W2, np.float32)),
        w3=np.ascontiguousarray(np.asarray(L0_ffn_W3, np.float32)),
    )
    in_maps = []
    for c in range(NCORE):
        xs = np.ascontiguousarray(x[c * SL:(c + 1) * SL, :].T)
        in_maps.append(dict(xT=xs, **com))
    nc = build_program()
    res = run_bass_kernel_spmd(nc, in_maps, core_ids=list(range(NCORE)))
    outs = [np.asarray(r["outT"]).T for r in res.results]
    return np.concatenate(outs, axis=0).reshape(B, S, D)


# version 09
# speedup vs baseline: 1.1203x; 1.0607x over previous
import numpy as np
import concourse.bass as bass
import concourse.mybir as mybir
from concourse.bass_utils import run_bass_kernel_spmd

F32 = mybir.dt.float32
BF16 = mybir.dt.bfloat16
AF = mybir.ActivationFunctionType
ALU = mybir.AluOpType
AX = mybir.AxisListType

NCORE = 8
B, S, D = 4, 4096, 1024
SL = 2048
NH, HD = 16, 64
E, H = 8, 4096
NDC = D // 128
T = 512
NTT = SL // T
HC = 512
NHC = H // HC
SF = 4096
NKC = SF // 128
NQC = SL // 128
SCALE = 0.125
EPS = 1e-5


class Sem:
    def __init__(self, nc, ctx, name):
        self.h = ctx.enter_context(nc.semaphore(name))
        self.n = 0

    def inc(self, inst, k=1):
        inst.then_inc(self.h, k)
        self.n += k
        return self.n


def build_program(stop_after="all"):
    nc = bass.Bass("TRN2", target_bir_lowering=False, detect_race_conditions=False)

    xT = nc.dram_tensor("xT", [D, SL], F32, kind="ExternalInput").ap()
    wqT = nc.dram_tensor("wqT", [D, D], F32, kind="ExternalInput").ap()
    wkT = nc.dram_tensor("wkT", [D, D], F32, kind="ExternalInput").ap()
    wvT = nc.dram_tensor("wvT", [D, D], F32, kind="ExternalInput").ap()
    woT = nc.dram_tensor("woT", [D, D], F32, kind="ExternalInput").ap()
    bqd = nc.dram_tensor("bq", [D], F32, kind="ExternalInput").ap()
    bkd = nc.dram_tensor("bk", [D], F32, kind="ExternalInput").ap()
    bvd = nc.dram_tensor("bv", [D], F32, kind="ExternalInput").ap()
    bod = nc.dram_tensor("bo", [D], F32, kind="ExternalInput").ap()
    wgd = nc.dram_tensor("wg", [D, E], F32, kind="ExternalInput").ap()
    bgd = nc.dram_tensor("bg", [E], F32, kind="ExternalInput").ap()
    w1d = nc.dram_tensor("w1", [E, D, H], F32, kind="ExternalInput").ap()
    w2d = nc.dram_tensor("w2", [E, H, D], F32, kind="ExternalInput").ap()
    w3d = nc.dram_tensor("w3", [E, D, H], F32, kind="ExternalInput").ap()
    outT = nc.dram_tensor("outT", [D, SL], F32, kind="ExternalOutput").ap()

    kexp = nc.dram_tensor("kexp", [D, SL], BF16).ap()
    vexp = nc.dram_tensor("vexp", [SL, D], BF16).ap()
    kg = nc.dram_tensor("kg", [2 * D, SL], BF16).ap()
    vg = nc.dram_tensor("vg", [SF, D], BF16).ap()
    cdram = nc.dram_tensor("cdram", [E, SL], BF16).ap()

    from contextlib import ExitStack
    with ExitStack() as ctx:
        BASE = 16512
        off = [BASE]

        def At(name, shape, dt_, offset=None):
            nbytes = int(np.prod(shape[1:])) * mybir.dt.size(dt_)
            if offset is None:
                offset = off[0]
                off[0] += (nbytes + 31) // 32 * 32
            return nc.alloc_sbuf_tensor_at(name, list(shape), dt_, offset=offset).ap()

        xfm = At("xfm", [128, NDC * SL], F32)
        ub = At("ub", [128, NDC * SL], BF16)
        qoz = At("qoz", [128, NDC * SL], BF16)
        wreg = off[0]
        w1t = [At(f"w1t{p}", [128, NDC * HC], BF16, wreg + p * 24576) for p in range(2)]
        w3t = [At(f"w3t{p}", [128, NDC * HC], BF16, wreg + p * 24576 + 8192) for p in range(2)]
        w2t = [At(f"w2t{p}", [128, 4 * D], BF16, wreg + p * 24576 + 16384) for p in range(2)]
        wta = At("wta", [128, NDC * D], BF16, wreg)
        qtile = [At(f"qtile{p}", [128, SL], BF16, wreg + p * 4096) for p in range(2)]
        khb = [At(f"khb{p}", [128, SF], BF16, wreg + 8192 + p * 8192) for p in range(2)]
        vh = [At(f"vh{p}", [128, NKC * 65], BF16, wreg + 24576 + p * 4160) for p in range(2)]
        kvx = At("kvx", [128, SL], BF16, wreg + 16384)
        bvrowb = At("bvrowb", [1, D], BF16, wreg + 20480)
        off[0] = wreg + 49152
        screg = off[0]
        sq = [At(f"sq{p}", [128, T], F32, screg + p * 2048) for p in range(2)]
        uf32 = [At(f"uf32{p}", [128, T], F32, screg + 4096 + p * 2048) for p in range(2)]
        hbuf = At("hbuf", [128, 4 * T], BF16, screg + 8192)
        s1 = [At(f"s1{p}", [128, T], F32, screg + 12288 + p * 2048) for p in range(2)]
        mrow = At("mrow", [1, SL], BF16, screg)
        rsb = At("rsb", [128, 512], F32, screg + 5120)
        srow = At("srow", [128, 512], F32, screg + 7168)
        obuf2 = [At(f"obuf{p}", [64, 512], BF16, screg + 9216 + p * 1024) for p in range(2)]
        att = [At("att0", [128, 4 * 512], BF16, wreg + 32896),
               At("att1", [128, 4 * 512], BF16, wreg + 36992)]
        off[0] = screg + 16384
        rowbuf = At("rowbuf", [1, 4 * T], F32)
        ctoff = off[0]
        cT = At("cT", [8, SL], BF16)
        crow = At("crow", [1, SL], BF16, offset=ctoff)
        battn = At("battn", [128, 4 * NDC], F32)
        onesf = At("onesf", [128, 128], F32)
        onesb = At("onesb", [1, 128], BF16)
        idf = At("idf", [128, 128], F32)
        idb = At("idb", [128, 128], BF16)
        wg_sb = At("wg_sb", [128, NDC * E], F32)
        g1b = At("g1b", [128, E], F32)
        bgb = At("bgb", [128, E], F32)
        grow = At("grow", [1, 2 * E], F32)
        gsc = At("gsc", [128, 4 * E], F32)
        gcol = At("gcol", [128, 8], F32)
        assert off[0] <= 229344, off[0]

        ROW_TMP, ROW_R, ROW_MR, ROW_NM = (rowbuf[0:1, k * T:(k + 1) * T] for k in range(4))

        pA = ctx.enter_context(nc.psum_tensor("pA", [128, 2048], F32))
        pB = ctx.enter_context(nc.psum_tensor("pB", [128, 2048], F32))
        pAa, pBa = pA[:, :], pB[:, :]
        pAb = pA[:, :].bitcast(BF16)

        def bank(ap_, i):
            return ap_[:, i * 512:(i + 1) * 512]

        names = ["qsem", "dsem", "tsem", "vsem", "asem", "ck", "cv", "esem",
                 "ws0", "ws1", "ks0", "ks1", "kb0", "kb1"]
        SM = {nm: Sem(nc, ctx, nm) for nm in names}
        qsem, dsem, tsem, vsem, asem, ck, cv, esem, ws0, ws1, ks0, ks1, kb0, kb1 = \
            (SM[n] for n in names)
        wsp = [ws0, ws1]
        ksp = [ks0, ks1]
        kbp = [kb0, kb1]
        block = ctx.enter_context(nc.Block())

        @block.gpsimd
        def _(g):
            g.memset(onesf[:, :], 1.0)
            g.memset(onesb[:, :], 1.0)
            g.memset(idf[:, :], 1.0)
            esem.inc(g.memset(idb[:, :], 1.0))
            g.wait_ge(esem.h, 1)
            g.affine_select(idf[:, :], idf[:, :], [[1, 128]], ALU.is_equal, 0.0,
                            base=0, channel_multiplier=-1)
            esem.inc(g.affine_select(idb[:, :], idb[:, :], [[1, 128]], ALU.is_equal, 0.0,
                                     base=0, channel_multiplier=-1))
            with nc.allow_non_contiguous_dma("tiny one-time bias loads"):
                for j, bb in enumerate([bqd, bkd, bvd, bod]):
                    g.dma_start(out=battn[:, j * NDC:(j + 1) * NDC],
                                in_=bb.rearrange("(dc p) -> p dc", p=128),
                                allow_slow_non_contiguous=True).then_inc(dsem.h, 16)
            g.dma_start(out=wg_sb[:, :].rearrange("p (dc e) -> p dc e", e=E),
                        in_=wgd.rearrange("(dc p) e -> p dc e", p=128)).then_inc(dsem.h, 16)
            g.dma_start(out=grow[0:1, E:2 * E], in_=bgd[None, :]).then_inc(dsem.h, 16)
            dsem.n += 6 * 16

        @block.sync
        def _(sy):
            for dc in range(NDC):
                sy.dma_start(out=xfm[:, dc * SL:(dc + 1) * SL],
                             in_=xT[dc * 128:(dc + 1) * 128, :]).then_inc(qsem.h, 16)
            qsem.n += NDC * 16
        q_x = qsem.n

        @block.tensor
        def _(t):
            t.wait_ge(dsem.h, dsem.n)
            t.wait_ge(esem.h, 2)
            t.wait_ge(qsem.h, q_x)
            for dc in range(NDC):
                ins = t.matmul(pBa[0:1, 0:E], onesf[:, 0:1], wg_sb[:, dc * E:(dc + 1) * E],
                               start=(dc == 0), stop=(dc == NDC - 1))
            tsem.inc(ins)
        ev0 = tsem.n

        @block.vector
        def _(v):
            v.wait_ge(tsem.h, ev0)
            vsem.inc(v.tensor_copy(grow[0:1, 0:E], pBa[0:1, 0:E]))
        ev1 = vsem.n

        @block.tensor
        def _(t):
            t.wait_ge(vsem.h, ev1)
            tsem.inc(t.matmul(pAa[:, 0:2 * E], onesf[0:1, 0:128], grow[0:1, :],
                              start=True, stop=True))
        ev2 = tsem.n

        @block.vector
        def _(v):
            v.wait_ge(tsem.h, ev2)
            v.tensor_copy(g1b[:, :], pAa[:, 0:E])
            v.tensor_scalar_mul(battn[:, 0:NDC], battn[:, 0:NDC], SCALE)
            vsem.inc(v.tensor_copy(bgb[:, :], pAa[:, E:2 * E]))

        def gate(tt, v_rows):
            for tc in range(4):
                vg0 = vsem.n
                tok0 = tt * T + tc * 128

                @block.tensor
                def _(t, tt=tt, tc=tc, tok0=tok0, vg0=vg0, v_rows=v_rows):
                    if tc == 0:
                        t.wait_ge(vsem.h, v_rows)
                    else:
                        t.wait_ge(vsem.h, vg0)
                    for dc in range(NDC):
                        ins = t.matmul(pBa[0:128, 0:E], xfm[:, dc * SL + tok0:][:, 0:128],
                                       wg_sb[:, dc * E:(dc + 1) * E],
                                       start=(dc == 0), stop=(dc == NDC - 1))
                    tsem.inc(ins)
                    tsem.inc(t.transpose(pAa[0:128, 1536:1537],
                                         ROW_NM[0:1, tc * 128:(tc + 1) * 128], onesf[0:1, 0:1]))
                    tsem.inc(t.transpose(pAa[0:128, 1537:1538],
                                         ROW_R[0:1, tc * 128:(tc + 1) * 128], onesf[0:1, 0:1]))
                t_g = tsem.n

                @block.vector
                def _(v, t_g=t_g):
                    def vf(ins):
                        vsem.inc(ins)
                        v.wait_ge(vsem.h, vsem.n)
                    v.wait_ge(tsem.h, t_g)
                    vf(v.tensor_copy(gcol[:, 6:8], pAa[:, 1536:1538]))
                    vf(v.scalar_tensor_tensor(gsc[:, 0:E], g1b[:, :], gcol[:, 6:7],
                                              pBa[0:128, 0:E], ALU.mult, ALU.add))
                    vf(v.tensor_scalar(gsc[:, 0:E], gsc[:, 0:E], gcol[:, 7:8], None, ALU.mult))
                    vf(v.tensor_add(gsc[:, 0:E], gsc[:, 0:E], bgb[:, :]))
                    vf(v.reduce_max(gcol[:, 0:1], gsc[:, 0:E], axis=AX.X))
                    vf(v.tensor_scalar_mul(gcol[:, 1:2], gcol[:, 0:1], -1.0))
                    vf(v.tensor_scalar(gsc[:, E:2 * E], gsc[:, 0:E], gcol[:, 0:1], None, ALU.is_ge))
                    vf(v.scalar_tensor_tensor(gsc[:, 2 * E:3 * E], gsc[:, E:2 * E], -1e30,
                                              gsc[:, 0:E], ALU.mult, ALU.add))
                    vsem.inc(v.reduce_max(gcol[:, 2:3], gsc[:, 2 * E:3 * E], axis=AX.X))
                v_g1 = vsem.n

                @block.scalar
                def _(s, v_g1=v_g1):
                    s.wait_ge(vsem.h, v_g1)
                    s.activation(gsc[:, 2 * E:3 * E], gsc[:, 0:E], AF.Exp, bias=gcol[:, 1:2])
                    asem.inc(s.activation(gcol[:, 3:4], gcol[:, 2:3], AF.Exp, bias=gcol[:, 1:2]))
                a_g = asem.n

                @block.vector
                def _(v, a_g=a_g):
                    def vf(ins):
                        vsem.inc(ins)
                        v.wait_ge(vsem.h, vsem.n)
                    v.wait_ge(asem.h, a_g)
                    vf(v.tensor_scalar_add(gcol[:, 4:5], gcol[:, 3:4], 1.0))
                    vf(v.reciprocal(gcol[:, 5:6], gcol[:, 4:5]))
                    vf(v.tensor_scalar(gsc[:, E:2 * E], gsc[:, 0:E], gcol[:, 2:3], None, ALU.is_ge))
                    vf(v.tensor_mul(gsc[:, 2 * E:3 * E], gsc[:, 2 * E:3 * E],
                                    gsc[:, E:2 * E]))
                    vsem.inc(v.tensor_scalar(gsc[:, 3 * E:4 * E].bitcast(BF16)[:, 0:E],
                                             gsc[:, 2 * E:3 * E], gcol[:, 5:6], None, ALU.mult))
                v_g2 = vsem.n

                @block.tensor
                def _(t, v_g2=v_g2):
                    t.wait_ge(vsem.h, v_g2)
                    tsem.inc(t.transpose(pAb[0:8, 3072:3200],
                                         gsc[:, 3 * E:4 * E].bitcast(BF16)[:, 0:E], idb[:, :]))
                t_c = tsem.n

                @block.vector
                def _(v, t_c=t_c, tok0=tok0):
                    v.wait_ge(tsem.h, t_c)
                    vsem.inc(v.tensor_copy(cT[0:8, tok0:tok0 + 128], pAb[0:8, 3072:3200]))

        def layernorm(fold_x, with_gate):
            v_enter = vsem.n
            prev_vu = [v_enter]
            for tt in range(NTT):
                av0 = asem.n
                tv0 = tsem.n

                @block.scalar
                def _(s, tt=tt, tv0=tv0, v_enter=v_enter):
                    if tt == 0:
                        s.wait_ge(vsem.h, v_enter)
                        s.wait_ge(qsem.h, q_x)
                    for dc in range(NDC):
                        if dc >= 2:
                            s.wait_ge(tsem.h, tv0 + dc - 1)
                        asem.inc(s.activation(sq[dc % 2][:, :],
                                              xfm[:, dc * SL + tt * T:][:, 0:T], AF.Square))

                @block.tensor
                def _(t, tt=tt, av0=av0, wait_v=prev_vu[0]):
                    t.wait_ge(vsem.h, wait_v)
                    for dc in range(NDC):
                        x_sl = xfm[:, dc * SL + tt * T:][:, 0:T]
                        t.matmul(pBa[0:1, 1024:1536], onesf[:, 0:1], x_sl,
                                 start=(dc == 0), stop=(dc == NDC - 1))
                        t.wait_ge(asem.h, av0 + dc + 1)
                        tsem.inc(t.matmul(pBa[0:1, 1536:2048], onesf[:, 0:1], sq[dc % 2][:, :],
                                          start=(dc == 0), stop=(dc == NDC - 1)))
                t_sums = tsem.n

                @block.vector
                def _(v, t_sums=t_sums):
                    v.wait_ge(tsem.h, t_sums)
                    v.tensor_scalar_mul(pBa[0:1, 1024:1536], pBa[0:1, 1024:1536], 1.0 / D)
                    vsem.inc(v.tensor_scalar_mul(pBa[0:1, 1536:2048], pBa[0:1, 1536:2048], 1.0 / D))
                v_mean = vsem.n

                @block.scalar
                def _(s, v_mean=v_mean):
                    s.wait_ge(vsem.h, v_mean)
                    asem.inc(s.activation(ROW_TMP, pBa[0:1, 1024:1536], AF.Square))
                a_msq = asem.n

                @block.vector
                def _(v, a_msq=a_msq):
                    v.wait_ge(asem.h, a_msq)
                    v.tensor_sub(pBa[0:1, 1536:2048], pBa[0:1, 1536:2048], ROW_TMP)
                    vsem.inc(v.tensor_scalar_add(pBa[0:1, 1536:2048], pBa[0:1, 1536:2048], EPS))
                v_var = vsem.n

                @block.scalar
                def _(s, v_var=v_var):
                    s.wait_ge(vsem.h, v_var)
                    asem.inc(s.activation(ROW_TMP, pBa[0:1, 1536:2048], AF.Ln))
                a_ln = asem.n

                @block.vector
                def _(v, a_ln=a_ln):
                    v.wait_ge(asem.h, a_ln)
                    vsem.inc(v.tensor_scalar_mul(ROW_TMP, ROW_TMP, -0.5))
                v_ln = vsem.n

                @block.scalar
                def _(s, v_ln=v_ln):
                    s.wait_ge(vsem.h, v_ln)
                    asem.inc(s.activation(ROW_R, ROW_TMP, AF.Exp))
                a_r = asem.n

                @block.vector
                def _(v, a_r=a_r, with_gate=with_gate):
                    v.wait_ge(asem.h, a_r)
                    v.tensor_mul(ROW_MR, pBa[0:1, 1024:1536], ROW_R)
                    if with_gate:
                        vsem.inc(v.tensor_scalar_mul(ROW_NM, pBa[0:1, 1024:1536], -1.0))
                    else:
                        vsem.inc(v.engine_nop())
                v_rows = vsem.n

                if with_gate:
                    gate(tt, v_rows)

                @block.tensor
                def _(t, v_rows=v_rows):
                    t.wait_ge(vsem.h, v_rows)
                    tsem.inc(t.matmul(pBa[:, 1024:1536], onesf[0:1, 0:128], ROW_R,
                                      start=True, stop=True))
                    tsem.inc(t.matmul(pBa[:, 1536:2048], onesf[0:1, 0:128], ROW_MR,
                                      start=True, stop=True))
                t_bc = tsem.n
                vu0 = vsem.n
                au0 = asem.n

                @block.vector
                def _(v, tt=tt, t_bc=t_bc, fold_x=fold_x, au0=au0):
                    v.wait_ge(tsem.h, t_bc)
                    for dc in range(NDC):
                        x_sl = xfm[:, dc * SL + tt * T:][:, 0:T]
                        p = dc % 2
                        if dc >= 2:
                            v.wait_ge(asem.h, au0 + dc - 1)
                        v.tensor_mul(uf32[p][:, :], x_sl, pBa[:, 1024:1536])
                        vsem.inc(v.tensor_sub(uf32[p][:, :], uf32[p][:, :], pBa[:, 1536:2048]))
                        if fold_x:
                            vsem.inc(v.scalar_tensor_tensor(x_sl, uf32[p][:, :], 0.5, x_sl,
                                                            ALU.mult, ALU.add))
                        else:
                            vsem.inc(v.engine_nop())

                @block.scalar
                def _(s, tt=tt, vu0=vu0):
                    for dc in range(NDC):
                        u_sl = ub[:, dc * SL + tt * T:][:, 0:T]
                        s.wait_ge(vsem.h, vu0 + 2 * dc + 1)
                        asem.inc(s.copy(u_sl, uf32[dc % 2][:, :]))
                prev_vu[0] = vsem.n

        def ffn_experts(call):
            t0 = tsem.n
            v0 = vsem.n
            a0 = asem.n
            a_enter = asem.n
            v_pre = vsem.n
            d0w = [wsp[0].n, wsp[1].n]

            @block.sync
            def _(sy, v_pre=v_pre):
                sy.wait_ge(vsem.h, v_pre)
                sy.dma_start(out=cdram[:, :], in_=cT[0:8, :]).then_inc(qsem.h, 16)
                qsem.n += 16
            q_cflat = qsem.n

            def t_base(e):
                return t0 + e * (4 + 32 * 16)

            def t_cb(e, tt):
                return t_base(e) + tt + 1

            def t_blk(e, hc, tt):
                return t_base(e) + 4 + (hc * NTT + tt) * 16

            def t_h1(e, hc, tt, i):
                return t_blk(e, hc, tt) + 2 * i + 1

            def t_h3(e, hc, tt, i):
                return t_blk(e, hc, tt) + 2 * i + 2

            def t_y(e, hc, tt, dc):
                return t_blk(e, hc, tt) + 8 + dc + 1

            def v_base(e):
                return v0 + e * (4 + 32 * 12)

            def v_z(e, tt):
                return v_base(e) + tt + 1

            def v_blkbase(e, hc, tt):
                return v_base(e) + 4 + (hc * NTT + tt) * 12

            def v_hmul(e, hc, tt, i):
                return v_blkbase(e, hc, tt) + i + 1

            def v_yadd(e, hc, tt, dc):
                return v_blkbase(e, hc, tt) + 4 + dc + 1

            def lin_h(e, hc, tt, i):
                return ((e * NHC + hc) * NTT + tt) * 4 + i

            def lin_y(e, hc, tt, dc):
                return ((e * NHC + hc) * NTT + tt) * 8 + dc

            def unlin_h(k):
                return (k // (4 * NTT * NHC), (k // (4 * NTT)) % NHC,
                        (k // 4) % NTT, k % 4)

            def unlin_y(k):
                return (k // (8 * NTT * NHC), (k // (8 * NTT)) % NHC,
                        (k // 8) % NTT, k % 8)

            def a_silu(e, hc, tt, i):
                return a0 + ((e * NHC + hc) * NTT + tt) * 4 + i + 1

            @block.gpsimd
            def _(g, call=call, t0=t0):
                for c in range(E * NHC):
                    e, hc = divmod(c, NHC)
                    p = c % 2
                    if c < 2:
                        g.wait_ge(tsem.h, t0)
                    else:
                        ee, hh = divmod(c - 2, NHC)
                        g.wait_ge(tsem.h, t_y(ee, hh, NTT - 1, 7))
                    g.dma_start(out=w1t[p].rearrange("p (dc h) -> p dc h", h=HC),
                                in_=w1d[e][:, hc * HC:(hc + 1) * HC]
                                .rearrange("(dc p) h -> p dc h", p=128)).then_inc(wsp[p].h, 16)
                    g.dma_start(out=w3t[p].rearrange("p (dc h) -> p dc h", h=HC),
                                in_=w3d[e][:, hc * HC:(hc + 1) * HC]
                                .rearrange("(dc p) h -> p dc h", p=128)).then_inc(wsp[p].h, 16)
                    g.dma_start(out=w2t[p].rearrange("p (hh d) -> p hh d", d=D),
                                in_=w2d[e][hc * HC:(hc + 1) * HC, :]
                                .rearrange("(hh p) d -> p hh d", p=128)).then_inc(wsp[p].h, 16)
                    wsp[p].n += 48

            @block.sync
            def _(sy, q_cflat=q_cflat):
                for e in range(E):
                    if e == 0:
                        sy.wait_ge(qsem.h, q_cflat)
                    else:
                        sy.wait_ge(tsem.h, t_cb(e - 1, NTT - 1))
                    sy.dma_start(out=crow[0:1, :], in_=cdram[e:e + 1, :]).then_inc(qsem.h, 16)
                    qsem.n += 16
            q_crow0 = q_cflat

            @block.tensor
            def _(t, call=call, q_crow0=q_crow0, d0w=d0w):
                for e in range(E):
                    for tt in range(NTT):
                        t.wait_ge(qsem.h, q_crow0 + 16 * (e + 1))
                        if tt > 0:
                            t.wait_ge(vsem.h, v_z(e, tt - 1))
                        if e > 0 and tt == 0:
                            t.wait_ge(vsem.h, v_z(e - 1, NTT - 1))
                        tsem.inc(t.matmul(pBa[:, 1024:1536], onesb[:, 0:128],
                                          crow[0:1, tt * T:(tt + 1) * T], start=True, stop=True))
                        assert tsem.n == t_cb(e, tt)
                    for hc in range(NHC):
                        c = e * NHC + hc
                        p = c % 2
                        t.wait_ge(wsp[p].h, d0w[p] + 48 * (c // 2 + 1))
                        for tt in range(NTT):
                            t.wait_ge(vsem.h, v_z(e, tt))
                            for i in range(4):
                                kk = lin_h(e, hc, tt, i)
                                if kk >= 2:
                                    t.wait_ge(vsem.h, v_hmul(*unlin_h(kk - 2)))
                                pp = kk % 2
                                for dc in range(NDC):
                                    ins = t.matmul(bank(pAa, pp),
                                                   w1t[p][:, dc * HC + i * 128:][:, 0:128],
                                                   qoz[:, dc * SL + tt * T:][:, 0:T],
                                                   start=(dc == 0), stop=(dc == NDC - 1))
                                tsem.inc(ins)
                                for dc in range(NDC):
                                    ins = t.matmul(bank(pAa, 2 + pp),
                                                   w3t[p][:, dc * HC + i * 128:][:, 0:128],
                                                   qoz[:, dc * SL + tt * T:][:, 0:T],
                                                   start=(dc == 0), stop=(dc == NDC - 1))
                                tsem.inc(ins)
                                assert tsem.n == t_h3(e, hc, tt, i)
                            t.wait_ge(vsem.h, v_hmul(e, hc, tt, 3))
                            for dc in range(NDC):
                                kk = lin_y(e, hc, tt, dc)
                                if kk >= 2:
                                    t.wait_ge(vsem.h, v_yadd(*unlin_y(kk - 2)))
                                pp = kk % 2
                                for i in range(4):
                                    ins = t.matmul(bank(pBa, pp),
                                                   w2t[p][:, i * D + dc * 128:][:, 0:128],
                                                   hbuf[:, i * T:(i + 1) * T],
                                                   start=(i == 0), stop=(i == 3))
                                tsem.inc(ins)
                                assert tsem.n == t_y(e, hc, tt, dc)

            @block.vector
            def _(v, call=call, a_enter=a_enter):
                for e in range(E):
                    for tt in range(NTT):
                        if e == 0 and tt == 0:
                            v.wait_ge(asem.h, a_enter)
                        v.wait_ge(tsem.h, t_cb(e, tt))
                        if e > 0:
                            v.wait_ge(tsem.h, t_h3(e - 1, NHC - 1, tt, 3))
                        ins = None
                        for dc in range(NDC):
                            ins = v.tensor_mul(qoz[:, dc * SL + tt * T:][:, 0:T],
                                               ub[:, dc * SL + tt * T:][:, 0:T],
                                               pBa[:, 1024:1536])
                        vsem.inc(ins)
                        assert vsem.n == v_z(e, tt)
                    for hc in range(NHC):
                        for tt in range(NTT):
                            for i in range(4):
                                kk = lin_h(e, hc, tt, i)
                                pp = kk % 2
                                v.wait_ge(asem.h, a_silu(e, hc, tt, i))
                                v.wait_ge(tsem.h, t_h3(e, hc, tt, i))
                                vsem.inc(v.tensor_mul(hbuf[:, i * T:(i + 1) * T],
                                                      s1[pp][:, :], bank(pAa, 2 + pp)))
                                assert vsem.n == v_hmul(e, hc, tt, i)
                            for dc in range(NDC):
                                kk = lin_y(e, hc, tt, dc)
                                pp = kk % 2
                                v.wait_ge(tsem.h, t_y(e, hc, tt, dc))
                                x_sl = xfm[:, dc * SL + tt * T:][:, 0:T]
                                vsem.inc(v.scalar_tensor_tensor(x_sl, bank(pBa, pp), 0.5,
                                                                x_sl, ALU.mult, ALU.add))
                                assert vsem.n == v_yadd(e, hc, tt, dc)

            @block.scalar
            def _(s, call=call):
                for e in range(E):
                    for hc in range(NHC):
                        for tt in range(NTT):
                            for i in range(4):
                                kk = lin_h(e, hc, tt, i)
                                pp = kk % 2
                                s.wait_ge(tsem.h, t_h1(e, hc, tt, i))
                                if kk >= 2:
                                    s.wait_ge(vsem.h, v_hmul(*unlin_h(kk - 2)))
                                asem.inc(s.activation(s1[pp][:, :], bank(pAa, pp), AF.Silu))
                                assert asem.n == a_silu(e, hc, tt, i)

        def attention(stop_after="all"):
            a_enter = asem.n
            for wi, wsrc_ in enumerate([wqT, wkT, wvT]):
                t_pre = tsem.n

                @block.gpsimd
                def _(g, wsrc_=wsrc_, t_pre=t_pre, wi=wi):
                    g.wait_ge(tsem.h, t_pre)
                    if wi == 2:
                        g.dma_start(out=bvrowb[:, :], in_=bvd[None, :]).then_inc(dsem.h, 16)
                        dsem.n += 16
                    g.dma_start(out=wta[:, :].rearrange("p (dc f) -> p dc f", f=D),
                                in_=wsrc_.rearrange("(dc p) f -> p dc f", p=128)
                                ).then_inc(dsem.h, 16)
                    dsem.n += 16
                d_w = dsem.n

                if wi == 0:
                    a_pre = asem.n

                    @block.tensor
                    def _(t, d_w=d_w, a_pre=a_pre, a_enter=a_enter):
                        t.wait_ge(dsem.h, d_w)
                        t.wait_ge(asem.h, a_enter)
                        for fc in range(NDC):
                            for tt in range(NTT):
                                k = fc * NTT + tt
                                if k >= 2:
                                    t.wait_ge(asem.h, a_pre + k - 1)
                                for dc in range(NDC):
                                    ins = t.matmul(bank(pAa, k % 2),
                                                   wta[:, dc * D + fc * 128:][:, 0:128],
                                                   ub[:, dc * SL + tt * T:][:, 0:T],
                                                   start=(dc == 0), stop=(dc == NDC - 1))
                                tsem.inc(ins)

                    @block.scalar
                    def _(s, t_pre=t_pre):
                        for fc in range(NDC):
                            for tt in range(NTT):
                                k = fc * NTT + tt
                                s.wait_ge(tsem.h, t_pre + k + 1)
                                asem.inc(s.activation(qoz[:, fc * SL + tt * T:][:, 0:T],
                                                      bank(pAa, k % 2), AF.Identity,
                                                      bias=battn[:, 0 * NDC + fc:][:, 0:1],
                                                      scale=SCALE))
                elif wi == 1:
                    a_pre = asem.n
                    q_pre = qsem.n

                    @block.tensor
                    def _(t, d_w=d_w, a_pre=a_pre):
                        t.wait_ge(dsem.h, d_w)
                        t.wait_ge(asem.h, a_pre)
                        for fc in range(NDC):
                            for tt in range(NTT):
                                k = fc * NTT + tt
                                if k >= 2:
                                    t.wait_ge(asem.h, a_pre + k - 1)
                                for dc in range(NDC):
                                    ins = t.matmul(bank(pAa, k % 2),
                                                   wta[:, dc * D + fc * 128:][:, 0:128],
                                                   ub[:, dc * SL + tt * T:][:, 0:T],
                                                   start=(dc == 0), stop=(dc == NDC - 1))
                                tsem.inc(ins)

                    @block.scalar
                    def _(s, t_pre=t_pre, q_pre=q_pre, a_pre=a_pre):
                        for fc in range(NDC):
                            for tt in range(NTT):
                                k = fc * NTT + tt
                                s.wait_ge(tsem.h, t_pre + k + 1)
                                if fc >= 1 and tt == 0:
                                    s.wait_ge(qsem.h, q_pre + 16 * fc)
                                asem.inc(s.activation(kvx[:, tt * T:(tt + 1) * T],
                                                      bank(pAa, k % 2), AF.Identity,
                                                      bias=battn[:, 1 * NDC + fc:][:, 0:1]))

                    @block.sync
                    def _(sy, a_pre=a_pre):
                        for fc in range(NDC):
                            sy.wait_ge(asem.h, a_pre + NTT * (fc + 1))
                            sy.dma_start(out=kexp[fc * 128:(fc + 1) * 128, :],
                                         in_=kvx[:, :]).then_inc(qsem.h, 16)
                            qsem.n += 16
                    q_kexp = qsem.n

                    @block.gpsimd
                    def _(g, q_kexp=q_kexp):
                        g.wait_ge(qsem.h, q_kexp)
                        g.collective_compute(
                            "AllGather", ALU.bypass,
                            replica_groups=[[0, 1], [2, 3], [4, 5], [6, 7]],
                            ins=[kexp.opt()], outs=[kg.opt()]).then_inc(ck.h, 1)
                        ck.n = 1
                elif wi == 2:
                    v_pre = vsem.n
                    q_pre = qsem.n

                    @block.tensor
                    def _(t, d_w=d_w, v_pre=v_pre):
                        t.wait_ge(dsem.h, d_w)
                        t.wait_ge(vsem.h, v_pre)
                        for tc in range(NQC):
                            for fb in range(2):
                                k = tc * 2 + fb
                                if k >= 4:
                                    t.wait_ge(vsem.h, v_pre + k - 3)
                                for dc in range(NDC):
                                    t.matmul(bank(pBa, k % 4),
                                             ub[:, dc * SL + tc * 128:][:, 0:128],
                                             wta[:, dc * D + fb * T:][:, 0:T],
                                             start=(dc == 0), stop=False)
                                ins = t.matmul(bank(pBa, k % 4), onesb[:, 0:128],
                                               bvrowb[0:1, fb * T:(fb + 1) * T],
                                               start=False, stop=True)
                                tsem.inc(ins)

                    @block.vector
                    def _(v, t_pre=t_pre, q_pre=q_pre):
                        for tc in range(NQC):
                            for fb in range(2):
                                k = tc * 2 + fb
                                v.wait_ge(tsem.h, t_pre + k + 1)
                                if tc >= 1 and fb == 0:
                                    v.wait_ge(qsem.h, q_pre + 16 * tc)
                                vsem.inc(v.tensor_copy(kvx[:, fb * T:(fb + 1) * T],
                                                       bank(pBa, k % 4)))

                    @block.sync
                    def _(sy, v_pre=v_pre):
                        for tc in range(NQC):
                            sy.wait_ge(vsem.h, v_pre + 2 * (tc + 1))
                            sy.dma_start(out=vexp[tc * 128:(tc + 1) * 128, :],
                                         in_=kvx[:, 0:1024]).then_inc(qsem.h, 16)
                            qsem.n += 16
                    q_vexp = qsem.n

                    @block.gpsimd
                    def _(g, q_vexp=q_vexp):
                        g.wait_ge(qsem.h, q_vexp)
                        g.collective_compute(
                            "AllGather", ALU.bypass,
                            replica_groups=[[0, 1], [2, 3], [4, 5], [6, 7]],
                            ins=[vexp.opt()], outs=[vg.opt()]).then_inc(cv.h, 1)
                        cv.n = 1

            t_postv = tsem.n

            @block.gpsimd
            def _(g, t_postv=t_postv):
                g.wait_ge(tsem.h, t_postv)
                for p in range(2):
                    vv = vh[p].rearrange("p (k c) -> p k c", c=65)
                    g.memset(vv[:, :, 64:65], 1.0)
                    g.memset(khb[p][64:128, :], 0.0)
                    g.memset(khb[p][64:65, :], 1.0)
                    g.memset(qtile[p][64:128, :], 0.0)
                g.memset(srow[:, :], 1.0).then_inc(esem.h, 1)
                esem.n += 1

            if stop_after == "qkv":
                return
            NF = 8 * NQC
            tfill = {}
            ttp = {}
            tBsT = {}
            tBAV = {}
            tBrs = {}
            tBend = {}
            tAend = {}
            vbaseA = {}
            sbaseB = {}
            q_mst = {}
            q_kb = {}
            prev_osc = [0]
            prev_vred_tail = [None, None]

            for s in range(NH + 1):
                hA = s if s < NH else None
                hB = s - 1 if s >= 1 else None
                pA_ = None if hA is None else hA % 2
                pB_ = None if hB is None else hB % 2

                def emit_loads(hh):
                    pp = hh % 2
                    kbe = kbp[pp]
                    lo = 64 * (hh % 2)
                    fcq = hh // 2

                    @block.sync
                    def _(sy, hh=hh, lo=lo, fcq=fcq, kbe=kbe, pp=pp):
                        if hh == 0:
                            sy.wait_ge(ck.h, 1)
                            sy.wait_ge(cv.h, 1)
                            sy.wait_ge(esem.h, esem.n)
                        if hh >= 2:
                            sy.wait_ge(tsem.h, tBend[hh - 2])
                        sy.dma_start(out=khb[pp][0:64, 0:SL],
                                     in_=kg[64 * hh:64 * hh + 64, :]).then_inc(kbe.h, 16)
                        sy.dma_start(out=khb[pp][0:64, SL:2 * SL],
                                     in_=kg[D + 64 * hh:D + 64 * hh + 64, :]).then_inc(kbe.h, 16)
                        vv = vh[pp].rearrange("p (k c) -> p k c", c=65)
                        sy.dma_start(out=vv[:, :, 0:64],
                                     in_=vg.rearrange("(k p) f -> p k f", p=128)
                                     [:, :, 64 * hh:64 * hh + 64]).then_inc(kbe.h, 16)
                        sy.dma_start(out=qtile[pp][0:64, :],
                                     in_=qoz[lo:lo + 64, fcq * SL:(fcq + 1) * SL]
                                     ).then_inc(kbe.h, 16)
                        kbe.n += 64

                    @block.gpsimd
                    def _(g, hh=hh, kbe=kbe, pp=pp):
                        if hh >= 2:
                            g.wait_ge(tsem.h, tBend[hh - 2])
                        kbe.inc(g.memset(qtile[pp][64:65, :], 0.0))
                    q_kb[hh] = kbp[pp].n

                if s == 0:
                    emit_loads(0)
                if hB is not None:
                    q_kB = q_kb[hB]

                vbase = vsem.n
                if hA is not None:
                    vbaseA[hA] = vbase

                def v_red(qc, f, vbase=vbase, hB=hB):
                    ex = 2 * (qc // 4) if hB is not None else 0
                    return vbase + 10 * qc + ex + f + 1

                def v_rfin(qc, vbase=vbase, hB=hB):
                    ex = 2 * (qc // 4) if hB is not None else 0
                    return vbase + 10 * qc + ex + 9

                def v_copy(qc, vbase=vbase, hB=hB):
                    ex = 2 * (qc // 4) if hB is not None else 0
                    return vbase + 10 * qc + ex + 10

                def v_recip(qb, vbase=vbase, hA=hA):
                    return vbase + (42 * qb + 41 if hA is not None else 2 * qb + 1)

                def v_osc(qb, vbase=vbase, hA=hA):
                    return vbase + (42 * qb + 42 if hA is not None else 2 * qb + 2)

                sbase = asem.n
                if hB is not None:
                    sbaseB[hB] = sbase

                def s_exp(qb, g, sbase=sbase):
                    return sbase + 33 * qb + g + 1

                def s_rsb(qb, sbase=sbase):
                    return sbase + 33 * qb + 33

                qpred = [None]

                @block.tensor
                def _(t, s=s, hA=hA, hB=hB, pA_=pA_, pB_=pB_,
                      v_red=v_red, v_rfin=v_rfin, v_copy=v_copy,
                      v_recip=v_recip, v_osc=v_osc, s_exp=s_exp, s_rsb=s_rsb,
                      vbase=vbase, sbase=sbase):
                    items = []
                    if hA is not None:
                        loA = 64 * (hA % 2)
                        fcqA = hA // 2

                        def mk_fill(qc, f, hA=hA, pA_=pA_):
                            def emit():
                                FF = 8 * qc + f
                                if FF == 0:
                                    t.wait_ge(kbp[pA_].h, q_kb[hA])
                                if FF < 2:
                                    if prev_vred_tail[f] is not None:
                                        t.wait_ge(vsem.h, prev_vred_tail[f])
                                elif f == 0 and FF % 8 == 0:
                                    t.wait_ge(vsem.h, v_copy(qc - 1))
                                else:
                                    t.wait_ge(vsem.h, v_red((FF - 2) // 8, (FF - 2) % 8))
                                q_sl = qtile[pA_][0:128, qc * 128:(qc + 1) * 128]
                                ins = t.matmul(pBa[:, (2 + f % 2) * 512:(3 + f % 2) * 512],
                                               q_sl, khb[pA_][0:128, f * 512:(f + 1) * 512],
                                               start=True, stop=True)
                                tsem.inc(ins)
                                tfill[(hA, FF)] = tsem.n
                            return emit

                        def mk_tp(qc, hA=hA):
                            def emit():
                                t.wait_ge(vsem.h, v_rfin(qc))
                                tsem.inc(t.transpose(pBa[0:1, 1024:1152],
                                                     gsc[:, 0:1], idf[:, :]))
                                ttp[(hA, qc)] = tsem.n
                            return emit

                        for qc in range(NQC):
                            for f in range(8):
                                items.append(mk_fill(qc, f))
                            items.append(mk_tp(qc))
                    it = iter(items)

                    def inject(n):
                        for _i in range(n):
                            em = next(it, None)
                            if em is None:
                                return
                            em()

                    if hB is not None:
                        loB = 64 * (hB % 2)
                        kbe = kbp[pB_]
                        t.wait_ge(kbe.h, q_kB)
                        t.wait_ge(qsem.h, q_mst[hB])

                        def attsl(g):
                            sl = g % 8
                            return att[sl // 4][:, (sl % 4) * 512:(sl % 4 + 1) * 512]

                        def sTq(qb, g, hB=hB, pB_=pB_):
                            ins = t.matmul(pAa[:, (g % 4) * 512:(g % 4 + 1) * 512],
                                           khb[pB_][0:128, g * 128:(g + 1) * 128],
                                           qtile[pB_][0:128, qb * T:(qb + 1) * T],
                                           start=True, stop=True)
                            tsem.inc(ins)
                            tBsT[(hB, qb, g)] = tsem.n
                        for qb in range(NTT):
                            for g in range(4):
                                if qb == 0:
                                    t.wait_ge(asem.h, sbase)
                                else:
                                    t.wait_ge(asem.h, s_exp(qb - 1, 28 + g))
                                sTq(qb, g)
                            for g in range(NKC):
                                inject((2 if g < 4 else 1) if hA is not None else 0)
                                t.wait_ge(asem.h, s_exp(qb, g))
                                if g == 0:
                                    t.wait_ge(vsem.h,
                                              v_osc(qb - 1) if qb > 0 else prev_osc[0])
                                ins = t.matmul(pBa[0:65, 0:512],
                                               vh[pB_][:, g * 65:(g + 1) * 65],
                                               attsl(g),
                                               start=(g == 0), stop=(g == NKC - 1))
                                tsem.inc(ins)
                                tBAV[(hB, qb, g)] = tsem.n
                                if g + 4 < NKC:
                                    sTq(qb, g + 4)
                            t.wait_ge(vsem.h, v_recip(qb))
                            tsem.inc(t.matmul(pBa[:, 512:1024], onesf[64:65, 0:128],
                                              srow[64:65, :], start=True, stop=True))
                            tBrs[(hB, qb)] = tsem.n
                    inject(10 ** 9)
                    if hA is not None:
                        tAend[hA] = tsem.n
                    if hB is not None:
                        tBend[hB] = tsem.n

                if hB is not None:
                    @block.scalar
                    def _(s_, hB=hB):
                        def attsl(g):
                            sl = g % 8
                            return att[sl // 4][:, (sl % 4) * 512:(sl % 4 + 1) * 512]
                        for qb in range(NTT):
                            for g in range(NKC):
                                s_.wait_ge(tsem.h, tBsT[(hB, qb, g)])
                                if g >= 8:
                                    s_.wait_ge(tsem.h, tBAV[(hB, qb, g - 8)])
                                elif qb > 0:
                                    s_.wait_ge(tsem.h, tBAV[(hB, qb - 1, 24 + g)])
                                asem.inc(s_.activation(attsl(g),
                                                       pAa[:, (g % 4) * 512:(g % 4 + 1) * 512],
                                                       AF.Exp))
                            s_.wait_ge(tsem.h, tBrs[(hB, qb)])
                            asem.inc(s_.copy(rsb[0:64, :], pBa[0:64, 512:1024]))

                qpred[0] = qsem.n

                @block.vector
                def _(v, s=s, hA=hA, hB=hB, pB_=pB_, vbase=vbase, sbase=sbase,
                      v_red=v_red, v_rfin=v_rfin, v_copy=v_copy,
                      v_recip=v_recip, v_osc=v_osc, s_rsb=s_rsb, qpred=qpred):
                    def a_block(qc, hA=hA):
                        if qc == 0 and hA is not None and hA >= 1:
                            v.wait_ge(qsem.h, q_mst[hA - 1])
                        for f in range(8):
                            v.wait_ge(tsem.h, tfill[(hA, 8 * qc + f)])
                            vsem.inc(v.reduce_max(gcol[:, f:f + 1],
                                                  pBa[:, (2 + f % 2) * 512:(3 + f % 2) * 512],
                                                  axis=AX.X))
                        v.wait_ge(vsem.h, vsem.n)
                        vsem.inc(v.reduce_max(gsc[:, 0:1], gcol[:, 0:8], axis=AX.X,
                                              negate=True))
                        v.wait_ge(tsem.h, ttp[(hA, qc)])
                        vsem.inc(v.tensor_copy(mrow[0:1, qc * 128:(qc + 1) * 128],
                                               pBa[0:1, 1024:1152]))

                    def b_block(qb, hB=hB, pB_=pB_):
                        loB = 64 * (hB % 2)
                        fcqB = hB // 2
                        v.wait_ge(tsem.h, tBAV[(hB, qb, 31)])
                        vsem.inc(v.reciprocal(srow[64:65, :], pBa[64:65, 0:512]))
                        v.wait_ge(asem.h, s_rsb(qb))
                        if loB == 0:
                            vsem.inc(v.tensor_mul(
                                qoz[0:64, fcqB * SL + qb * T:][:, 0:T],
                                pBa[0:64, 0:512], rsb[0:64, :]))
                        else:
                            if qb >= 2:
                                v.wait_ge(qsem.h, qpred[0] + 16 * (qb - 1))
                            vsem.inc(v.tensor_mul(obuf2[qb % 2][:, :],
                                                  pBa[0:64, 0:512], rsb[0:64, :]))

                    if hA is not None and hB is not None:
                        for qb in range(NTT):
                            for qc in range(4 * qb, 4 * qb + 4):
                                a_block(qc)
                            b_block(qb)
                    elif hA is not None:
                        for qc in range(NQC):
                            a_block(qc)
                    else:
                        for qb in range(NTT):
                            b_block(qb)
                    if hB is not None:
                        assert vsem.n == v_osc(NTT - 1), (vsem.n, v_osc(NTT - 1))
                    else:
                        assert vsem.n == v_copy(NQC - 1), (vsem.n, v_copy(NQC - 1))

                if hA is not None:
                    prev_vred_tail[0] = v_copy(NQC - 1)
                    prev_vred_tail[1] = v_red(NQC - 1, 7)
                if hB is not None:
                    prev_osc[0] = v_osc(NTT - 1)

                @block.sync
                def _(sy, hA=hA, hB=hB, v_copy=v_copy, v_osc=v_osc):
                    if hB is not None and hB % 2 == 1:
                        fcqB = hB // 2
                        for qb in range(NTT):
                            sy.wait_ge(vsem.h, v_osc(qb))
                            sy.dma_start(out=qoz[64:128, fcqB * SL + qb * T:][:, 0:T],
                                         in_=obuf2[qb % 2][:, :]).then_inc(qsem.h, 16)
                            qsem.n += 16
                    if hA is not None:
                        sy.wait_ge(vsem.h, v_copy(NQC - 1))
                        sy.dma_start(out=qtile[pA_][64:65, :],
                                     in_=mrow[0:1, :]).then_inc(qsem.h, 16)
                        qsem.n += 16
                if hA is not None:
                    q_mst[hA] = qsem.n
                if s + 1 < NH:
                    emit_loads(s + 1)

            t_heads = tsem.n

            @block.gpsimd
            def _(g, t_heads=t_heads):
                g.wait_ge(tsem.h, t_heads)
                g.dma_start(out=wta[:, :].rearrange("p (dc f) -> p dc f", f=D),
                            in_=woT.rearrange("(dc p) f -> p dc f", p=128)
                            ).then_inc(dsem.h, 16)
                dsem.n += 16
            d_w_wo = dsem.n
            v_wo0, q_wo0 = vsem.n, qsem.n

            @block.tensor
            def _(t, v_wo0=v_wo0, q_wo0=q_wo0, d_w_wo=d_w_wo):
                t.wait_ge(vsem.h, v_wo0)
                t.wait_ge(qsem.h, q_wo0)
                t.wait_ge(dsem.h, d_w_wo)
                for dc in range(NDC):
                    for tt in range(NTT):
                        k = dc * NTT + tt
                        if k >= 2:
                            t.wait_ge(vsem.h, v_wo0 + k - 1)
                        for fc in range(NDC):
                            ins = t.matmul(bank(pAa, k % 2),
                                           wta[:, fc * D + dc * 128:][:, 0:128],
                                           qoz[:, fc * SL + tt * T:][:, 0:T],
                                           start=(fc == 0), stop=(fc == NDC - 1))
                        tsem.inc(ins)
            t_wo0 = tsem.n - NDC * NTT

            @block.vector
            def _(v, t_wo0=t_wo0):
                for dc in range(NDC):
                    for tt in range(NTT):
                        k = dc * NTT + tt
                        v.wait_ge(tsem.h, t_wo0 + k + 1)
                        x_sl = xfm[:, dc * SL + tt * T:][:, 0:T]
                        vsem.inc(v.scalar_tensor_tensor(
                            x_sl, bank(pAa, k % 2), battn[:, 3 * NDC + dc:][:, 0:1],
                            x_sl, ALU.add, ALU.add))

        if stop_after != "p0":
            layernorm(fold_x=True, with_gate=True)
        if stop_after in ("u1", "c1"):
            t_d, v_d, a_d = tsem.n, vsem.n, asem.n

            @block.gpsimd
            def _(g):
                g.wait_ge(tsem.h, t_d)
                g.wait_ge(vsem.h, v_d)
                g.wait_ge(asem.h, a_d)
                if stop_after == "u1":
                    for dc in range(NDC):
                        g.dma_start(out=outT[dc * 128:(dc + 1) * 128, :],
                                    in_=ub[:, dc * SL:(dc + 1) * SL]).then_inc(dsem.h, 16)
                        dsem.n += 16
                else:
                    g.dma_start(out=outT[0:8, :], in_=cT[0:8, :]).then_inc(dsem.h, 16)
                    dsem.n += 16
                g.wait_ge(dsem.h, dsem.n)
        if stop_after not in ("p0", "ln1", "u1", "c1"):
            ffn_experts(0)
        if stop_after not in ("p0", "ln1", "u1", "c1", "ffn1"):
            layernorm(fold_x=False, with_gate=False)
            attention("qkv" if stop_after == "qkv" else "all")
        if stop_after not in ("p0", "ln1", "u1", "c1", "ffn1", "qkv", "attn"):
            layernorm(fold_x=True, with_gate=True)
            ffn_experts(1)

        t_fin, v_fin, a_fin = tsem.n, vsem.n, asem.n
        skip_out = stop_after in ("u1", "c1")

        @block.sync
        def _(sy):
            if skip_out:
                sy.wait_ge(dsem.h, dsem.n)
                return
            sy.wait_ge(vsem.h, v_fin)
            sy.wait_ge(tsem.h, t_fin)
            sy.wait_ge(asem.h, a_fin)
            for dc in range(NDC):
                sy.dma_start(out=outT[dc * 128:(dc + 1) * 128, :],
                             in_=xfm[:, dc * SL:(dc + 1) * SL]).then_inc(qsem.h, 16)
            qsem.n += NDC * 16
            sy.wait_ge(qsem.h, qsem.n)

    return nc


def kernel(L0_attn_x, L0_attn_Wq, L0_attn_bq, L0_attn_Wk, L0_attn_bk,
           L0_attn_Wv, L0_attn_bv, L0_attn_Wo, L0_attn_bo,
           L0_ffn_Wg, L0_ffn_bg, L0_ffn_W1, L0_ffn_W2, L0_ffn_W3):
    x = np.ascontiguousarray(np.asarray(L0_attn_x, dtype=np.float32)).reshape(B * S, D)
    com = dict(
        wqT=np.ascontiguousarray(np.asarray(L0_attn_Wq, np.float32).T),
        wkT=np.ascontiguousarray(np.asarray(L0_attn_Wk, np.float32).T),
        wvT=np.ascontiguousarray(np.asarray(L0_attn_Wv, np.float32).T),
        woT=np.ascontiguousarray(np.asarray(L0_attn_Wo, np.float32).T),
        bq=np.asarray(L0_attn_bq, np.float32), bk=np.asarray(L0_attn_bk, np.float32),
        bv=np.asarray(L0_attn_bv, np.float32), bo=np.asarray(L0_attn_bo, np.float32),
        wg=np.asarray(L0_ffn_Wg, np.float32), bg=np.asarray(L0_ffn_bg, np.float32),
        w1=np.ascontiguousarray(np.asarray(L0_ffn_W1, np.float32)),
        w2=np.ascontiguousarray(np.asarray(L0_ffn_W2, np.float32)),
        w3=np.ascontiguousarray(np.asarray(L0_ffn_W3, np.float32)),
    )
    in_maps = []
    for c in range(NCORE):
        xs = np.ascontiguousarray(x[c * SL:(c + 1) * SL, :].T)
        in_maps.append(dict(xT=xs, **com))
    nc = build_program()
    res = run_bass_kernel_spmd(nc, in_maps, core_ids=list(range(NCORE)))
    outs = [np.asarray(r["outT"]).T for r in res.results]
    return np.concatenate(outs, axis=0).reshape(B, S, D)


# version 10 (final)
# speedup vs baseline: 1.1373x; 1.0152x over previous
import numpy as np
import concourse.bass as bass
import concourse.mybir as mybir
from concourse.bass_utils import run_bass_kernel_spmd

F32 = mybir.dt.float32
BF16 = mybir.dt.bfloat16
AF = mybir.ActivationFunctionType
ALU = mybir.AluOpType
AX = mybir.AxisListType

NCORE = 8
B, S, D = 4, 4096, 1024
SL = 2048
NH, HD = 16, 64
E, H = 8, 4096
NDC = D // 128
T = 512
NTT = SL // T
HC = 512
NHC = H // HC
SF = 4096
NKC = SF // 128
NQC = SL // 128
SCALE = 0.125
EPS = 1e-5


class Sem:
    def __init__(self, nc, ctx, name):
        self.h = ctx.enter_context(nc.semaphore(name))
        self.n = 0

    def inc(self, inst, k=1):
        inst.then_inc(self.h, k)
        self.n += k
        return self.n


def build_program(stop_after="all"):
    nc = bass.Bass("TRN2", target_bir_lowering=False, detect_race_conditions=False)

    xT = nc.dram_tensor("xT", [D, SL], F32, kind="ExternalInput").ap()
    wqT = nc.dram_tensor("wqT", [D, D], F32, kind="ExternalInput").ap()
    wkT = nc.dram_tensor("wkT", [D, D], F32, kind="ExternalInput").ap()
    wvT = nc.dram_tensor("wvT", [D, D], F32, kind="ExternalInput").ap()
    woT = nc.dram_tensor("woT", [D, D], F32, kind="ExternalInput").ap()
    bqd = nc.dram_tensor("bq", [D], F32, kind="ExternalInput").ap()
    bkd = nc.dram_tensor("bk", [D], F32, kind="ExternalInput").ap()
    bvd = nc.dram_tensor("bv", [D], F32, kind="ExternalInput").ap()
    bod = nc.dram_tensor("bo", [D], F32, kind="ExternalInput").ap()
    wgd = nc.dram_tensor("wg", [D, E], F32, kind="ExternalInput").ap()
    bgd = nc.dram_tensor("bg", [E], F32, kind="ExternalInput").ap()
    w1d = nc.dram_tensor("w1", [E, D, H], F32, kind="ExternalInput").ap()
    w2d = nc.dram_tensor("w2", [E, H, D], F32, kind="ExternalInput").ap()
    w3d = nc.dram_tensor("w3", [E, D, H], F32, kind="ExternalInput").ap()
    outT = nc.dram_tensor("outT", [D, SL], F32, kind="ExternalOutput").ap()

    kexp = nc.dram_tensor("kexp", [D, SL], BF16).ap()
    vexp = nc.dram_tensor("vexp", [SL, D], BF16).ap()
    kg = nc.dram_tensor("kg", [2 * D, SL], BF16).ap()
    vg = nc.dram_tensor("vg", [SF, D], BF16).ap()
    cdram = nc.dram_tensor("cdram", [E, SL], BF16).ap()

    from contextlib import ExitStack
    with ExitStack() as ctx:
        BASE = 16512
        off = [BASE]

        def At(name, shape, dt_, offset=None):
            nbytes = int(np.prod(shape[1:])) * mybir.dt.size(dt_)
            if offset is None:
                offset = off[0]
                off[0] += (nbytes + 31) // 32 * 32
            return nc.alloc_sbuf_tensor_at(name, list(shape), dt_, offset=offset).ap()

        xfm = At("xfm", [128, NDC * SL], F32)
        ub = At("ub", [128, NDC * SL], BF16)
        qoz = At("qoz", [128, NDC * SL], BF16)
        wreg = off[0]
        w1t = [At(f"w1t{p}", [128, NDC * HC], BF16, wreg + p * 24576) for p in range(2)]
        w3t = [At(f"w3t{p}", [128, NDC * HC], BF16, wreg + p * 24576 + 8192) for p in range(2)]
        w2t = [At(f"w2t{p}", [128, 4 * D], BF16, wreg + p * 24576 + 16384) for p in range(2)]
        wta = At("wta", [128, NDC * D], BF16, wreg)
        qtile = [At(f"qtile{p}", [128, SL], BF16, wreg + p * 4096) for p in range(2)]
        khb = [At(f"khb{p}", [128, SF], BF16, wreg + 8192 + p * 8192) for p in range(2)]
        vh = [At(f"vh{p}", [128, NKC * 65], BF16, wreg + 24576 + p * 4160) for p in range(2)]
        kvx = At("kvx", [128, SL], BF16, wreg + 16384)
        bvrowb = At("bvrowb", [1, D], BF16, wreg + 20480)
        meanrow = At("meanrow", [1, SL], F32, wreg + 24576)
        rrow = At("rrow", [1, SL], F32, wreg + 32896)
        off[0] = wreg + 49152
        screg = off[0]
        sq = [At(f"sq{p}", [128, T], F32, screg + p * 2048) for p in range(2)]
        uf32 = [At(f"uf32{p}", [128, T], F32, screg + 4096 + p * 2048) for p in range(2)]
        hbuf = At("hbuf", [128, 4 * T], BF16, screg + 8192)
        s1 = [At(f"s1{p}", [128, T], F32, screg + 12288 + p * 2048) for p in range(2)]
        mrow = At("mrow", [1, SL], BF16, screg)
        rsb = At("rsb", [128, 512], F32, screg + 5120)
        srow = At("srow", [128, 512], F32, screg + 7168)
        obuf2 = [At(f"obuf{p}", [64, 512], BF16, screg + 9216 + p * 1024) for p in range(2)]
        att = [At("att0", [128, 4 * 512], BF16, wreg + 32896),
               At("att1", [128, 4 * 512], BF16, wreg + 36992)]
        off[0] = screg + 16384
        rowbuf = At("rowbuf", [1, 4 * T], F32)
        ctoff = off[0]
        cT = At("cT", [8, SL], BF16)
        crow = At("crow", [1, SL], BF16, offset=ctoff)
        battn = At("battn", [128, 4 * NDC], F32)
        onesf = At("onesf", [128, 128], F32)
        onesb = At("onesb", [1, 128], BF16)
        idf = At("idf", [128, 128], F32)
        idb = At("idb", [128, 128], BF16)
        wg_sb = At("wg_sb", [128, NDC * E], F32)
        g1b = At("g1b", [128, E], F32)
        bgb = At("bgb", [128, E], F32)
        grow = At("grow", [1, 2 * E], F32)
        gsc = At("gsc", [128, 4 * E], F32)
        gcol = At("gcol", [128, 8], F32)
        assert off[0] <= 229344, off[0]

        pA = ctx.enter_context(nc.psum_tensor("pA", [128, 2048], F32))
        pB = ctx.enter_context(nc.psum_tensor("pB", [128, 2048], F32))
        pAa, pBa = pA[:, :], pB[:, :]
        pAb = pA[:, :].bitcast(BF16)

        def bank(ap_, i):
            return ap_[:, i * 512:(i + 1) * 512]

        names = ["qsem", "dsem", "tsem", "vsem", "asem", "ck", "cv", "esem",
                 "ws0", "ws1", "ks0", "ks1", "kb0", "kb1", "vb0", "vb1"]
        SM = {nm: Sem(nc, ctx, nm) for nm in names}
        (qsem, dsem, tsem, vsem, asem, ck, cv, esem, ws0, ws1, ks0, ks1, kb0,
         kb1, vb0, vb1) = (SM[n] for n in names)
        wsp = [ws0, ws1]
        ksp = [ks0, ks1]
        kbp = [kb0, kb1]
        vbp = [vb0, vb1]
        block = ctx.enter_context(nc.Block())

        @block.gpsimd
        def _(g):
            g.memset(onesf[:, :], 1.0)
            g.memset(onesb[:, :], 1.0)
            g.memset(idf[:, :], 1.0)
            esem.inc(g.memset(idb[:, :], 1.0))
            g.wait_ge(esem.h, 1)
            g.affine_select(idf[:, :], idf[:, :], [[1, 128]], ALU.is_equal, 0.0,
                            base=0, channel_multiplier=-1)
            esem.inc(g.affine_select(idb[:, :], idb[:, :], [[1, 128]], ALU.is_equal, 0.0,
                                     base=0, channel_multiplier=-1))
            with nc.allow_non_contiguous_dma("tiny one-time bias loads"):
                for j, bb in enumerate([bqd, bkd, bvd, bod]):
                    g.dma_start(out=battn[:, j * NDC:(j + 1) * NDC],
                                in_=bb.rearrange("(dc p) -> p dc", p=128),
                                allow_slow_non_contiguous=True).then_inc(dsem.h, 16)
            g.dma_start(out=wg_sb[:, :].rearrange("p (dc e) -> p dc e", e=E),
                        in_=wgd.rearrange("(dc p) e -> p dc e", p=128)).then_inc(dsem.h, 16)
            g.dma_start(out=grow[0:1, E:2 * E], in_=bgd[None, :]).then_inc(dsem.h, 16)
            dsem.n += 6 * 16

        @block.sync
        def _(sy):
            for dc in range(NDC):
                sy.dma_start(out=xfm[:, dc * SL:(dc + 1) * SL],
                             in_=xT[dc * 128:(dc + 1) * 128, :]).then_inc(qsem.h, 16)
            qsem.n += NDC * 16
        q_x = qsem.n

        @block.tensor
        def _(t):
            t.wait_ge(dsem.h, dsem.n)
            t.wait_ge(esem.h, 2)
            for dc in range(NDC):
                ins = t.matmul(pBa[0:1, 0:E], onesf[:, 0:1], wg_sb[:, dc * E:(dc + 1) * E],
                               start=(dc == 0), stop=(dc == NDC - 1))
            tsem.inc(ins)
        ev0 = tsem.n

        @block.vector
        def _(v):
            v.wait_ge(tsem.h, ev0)
            vsem.inc(v.tensor_copy(grow[0:1, 0:E], pBa[0:1, 0:E]))
        ev1 = vsem.n

        @block.tensor
        def _(t):
            t.wait_ge(vsem.h, ev1)
            tsem.inc(t.matmul(pAa[:, 0:2 * E], onesf[0:1, 0:128], grow[0:1, :],
                              start=True, stop=True))
        ev2 = tsem.n

        @block.vector
        def _(v):
            v.wait_ge(tsem.h, ev2)
            v.tensor_scalar_mul(g1b[:, :], pAa[:, 0:E], -1.0)
            v.tensor_scalar_mul(battn[:, 0:NDC], battn[:, 0:NDC], SCALE)
            vsem.inc(v.tensor_copy(bgb[:, :], pAa[:, E:2 * E]))

        def gate(tt, a_r):
            for tc in range(4):
                vg0 = vsem.n
                tok0 = tt * T + tc * 128

                @block.tensor
                def _(t, tt=tt, tc=tc, tok0=tok0, vg0=vg0, a_r=a_r):
                    if tt == 0 and tc == 0:
                        t.wait_ge(asem.h, a_r)
                    else:
                        t.wait_ge(vsem.h, vg0)
                    for dc in range(NDC):
                        ins = t.matmul(pBa[0:128, 0:E], xfm[:, dc * SL + tok0:][:, 0:128],
                                       wg_sb[:, dc * E:(dc + 1) * E],
                                       start=(dc == 0), stop=(dc == NDC - 1))
                    tsem.inc(ins)
                    tsem.inc(t.transpose(pAa[0:128, 1536:1537],
                                         meanrow[0:1, tok0:tok0 + 128], onesf[0:1, 0:1]))
                    tsem.inc(t.transpose(pAa[0:128, 1537:1538],
                                         rrow[0:1, tok0:tok0 + 128], onesf[0:1, 0:1]))
                t_g = tsem.n

                @block.vector
                def _(v, t_g=t_g):
                    def vf(ins):
                        vsem.inc(ins)
                        v.wait_ge(vsem.h, vsem.n)
                    v.wait_ge(tsem.h, t_g)
                    vf(v.tensor_copy(gcol[:, 6:8], pAa[:, 1536:1538]))
                    vf(v.scalar_tensor_tensor(gsc[:, 0:E], g1b[:, :], gcol[:, 6:7],
                                              pBa[0:128, 0:E], ALU.mult, ALU.add))
                    vf(v.tensor_scalar(gsc[:, 0:E], gsc[:, 0:E], gcol[:, 7:8], None, ALU.mult))
                    vf(v.tensor_add(gsc[:, 0:E], gsc[:, 0:E], bgb[:, :]))
                    vf(v.reduce_max(gcol[:, 0:1], gsc[:, 0:E], axis=AX.X))
                    vf(v.tensor_scalar_mul(gcol[:, 1:2], gcol[:, 0:1], -1.0))
                    vf(v.tensor_scalar(gsc[:, E:2 * E], gsc[:, 0:E], gcol[:, 0:1], None, ALU.is_ge))
                    vf(v.scalar_tensor_tensor(gsc[:, 2 * E:3 * E], gsc[:, E:2 * E], -1e30,
                                              gsc[:, 0:E], ALU.mult, ALU.add))
                    vsem.inc(v.reduce_max(gcol[:, 2:3], gsc[:, 2 * E:3 * E], axis=AX.X))
                v_g1 = vsem.n

                @block.scalar
                def _(s, v_g1=v_g1):
                    s.wait_ge(vsem.h, v_g1)
                    s.activation(gsc[:, 2 * E:3 * E], gsc[:, 0:E], AF.Exp, bias=gcol[:, 1:2])
                    asem.inc(s.activation(gcol[:, 3:4], gcol[:, 2:3], AF.Exp, bias=gcol[:, 1:2]))
                a_g = asem.n

                @block.vector
                def _(v, a_g=a_g):
                    def vf(ins):
                        vsem.inc(ins)
                        v.wait_ge(vsem.h, vsem.n)
                    v.wait_ge(asem.h, a_g)
                    vf(v.tensor_scalar_add(gcol[:, 4:5], gcol[:, 3:4], 1.0))
                    vf(v.reciprocal(gcol[:, 5:6], gcol[:, 4:5]))
                    vf(v.tensor_scalar(gsc[:, E:2 * E], gsc[:, 0:E], gcol[:, 2:3], None, ALU.is_ge))
                    vf(v.tensor_mul(gsc[:, 2 * E:3 * E], gsc[:, 2 * E:3 * E],
                                    gsc[:, E:2 * E]))
                    vsem.inc(v.tensor_scalar(gsc[:, 3 * E:4 * E].bitcast(BF16)[:, 0:E],
                                             gsc[:, 2 * E:3 * E], gcol[:, 5:6], None, ALU.mult))
                v_g2 = vsem.n

                @block.tensor
                def _(t, v_g2=v_g2):
                    t.wait_ge(vsem.h, v_g2)
                    tsem.inc(t.transpose(pAb[0:8, 3072:3200],
                                         gsc[:, 3 * E:4 * E].bitcast(BF16)[:, 0:E], idb[:, :]))
                t_c = tsem.n

                @block.vector
                def _(v, t_c=t_c, tok0=tok0):
                    v.wait_ge(tsem.h, t_c)
                    vsem.inc(v.tensor_copy(cT[0:8, tok0:tok0 + 128], pAb[0:8, 3072:3200]))

        def layernorm(fold_x, with_gate, first=False):
            v_enter = vsem.n
            a0 = asem.n
            t0s = tsem.n

            @block.scalar
            def _(s):
                for tt in range(NTT):
                    for dc in range(NDC):
                        k = tt * NDC + dc
                        if k == 0:
                            s.wait_ge(vsem.h, v_enter)
                        if first:
                            s.wait_ge(qsem.h, 16 * (dc + 1))
                        if k >= 2:
                            s.wait_ge(tsem.h, t0s + k - 1)
                        asem.inc(s.activation(sq[k % 2][:, :],
                                              xfm[:, dc * SL + tt * T:][:, 0:T], AF.Square))

            @block.tensor
            def _(t):
                for tt in range(NTT):
                    for dc in range(NDC):
                        k = tt * NDC + dc
                        if k == 0:
                            t.wait_ge(vsem.h, v_enter)
                        if first:
                            t.wait_ge(qsem.h, 16 * (dc + 1))
                        t.matmul(pBa[0:1, tt * T:(tt + 1) * T], onesf[:, 0:1],
                                 xfm[:, dc * SL + tt * T:][:, 0:T],
                                 start=(dc == 0), stop=(dc == NDC - 1))
                        t.wait_ge(asem.h, a0 + k + 1)
                        tsem.inc(t.matmul(pAa[0:1, tt * T:(tt + 1) * T], onesf[:, 0:1],
                                          sq[k % 2][:, :],
                                          start=(dc == 0), stop=(dc == NDC - 1)))
            t_sums = tsem.n

            @block.vector
            def _(v):
                v.wait_ge(tsem.h, t_sums)
                ins = None
                for k in range(NTT):
                    v.tensor_scalar_mul(meanrow[0:1, k * T:(k + 1) * T],
                                        pBa[0:1, k * T:(k + 1) * T], 1.0 / D)
                    ins = v.tensor_scalar_mul(pAa[0:1, k * T:(k + 1) * T],
                                              pAa[0:1, k * T:(k + 1) * T], 1.0 / D)
                vsem.inc(ins)
            v_m = vsem.n

            @block.scalar
            def _(s):
                s.wait_ge(vsem.h, v_m)
                ins = None
                for k in range(NTT):
                    ins = s.activation(rowbuf[0:1, k * T:(k + 1) * T],
                                       meanrow[0:1, k * T:(k + 1) * T], AF.Square)
                asem.inc(ins)
            a_sq = asem.n

            @block.vector
            def _(v):
                v.wait_ge(asem.h, a_sq)
                ins = None
                for k in range(NTT):
                    ins = v.tensor_sub(pAa[0:1, k * T:(k + 1) * T],
                                       pAa[0:1, k * T:(k + 1) * T],
                                       rowbuf[0:1, k * T:(k + 1) * T])
                vsem.inc(ins)
                v.wait_ge(vsem.h, vsem.n)
                for k in range(NTT):
                    ins = v.tensor_scalar_add(pAa[0:1, k * T:(k + 1) * T],
                                              pAa[0:1, k * T:(k + 1) * T], EPS)
                vsem.inc(ins)
            v_eps = vsem.n

            @block.scalar
            def _(s):
                s.wait_ge(vsem.h, v_eps)
                ins = None
                for k in range(NTT):
                    ins = s.activation(rowbuf[0:1, k * T:(k + 1) * T],
                                       pAa[0:1, k * T:(k + 1) * T], AF.Ln)
                asem.inc(ins)
            a_lnv = asem.n

            @block.vector
            def _(v):
                v.wait_ge(asem.h, a_lnv)
                ins = None
                for k in range(NTT):
                    ins = v.tensor_scalar_mul(rowbuf[0:1, k * T:(k + 1) * T],
                                              rowbuf[0:1, k * T:(k + 1) * T], -0.5)
                vsem.inc(ins)
            v_half = vsem.n

            @block.scalar
            def _(s):
                s.wait_ge(vsem.h, v_half)
                ins = None
                for k in range(NTT):
                    ins = s.activation(rrow[0:1, k * T:(k + 1) * T],
                                       rowbuf[0:1, k * T:(k + 1) * T], AF.Exp)
                asem.inc(ins)
            a_r = asem.n

            if with_gate:
                for tt in range(NTT):
                    gate(tt, a_r)
            v_gates = vsem.n

            v_u_end = [0, 0]
            a_ub = []
            for tt in range(NTT):
                tbw = v_u_end[tt % 2]
                if tt % 2 == 0:
                    mb, rb = pBa[:, 1024:1536], pBa[:, 1536:2048]
                else:
                    mb, rb = pAa[:, 0:512], pAa[:, 512:1024]

                @block.tensor
                def _(t, tt=tt, mb=mb, rb=rb, tbw=tbw):
                    if tt == 0:
                        t.wait_ge(asem.h, a_r)
                    if tbw:
                        t.wait_ge(vsem.h, tbw)
                    tsem.inc(t.matmul(mb, onesf[0:1, 0:128],
                                      meanrow[0:1, tt * T:(tt + 1) * T],
                                      start=True, stop=True))
                    tsem.inc(t.matmul(rb, onesf[0:1, 0:128],
                                      rrow[0:1, tt * T:(tt + 1) * T],
                                      start=True, stop=True))
                t_bc = tsem.n
                vu0 = vsem.n
                au0 = asem.n

                @block.vector
                def _(v, tt=tt, t_bc=t_bc, mb=mb, rb=rb, au0=au0, fold_x=fold_x):
                    v.wait_ge(tsem.h, t_bc)
                    for dc in range(NDC):
                        x_sl = xfm[:, dc * SL + tt * T:][:, 0:T]
                        p = dc % 2
                        if tt > 0 or dc >= 2:
                            v.wait_ge(asem.h, au0 + dc - 1)
                        v.tensor_sub(uf32[p][:, :], x_sl, mb)
                        vsem.inc(v.tensor_mul(uf32[p][:, :], uf32[p][:, :], rb))
                        if fold_x:
                            vsem.inc(v.scalar_tensor_tensor(x_sl, uf32[p][:, :], 0.5, x_sl,
                                                            ALU.mult, ALU.add))
                        else:
                            vsem.inc(v.engine_nop())
                v_u_end[tt % 2] = vsem.n

                @block.scalar
                def _(s, tt=tt, vu0=vu0):
                    for dc in range(NDC):
                        u_sl = ub[:, dc * SL + tt * T:][:, 0:T]
                        s.wait_ge(vsem.h, vu0 + 2 * dc + 1)
                        asem.inc(s.copy(u_sl, uf32[dc % 2][:, :]))
                a_ub.append(asem.n)
            return a_ub, v_gates, vsem.n

        def ffn_experts(call, v_c, a_ub, v_ln, t_c0):
            t0 = tsem.n
            v0 = vsem.n
            a0 = asem.n
            d0w = [wsp[0].n, wsp[1].n]

            @block.sync
            def _(sy, v_c=v_c):
                sy.wait_ge(vsem.h, v_c)
                sy.dma_start(out=cdram[:, :], in_=cT[0:8, :]).then_inc(qsem.h, 16)
                qsem.n += 16
            q_cflat = qsem.n

            def t_base(e):
                return t0 + e * (4 + 32 * 16)

            def t_cb(e, tt):
                return t_base(e) + tt + 1

            def t_blk(e, hc, tt):
                return t_base(e) + 4 + (hc * NTT + tt) * 16

            def t_h1(e, hc, tt, i):
                return t_blk(e, hc, tt) + 2 * i + 1

            def t_h3(e, hc, tt, i):
                return t_blk(e, hc, tt) + 2 * i + 2

            def t_y(e, hc, tt, dc):
                return t_blk(e, hc, tt) + 8 + dc + 1

            def v_base(e):
                return v0 + e * (4 + 32 * 12)

            def v_z(e, tt):
                return v_base(e) + tt + 1

            def v_blkbase(e, hc, tt):
                return v_base(e) + 4 + (hc * NTT + tt) * 12

            def v_hmul(e, hc, tt, i):
                return v_blkbase(e, hc, tt) + i + 1

            def v_yadd(e, hc, tt, dc):
                return v_blkbase(e, hc, tt) + 4 + dc + 1

            def lin_h(e, hc, tt, i):
                return ((e * NHC + hc) * NTT + tt) * 4 + i

            def lin_y(e, hc, tt, dc):
                return ((e * NHC + hc) * NTT + tt) * 8 + dc

            def unlin_h(k):
                return (k // (4 * NTT * NHC), (k // (4 * NTT)) % NHC,
                        (k // 4) % NTT, k % 4)

            def unlin_y(k):
                return (k // (8 * NTT * NHC), (k // (8 * NTT)) % NHC,
                        (k // 8) % NTT, k % 8)

            def a_silu(e, hc, tt, i):
                return a0 + ((e * NHC + hc) * NTT + tt) * 4 + i + 1

            @block.gpsimd
            def _(g, call=call, t0=t0, t_c0=t_c0):
                for c in range(E * NHC):
                    e, hc = divmod(c, NHC)
                    p = c % 2
                    if c < 2:
                        g.wait_ge(tsem.h, t_c0 if c == 0 else t0)
                    else:
                        ee, hh = divmod(c - 2, NHC)
                        g.wait_ge(tsem.h, t_y(ee, hh, NTT - 1, 7))
                    g.dma_start(out=w1t[p].rearrange("p (dc h) -> p dc h", h=HC),
                                in_=w1d[e][:, hc * HC:(hc + 1) * HC]
                                .rearrange("(dc p) h -> p dc h", p=128)).then_inc(wsp[p].h, 16)
                    g.dma_start(out=w3t[p].rearrange("p (dc h) -> p dc h", h=HC),
                                in_=w3d[e][:, hc * HC:(hc + 1) * HC]
                                .rearrange("(dc p) h -> p dc h", p=128)).then_inc(wsp[p].h, 16)
                    g.dma_start(out=w2t[p].rearrange("p (hh d) -> p hh d", d=D),
                                in_=w2d[e][hc * HC:(hc + 1) * HC, :]
                                .rearrange("(hh p) d -> p hh d", p=128)).then_inc(wsp[p].h, 16)
                    wsp[p].n += 48

            @block.sync
            def _(sy, q_cflat=q_cflat):
                for e in range(E):
                    if e == 0:
                        sy.wait_ge(qsem.h, q_cflat)
                    else:
                        sy.wait_ge(tsem.h, t_cb(e - 1, NTT - 1))
                    sy.dma_start(out=crow[0:1, :], in_=cdram[e:e + 1, :]).then_inc(qsem.h, 16)
                    qsem.n += 16
            q_crow0 = q_cflat

            @block.tensor
            def _(t, call=call, q_crow0=q_crow0, d0w=d0w, v_ln=v_ln):
                for e in range(E):
                    for tt in range(NTT):
                        t.wait_ge(qsem.h, q_crow0 + 16 * (e + 1))
                        if e == 0 and tt == 0:
                            t.wait_ge(vsem.h, v_ln)
                        if tt > 0:
                            t.wait_ge(vsem.h, v_z(e, tt - 1))
                        if e > 0 and tt == 0:
                            t.wait_ge(vsem.h, v_z(e - 1, NTT - 1))
                        tsem.inc(t.matmul(pBa[:, 1024:1536], onesb[:, 0:128],
                                          crow[0:1, tt * T:(tt + 1) * T], start=True, stop=True))
                        assert tsem.n == t_cb(e, tt)
                    for hc in range(NHC):
                        c = e * NHC + hc
                        p = c % 2
                        t.wait_ge(wsp[p].h, d0w[p] + 48 * (c // 2 + 1))
                        for tt in range(NTT):
                            t.wait_ge(vsem.h, v_z(e, tt))
                            for i in range(4):
                                kk = lin_h(e, hc, tt, i)
                                if kk >= 2:
                                    t.wait_ge(vsem.h, v_hmul(*unlin_h(kk - 2)))
                                pp = kk % 2
                                for dc in range(NDC):
                                    ins = t.matmul(bank(pAa, pp),
                                                   w1t[p][:, dc * HC + i * 128:][:, 0:128],
                                                   qoz[:, dc * SL + tt * T:][:, 0:T],
                                                   start=(dc == 0), stop=(dc == NDC - 1))
                                tsem.inc(ins)
                                for dc in range(NDC):
                                    ins = t.matmul(bank(pAa, 2 + pp),
                                                   w3t[p][:, dc * HC + i * 128:][:, 0:128],
                                                   qoz[:, dc * SL + tt * T:][:, 0:T],
                                                   start=(dc == 0), stop=(dc == NDC - 1))
                                tsem.inc(ins)
                                assert tsem.n == t_h3(e, hc, tt, i)
                            t.wait_ge(vsem.h, v_hmul(e, hc, tt, 3))
                            for dc in range(NDC):
                                kk = lin_y(e, hc, tt, dc)
                                if kk >= 2:
                                    t.wait_ge(vsem.h, v_yadd(*unlin_y(kk - 2)))
                                pp = kk % 2
                                for i in range(4):
                                    ins = t.matmul(bank(pBa, pp),
                                                   w2t[p][:, i * D + dc * 128:][:, 0:128],
                                                   hbuf[:, i * T:(i + 1) * T],
                                                   start=(i == 0), stop=(i == 3))
                                tsem.inc(ins)
                                assert tsem.n == t_y(e, hc, tt, dc)

            @block.vector
            def _(v, call=call, a_ub=a_ub):
                for e in range(E):
                    for tt in range(NTT):
                        if e == 0:
                            v.wait_ge(asem.h, a_ub[tt])
                        v.wait_ge(tsem.h, t_cb(e, tt))
                        if e > 0:
                            v.wait_ge(tsem.h, t_h3(e - 1, NHC - 1, tt, 3))
                        ins = None
                        for dc in range(NDC):
                            ins = v.tensor_mul(qoz[:, dc * SL + tt * T:][:, 0:T],
                                               ub[:, dc * SL + tt * T:][:, 0:T],
                                               pBa[:, 1024:1536])
                        vsem.inc(ins)
                        assert vsem.n == v_z(e, tt)
                    for hc in range(NHC):
                        for tt in range(NTT):
                            for i in range(4):
                                kk = lin_h(e, hc, tt, i)
                                pp = kk % 2
                                v.wait_ge(asem.h, a_silu(e, hc, tt, i))
                                v.wait_ge(tsem.h, t_h3(e, hc, tt, i))
                                vsem.inc(v.tensor_mul(hbuf[:, i * T:(i + 1) * T],
                                                      s1[pp][:, :], bank(pAa, 2 + pp)))
                                assert vsem.n == v_hmul(e, hc, tt, i)
                            for dc in range(NDC):
                                kk = lin_y(e, hc, tt, dc)
                                pp = kk % 2
                                v.wait_ge(tsem.h, t_y(e, hc, tt, dc))
                                x_sl = xfm[:, dc * SL + tt * T:][:, 0:T]
                                vsem.inc(v.scalar_tensor_tensor(x_sl, bank(pBa, pp), 0.5,
                                                                x_sl, ALU.mult, ALU.add))
                                assert vsem.n == v_yadd(e, hc, tt, dc)

            @block.scalar
            def _(s, call=call):
                for e in range(E):
                    for hc in range(NHC):
                        for tt in range(NTT):
                            for i in range(4):
                                kk = lin_h(e, hc, tt, i)
                                pp = kk % 2
                                s.wait_ge(tsem.h, t_h1(e, hc, tt, i))
                                if kk >= 2:
                                    s.wait_ge(vsem.h, v_hmul(*unlin_h(kk - 2)))
                                asem.inc(s.activation(s1[pp][:, :], bank(pAa, pp), AF.Silu))
                                assert asem.n == a_silu(e, hc, tt, i)

        def attention(stop_after, a_ub, v_ln, t_c0):
            for wi, wsrc_ in enumerate([wqT, wkT, wvT]):
                t_pre = tsem.n

                @block.gpsimd
                def _(g, wsrc_=wsrc_, t_pre=(t_c0 if wi == 0 else t_pre), wi=wi):
                    g.wait_ge(tsem.h, t_pre)
                    if wi == 2:
                        g.dma_start(out=bvrowb[:, :], in_=bvd[None, :]).then_inc(dsem.h, 16)
                        dsem.n += 16
                    g.dma_start(out=wta[:, :].rearrange("p (dc f) -> p dc f", f=D),
                                in_=wsrc_.rearrange("(dc p) f -> p dc f", p=128)
                                ).then_inc(dsem.h, 16)
                    dsem.n += 16
                d_w = dsem.n

                if wi == 0:
                    a_pre = asem.n

                    @block.tensor
                    def _(t, d_w=d_w, a_pre=a_pre, v_ln=v_ln, a_ub=a_ub):
                        t.wait_ge(dsem.h, d_w)
                        t.wait_ge(vsem.h, v_ln)
                        for fc in range(NDC):
                            for tt in range(NTT):
                                k = fc * NTT + tt
                                if fc == 0:
                                    t.wait_ge(asem.h, a_ub[tt])
                                if k >= 2:
                                    t.wait_ge(asem.h, a_pre + k - 1)
                                for dc in range(NDC):
                                    ins = t.matmul(bank(pAa, k % 2),
                                                   wta[:, dc * D + fc * 128:][:, 0:128],
                                                   ub[:, dc * SL + tt * T:][:, 0:T],
                                                   start=(dc == 0), stop=(dc == NDC - 1))
                                tsem.inc(ins)

                    @block.scalar
                    def _(s, t_pre=t_pre):
                        for fc in range(NDC):
                            for tt in range(NTT):
                                k = fc * NTT + tt
                                s.wait_ge(tsem.h, t_pre + k + 1)
                                asem.inc(s.activation(qoz[:, fc * SL + tt * T:][:, 0:T],
                                                      bank(pAa, k % 2), AF.Identity,
                                                      bias=battn[:, 0 * NDC + fc:][:, 0:1],
                                                      scale=SCALE))
                elif wi == 1:
                    a_pre = asem.n
                    q_pre = qsem.n

                    @block.tensor
                    def _(t, d_w=d_w, a_pre=a_pre):
                        t.wait_ge(dsem.h, d_w)
                        t.wait_ge(asem.h, a_pre)
                        for fc in range(NDC):
                            for tt in range(NTT):
                                k = fc * NTT + tt
                                if k >= 2:
                                    t.wait_ge(asem.h, a_pre + k - 1)
                                for dc in range(NDC):
                                    ins = t.matmul(bank(pAa, k % 2),
                                                   wta[:, dc * D + fc * 128:][:, 0:128],
                                                   ub[:, dc * SL + tt * T:][:, 0:T],
                                                   start=(dc == 0), stop=(dc == NDC - 1))
                                tsem.inc(ins)

                    @block.scalar
                    def _(s, t_pre=t_pre, q_pre=q_pre, a_pre=a_pre):
                        for fc in range(NDC):
                            for tt in range(NTT):
                                k = fc * NTT + tt
                                s.wait_ge(tsem.h, t_pre + k + 1)
                                if fc >= 1 and tt == 0:
                                    s.wait_ge(qsem.h, q_pre + 16 * fc)
                                asem.inc(s.activation(kvx[:, tt * T:(tt + 1) * T],
                                                      bank(pAa, k % 2), AF.Identity,
                                                      bias=battn[:, 1 * NDC + fc:][:, 0:1]))

                    @block.sync
                    def _(sy, a_pre=a_pre):
                        for fc in range(NDC):
                            sy.wait_ge(asem.h, a_pre + NTT * (fc + 1))
                            sy.dma_start(out=kexp[fc * 128:(fc + 1) * 128, :],
                                         in_=kvx[:, :]).then_inc(qsem.h, 16)
                            qsem.n += 16
                    q_kexp = qsem.n

                    @block.gpsimd
                    def _(g, q_kexp=q_kexp):
                        g.wait_ge(qsem.h, q_kexp)
                        g.collective_compute(
                            "AllGather", ALU.bypass,
                            replica_groups=[[0, 1], [2, 3], [4, 5], [6, 7]],
                            ins=[kexp.opt()], outs=[kg.opt()]).then_inc(ck.h, 1)
                        ck.n = 1
                elif wi == 2:
                    v_pre = vsem.n
                    q_pre = qsem.n

                    @block.tensor
                    def _(t, d_w=d_w, v_pre=v_pre):
                        t.wait_ge(dsem.h, d_w)
                        t.wait_ge(vsem.h, v_pre)
                        for tc in range(NQC):
                            for fb in range(2):
                                k = tc * 2 + fb
                                if k >= 4:
                                    t.wait_ge(vsem.h, v_pre + k - 3)
                                for dc in range(NDC):
                                    t.matmul(bank(pBa, k % 4),
                                             ub[:, dc * SL + tc * 128:][:, 0:128],
                                             wta[:, dc * D + fb * T:][:, 0:T],
                                             start=(dc == 0), stop=False)
                                ins = t.matmul(bank(pBa, k % 4), onesb[:, 0:128],
                                               bvrowb[0:1, fb * T:(fb + 1) * T],
                                               start=False, stop=True)
                                tsem.inc(ins)

                    @block.vector
                    def _(v, t_pre=t_pre, q_pre=q_pre):
                        for tc in range(NQC):
                            for fb in range(2):
                                k = tc * 2 + fb
                                v.wait_ge(tsem.h, t_pre + k + 1)
                                if tc >= 1 and fb == 0:
                                    v.wait_ge(qsem.h, q_pre + 16 * tc)
                                vsem.inc(v.tensor_copy(kvx[:, fb * T:(fb + 1) * T],
                                                       bank(pBa, k % 4)))

                    @block.sync
                    def _(sy, v_pre=v_pre):
                        for tc in range(NQC):
                            sy.wait_ge(vsem.h, v_pre + 2 * (tc + 1))
                            sy.dma_start(out=vexp[tc * 128:(tc + 1) * 128, :],
                                         in_=kvx[:, 0:1024]).then_inc(qsem.h, 16)
                            qsem.n += 16
                    q_vexp = qsem.n

                    @block.gpsimd
                    def _(g, q_vexp=q_vexp):
                        g.wait_ge(qsem.h, q_vexp)
                        g.collective_compute(
                            "AllGather", ALU.bypass,
                            replica_groups=[[0, 1], [2, 3], [4, 5], [6, 7]],
                            ins=[vexp.opt()], outs=[vg.opt()]).then_inc(cv.h, 1)
                        cv.n = 1

            t_postv = tsem.n

            @block.gpsimd
            def _(g, t_postv=t_postv):
                g.wait_ge(tsem.h, t_postv)
                for p in range(2):
                    vv = vh[p].rearrange("p (k c) -> p k c", c=65)
                    g.memset(vv[:, :, 64:65], 1.0)
                    g.memset(khb[p][64:128, :], 0.0)
                    g.memset(khb[p][64:65, :], 1.0)
                    g.memset(qtile[p][64:128, :], 0.0)
                g.memset(srow[:, :], 1.0).then_inc(esem.h, 1)
                esem.n += 1

            if stop_after == "qkv":
                return
            NF = 8 * NQC
            tfill = {}
            ttp = {}
            tBsT = {}
            tBAV = {}
            tBrs = {}
            tBend = {}
            tAend = {}
            vbaseA = {}
            sbaseB = {}
            q_mst = {}
            q_kb = {}
            prev_osc = [0]
            prev_vred_tail = [None, None]
            q_vb = {}

            for s in range(NH + 1):
                hA = s if s < NH else None
                hB = s - 1 if s >= 1 else None
                pA_ = None if hA is None else hA % 2
                pB_ = None if hB is None else hB % 2

                def emit_loads(hh):
                    pp = hh % 2
                    kbe = kbp[pp]
                    vbe = vbp[pp]
                    lo = 64 * (hh % 2)
                    fcq = hh // 2

                    @block.sync
                    def _(sy, hh=hh, lo=lo, fcq=fcq, kbe=kbe, vbe=vbe, pp=pp):
                        if hh == 0:
                            sy.wait_ge(ck.h, 1)
                            sy.wait_ge(esem.h, esem.n)
                        if hh >= 2:
                            sy.wait_ge(tsem.h, tBend[hh - 2])
                        sy.dma_start(out=khb[pp][0:64, 0:SL],
                                     in_=kg[64 * hh:64 * hh + 64, :]).then_inc(kbe.h, 16)
                        sy.dma_start(out=khb[pp][0:64, SL:2 * SL],
                                     in_=kg[D + 64 * hh:D + 64 * hh + 64, :]).then_inc(kbe.h, 16)
                        sy.dma_start(out=qtile[pp][0:64, :],
                                     in_=qoz[lo:lo + 64, fcq * SL:(fcq + 1) * SL]
                                     ).then_inc(kbe.h, 16)
                        kbe.n += 48
                        if hh == 0:
                            sy.wait_ge(cv.h, 1)
                        vv = vh[pp].rearrange("p (k c) -> p k c", c=65)
                        sy.dma_start(out=vv[:, :, 0:64],
                                     in_=vg.rearrange("(k p) f -> p k f", p=128)
                                     [:, :, 64 * hh:64 * hh + 64]).then_inc(vbe.h, 16)
                        vbe.n += 16

                    @block.gpsimd
                    def _(g, hh=hh, kbe=kbe, pp=pp):
                        if hh >= 2:
                            g.wait_ge(tsem.h, tBend[hh - 2])
                        kbe.inc(g.memset(qtile[pp][64:65, :], 0.0))
                    q_kb[hh] = kbp[pp].n
                    q_vb[hh] = vbp[pp].n

                if s == 0:
                    emit_loads(0)
                if hB is not None:
                    q_kB = q_kb[hB]
                    q_vB = q_vb[hB]

                vbase = vsem.n
                if hA is not None:
                    vbaseA[hA] = vbase

                def v_red(qc, f, vbase=vbase, hB=hB):
                    ex = 2 * (qc // 4) if hB is not None else 0
                    return vbase + 10 * qc + ex + f + 1

                def v_rfin(qc, vbase=vbase, hB=hB):
                    ex = 2 * (qc // 4) if hB is not None else 0
                    return vbase + 10 * qc + ex + 9

                def v_copy(qc, vbase=vbase, hB=hB):
                    ex = 2 * (qc // 4) if hB is not None else 0
                    return vbase + 10 * qc + ex + 10

                def v_recip(qb, vbase=vbase, hA=hA):
                    return vbase + (42 * qb + 41 if hA is not None else 2 * qb + 1)

                def v_osc(qb, vbase=vbase, hA=hA):
                    return vbase + (42 * qb + 42 if hA is not None else 2 * qb + 2)

                sbase = asem.n
                if hB is not None:
                    sbaseB[hB] = sbase

                def s_exp(qb, g, sbase=sbase):
                    return sbase + 33 * qb + g + 1

                def s_rsb(qb, sbase=sbase):
                    return sbase + 33 * qb + 33

                qpred = [None]

                @block.tensor
                def _(t, s=s, hA=hA, hB=hB, pA_=pA_, pB_=pB_,
                      v_red=v_red, v_rfin=v_rfin, v_copy=v_copy,
                      v_recip=v_recip, v_osc=v_osc, s_exp=s_exp, s_rsb=s_rsb,
                      vbase=vbase, sbase=sbase):
                    items = []
                    if hA is not None:
                        loA = 64 * (hA % 2)
                        fcqA = hA // 2

                        def mk_fill(qc, f, hA=hA, pA_=pA_):
                            def emit():
                                FF = 8 * qc + f
                                if FF == 0:
                                    t.wait_ge(kbp[pA_].h, q_kb[hA])
                                if FF < 2:
                                    if prev_vred_tail[f] is not None:
                                        t.wait_ge(vsem.h, prev_vred_tail[f])
                                elif f == 0 and FF % 8 == 0:
                                    t.wait_ge(vsem.h, v_copy(qc - 1))
                                else:
                                    t.wait_ge(vsem.h, v_red((FF - 2) // 8, (FF - 2) % 8))
                                q_sl = qtile[pA_][0:128, qc * 128:(qc + 1) * 128]
                                ins = t.matmul(pBa[:, (2 + f % 2) * 512:(3 + f % 2) * 512],
                                               q_sl, khb[pA_][0:128, f * 512:(f + 1) * 512],
                                               start=True, stop=True)
                                tsem.inc(ins)
                                tfill[(hA, FF)] = tsem.n
                            return emit

                        def mk_tp(qc, hA=hA):
                            def emit():
                                t.wait_ge(vsem.h, v_rfin(qc))
                                tsem.inc(t.transpose(pBa[0:1, 1024:1152],
                                                     gsc[:, 0:1], idf[:, :]))
                                ttp[(hA, qc)] = tsem.n
                            return emit

                        for qc in range(NQC):
                            for f in range(8):
                                items.append(mk_fill(qc, f))
                            items.append(mk_tp(qc))
                    it = iter(items)

                    def inject(n):
                        for _i in range(n):
                            em = next(it, None)
                            if em is None:
                                return
                            em()

                    if hB is not None:
                        loB = 64 * (hB % 2)
                        kbe = kbp[pB_]
                        t.wait_ge(kbe.h, q_kB)
                        t.wait_ge(vbp[pB_].h, q_vB)
                        t.wait_ge(qsem.h, q_mst[hB])

                        def attsl(g):
                            sl = g % 8
                            return att[sl // 4][:, (sl % 4) * 512:(sl % 4 + 1) * 512]

                        def sTq(qb, g, hB=hB, pB_=pB_):
                            ins = t.matmul(pAa[:, (g % 4) * 512:(g % 4 + 1) * 512],
                                           khb[pB_][0:128, g * 128:(g + 1) * 128],
                                           qtile[pB_][0:128, qb * T:(qb + 1) * T],
                                           start=True, stop=True)
                            tsem.inc(ins)
                            tBsT[(hB, qb, g)] = tsem.n
                        for qb in range(NTT):
                            for g in range(4):
                                if qb == 0:
                                    t.wait_ge(asem.h, sbase)
                                else:
                                    t.wait_ge(asem.h, s_exp(qb - 1, 28 + g))
                                sTq(qb, g)
                            for g in range(NKC):
                                inject((2 if g < 4 else 1) if hA is not None else 0)
                                t.wait_ge(asem.h, s_exp(qb, g))
                                if g == 0:
                                    t.wait_ge(vsem.h,
                                              v_osc(qb - 1) if qb > 0 else prev_osc[0])
                                ins = t.matmul(pBa[0:65, 0:512],
                                               vh[pB_][:, g * 65:(g + 1) * 65],
                                               attsl(g),
                                               start=(g == 0), stop=(g == NKC - 1))
                                tsem.inc(ins)
                                tBAV[(hB, qb, g)] = tsem.n
                                if g + 4 < NKC:
                                    sTq(qb, g + 4)
                            t.wait_ge(vsem.h, v_recip(qb))
                            tsem.inc(t.matmul(pBa[:, 512:1024], onesf[64:65, 0:128],
                                              srow[64:65, :], start=True, stop=True))
                            tBrs[(hB, qb)] = tsem.n
                    inject(10 ** 9)
                    if hA is not None:
                        tAend[hA] = tsem.n
                    if hB is not None:
                        tBend[hB] = tsem.n

                if hB is not None:
                    @block.scalar
                    def _(s_, hB=hB):
                        def attsl(g):
                            sl = g % 8
                            return att[sl // 4][:, (sl % 4) * 512:(sl % 4 + 1) * 512]
                        for qb in range(NTT):
                            for g in range(NKC):
                                s_.wait_ge(tsem.h, tBsT[(hB, qb, g)])
                                if g >= 8:
                                    s_.wait_ge(tsem.h, tBAV[(hB, qb, g - 8)])
                                elif qb > 0:
                                    s_.wait_ge(tsem.h, tBAV[(hB, qb - 1, 24 + g)])
                                asem.inc(s_.activation(attsl(g),
                                                       pAa[:, (g % 4) * 512:(g % 4 + 1) * 512],
                                                       AF.Exp))
                            s_.wait_ge(tsem.h, tBrs[(hB, qb)])
                            asem.inc(s_.copy(rsb[0:64, :], pBa[0:64, 512:1024]))

                qpred[0] = qsem.n

                @block.vector
                def _(v, s=s, hA=hA, hB=hB, pB_=pB_, vbase=vbase, sbase=sbase,
                      v_red=v_red, v_rfin=v_rfin, v_copy=v_copy,
                      v_recip=v_recip, v_osc=v_osc, s_rsb=s_rsb, qpred=qpred):
                    def a_block(qc, hA=hA):
                        if qc == 0 and hA is not None and hA >= 1:
                            v.wait_ge(qsem.h, q_mst[hA - 1])
                        for f in range(8):
                            v.wait_ge(tsem.h, tfill[(hA, 8 * qc + f)])
                            vsem.inc(v.reduce_max(gcol[:, f:f + 1],
                                                  pBa[:, (2 + f % 2) * 512:(3 + f % 2) * 512],
                                                  axis=AX.X))
                        v.wait_ge(vsem.h, vsem.n)
                        vsem.inc(v.reduce_max(gsc[:, 0:1], gcol[:, 0:8], axis=AX.X,
                                              negate=True))
                        v.wait_ge(tsem.h, ttp[(hA, qc)])
                        vsem.inc(v.tensor_copy(mrow[0:1, qc * 128:(qc + 1) * 128],
                                               pBa[0:1, 1024:1152]))

                    def b_block(qb, hB=hB, pB_=pB_):
                        loB = 64 * (hB % 2)
                        fcqB = hB // 2
                        v.wait_ge(tsem.h, tBAV[(hB, qb, 31)])
                        vsem.inc(v.reciprocal(srow[64:65, :], pBa[64:65, 0:512]))
                        v.wait_ge(asem.h, s_rsb(qb))
                        if loB == 0:
                            vsem.inc(v.tensor_mul(
                                qoz[0:64, fcqB * SL + qb * T:][:, 0:T],
                                pBa[0:64, 0:512], rsb[0:64, :]))
                        else:
                            if qb >= 2:
                                v.wait_ge(qsem.h, qpred[0] + 16 * (qb - 1))
                            vsem.inc(v.tensor_mul(obuf2[qb % 2][:, :],
                                                  pBa[0:64, 0:512], rsb[0:64, :]))

                    if hA is not None and hB is not None:
                        for qb in range(NTT):
                            for qc in range(4 * qb, 4 * qb + 4):
                                a_block(qc)
                            b_block(qb)
                    elif hA is not None:
                        for qc in range(NQC):
                            a_block(qc)
                    else:
                        for qb in range(NTT):
                            b_block(qb)
                    if hB is not None:
                        assert vsem.n == v_osc(NTT - 1), (vsem.n, v_osc(NTT - 1))
                    else:
                        assert vsem.n == v_copy(NQC - 1), (vsem.n, v_copy(NQC - 1))

                if hA is not None:
                    prev_vred_tail[0] = v_copy(NQC - 1)
                    prev_vred_tail[1] = v_red(NQC - 1, 7)
                if hB is not None:
                    prev_osc[0] = v_osc(NTT - 1)

                @block.sync
                def _(sy, hA=hA, hB=hB, v_copy=v_copy, v_osc=v_osc):
                    if hB is not None and hB % 2 == 1:
                        fcqB = hB // 2
                        for qb in range(NTT):
                            sy.wait_ge(vsem.h, v_osc(qb))
                            sy.dma_start(out=qoz[64:128, fcqB * SL + qb * T:][:, 0:T],
                                         in_=obuf2[qb % 2][:, :]).then_inc(qsem.h, 16)
                            qsem.n += 16
                    if hA is not None:
                        sy.wait_ge(vsem.h, v_copy(NQC - 1))
                        sy.dma_start(out=qtile[pA_][64:65, :],
                                     in_=mrow[0:1, :]).then_inc(qsem.h, 16)
                        qsem.n += 16
                if hA is not None:
                    q_mst[hA] = qsem.n
                if s + 1 < NH:
                    emit_loads(s + 1)

            t_heads = tsem.n

            @block.gpsimd
            def _(g, t_heads=t_heads):
                g.wait_ge(tsem.h, t_heads)
                g.dma_start(out=wta[:, :].rearrange("p (dc f) -> p dc f", f=D),
                            in_=woT.rearrange("(dc p) f -> p dc f", p=128)
                            ).then_inc(dsem.h, 16)
                dsem.n += 16
            d_w_wo = dsem.n
            v_wo0, q_wo0 = vsem.n, qsem.n

            @block.tensor
            def _(t, v_wo0=v_wo0, q_wo0=q_wo0, d_w_wo=d_w_wo):
                t.wait_ge(vsem.h, v_wo0)
                t.wait_ge(qsem.h, q_wo0)
                t.wait_ge(dsem.h, d_w_wo)
                for dc in range(NDC):
                    for tt in range(NTT):
                        k = dc * NTT + tt
                        if k >= 2:
                            t.wait_ge(vsem.h, v_wo0 + k - 1)
                        for fc in range(NDC):
                            ins = t.matmul(bank(pAa, k % 2),
                                           wta[:, fc * D + dc * 128:][:, 0:128],
                                           qoz[:, fc * SL + tt * T:][:, 0:T],
                                           start=(fc == 0), stop=(fc == NDC - 1))
                        tsem.inc(ins)
            t_wo0 = tsem.n - NDC * NTT

            @block.vector
            def _(v, t_wo0=t_wo0):
                for dc in range(NDC):
                    for tt in range(NTT):
                        k = dc * NTT + tt
                        v.wait_ge(tsem.h, t_wo0 + k + 1)
                        x_sl = xfm[:, dc * SL + tt * T:][:, 0:T]
                        vsem.inc(v.scalar_tensor_tensor(
                            x_sl, bank(pAa, k % 2), battn[:, 3 * NDC + dc:][:, 0:1],
                            x_sl, ALU.add, ALU.add))

        if stop_after != "p0":
            t_c0_1 = tsem.n
            a_ub1, v_g1, v_ln1 = layernorm(fold_x=True, with_gate=True, first=True)
        if stop_after in ("u1", "c1"):
            t_d, v_d, a_d = tsem.n, vsem.n, asem.n

            @block.gpsimd
            def _(g):
                g.wait_ge(tsem.h, t_d)
                g.wait_ge(vsem.h, v_d)
                g.wait_ge(asem.h, a_d)
                if stop_after == "u1":
                    for dc in range(NDC):
                        g.dma_start(out=outT[dc * 128:(dc + 1) * 128, :],
                                    in_=ub[:, dc * SL:(dc + 1) * SL]).then_inc(dsem.h, 16)
                        dsem.n += 16
                else:
                    g.dma_start(out=outT[0:8, :], in_=cT[0:8, :]).then_inc(dsem.h, 16)
                    dsem.n += 16
                g.wait_ge(dsem.h, dsem.n)
        if stop_after not in ("p0", "ln1", "u1", "c1"):
            ffn_experts(0, v_g1, a_ub1, v_ln1, t_c0_1)
        if stop_after not in ("p0", "ln1", "u1", "c1", "ffn1"):
            t_c0_a = tsem.n
            a_ub2, _vg2, v_ln2 = layernorm(fold_x=False, with_gate=False)
            attention("qkv" if stop_after == "qkv" else "all", a_ub2, v_ln2, t_c0_a)
        if stop_after not in ("p0", "ln1", "u1", "c1", "ffn1", "qkv", "attn"):
            t_c0_2 = tsem.n
            a_ub3, v_g3, v_ln3 = layernorm(fold_x=True, with_gate=True)
            ffn_experts(1, v_g3, a_ub3, v_ln3, t_c0_2)

        t_fin, v_fin, a_fin = tsem.n, vsem.n, asem.n
        skip_out = stop_after in ("u1", "c1")

        @block.sync
        def _(sy):
            if skip_out:
                sy.wait_ge(dsem.h, dsem.n)
                return
            sy.wait_ge(vsem.h, v_fin)
            sy.wait_ge(tsem.h, t_fin)
            sy.wait_ge(asem.h, a_fin)
            for dc in range(NDC):
                sy.dma_start(out=outT[dc * 128:(dc + 1) * 128, :],
                             in_=xfm[:, dc * SL:(dc + 1) * SL]).then_inc(qsem.h, 16)
            qsem.n += NDC * 16
            sy.wait_ge(qsem.h, qsem.n)

    return nc


def kernel(L0_attn_x, L0_attn_Wq, L0_attn_bq, L0_attn_Wk, L0_attn_bk,
           L0_attn_Wv, L0_attn_bv, L0_attn_Wo, L0_attn_bo,
           L0_ffn_Wg, L0_ffn_bg, L0_ffn_W1, L0_ffn_W2, L0_ffn_W3):
    x = np.ascontiguousarray(np.asarray(L0_attn_x, dtype=np.float32)).reshape(B * S, D)
    com = dict(
        wqT=np.ascontiguousarray(np.asarray(L0_attn_Wq, np.float32).T),
        wkT=np.ascontiguousarray(np.asarray(L0_attn_Wk, np.float32).T),
        wvT=np.ascontiguousarray(np.asarray(L0_attn_Wv, np.float32).T),
        woT=np.ascontiguousarray(np.asarray(L0_attn_Wo, np.float32).T),
        bq=np.asarray(L0_attn_bq, np.float32), bk=np.asarray(L0_attn_bk, np.float32),
        bv=np.asarray(L0_attn_bv, np.float32), bo=np.asarray(L0_attn_bo, np.float32),
        wg=np.asarray(L0_ffn_Wg, np.float32), bg=np.asarray(L0_ffn_bg, np.float32),
        w1=np.ascontiguousarray(np.asarray(L0_ffn_W1, np.float32)),
        w2=np.ascontiguousarray(np.asarray(L0_ffn_W2, np.float32)),
        w3=np.ascontiguousarray(np.asarray(L0_ffn_W3, np.float32)),
    )
    in_maps = []
    for c in range(NCORE):
        xs = np.ascontiguousarray(x[c * SL:(c + 1) * SL, :].T)
        in_maps.append(dict(xT=xs, **com))
    nc = build_program()
    res = run_bass_kernel_spmd(nc, in_maps, core_ids=list(range(NCORE)))
    outs = [np.asarray(r["outT"]).T for r in res.results]
    return np.concatenate(outs, axis=0).reshape(B, S, D)
